# version 05
# baseline (speedup 1.0000x reference)
import math
import numpy as np
from contextlib import ExitStack

import concourse.bass as bass
import concourse.mybir as mybir
from concourse.bass_utils import run_bass_kernel_spmd
from concourse.alu_op_type import AluOpType as OP

F32 = mybir.dt.float32
BF16 = mybir.dt.bfloat16
AF = mybir.ActivationFunctionType
AX = mybir.AxisListType

NCORES = 8
S = 8192
D = 2048
TOK = S // NCORES
NCHUNK = 64
CH = 64
NPAIR = 64
HEADS = 4
DK = 128
DV = 128
ZC = 512
D_FF = 5632
NEG = -1e30
PH3_PAIRS = NPAIR
W1ROWS = 704
W3ROWS = 256
WQ_ELEMS = 2 * 768 * D + W3ROWS * D_FF

import ml_dtypes
NF4 = np.array([-1.0, -0.6962, -0.5251, -0.3949, -0.2844, -0.1848, -0.0911, 0.0,
                0.0796, 0.1609, 0.2461, 0.3379, 0.4407, 0.5626, 0.723, 1.0],
               dtype=ml_dtypes.bfloat16).astype(np.float32)
NF4_MID = ((NF4[:-1] + NF4[1:]) / 2.0).astype(np.float32)


KCVALS = {1e-5: 0, 1.0: 1, 1e-6: 2, 0.25: 3, -1e30: 4, 0.0: 5}


class Prog:
    def __init__(self, dump=()):
        self.nc = bass.Bass("TRN2", target_bir_lowering=False, debug=False)
        self.stack = ExitStack()
        self.dump = set(dump)
        self.sems = {}
        self.cnt = {}
        self.sb = {}
        self.ps = {}
        self.dram = {}

    def sem(self, name):
        if name not in self.sems:
            self.sems[name] = self.stack.enter_context(self.nc.semaphore(name))
            self.cnt[name] = 0
        return self.sems[name]

    def inc(self, instr, name, n=1):
        instr.then_inc(self.sem(name), n)
        self.cnt[name] += n
        return self.cnt[name]

    def tok(self, name):
        self.sem(name)
        return self.cnt[name]

    def kc(self, val):
        return self.sb["kcol"][:, KCVALS[val]:KCVALS[val] + 1]

    def sbuf(self, name, shape, dtype):
        cm = self.nc.sbuf_tensor(name, shape, dtype)
        t = cm.__enter__()
        self._cms = getattr(self, "_cms", {})
        self._cms[name] = cm
        self.sb[name] = t
        return t

    def psum(self, name, shape, dtype):
        cm = self.nc.psum_tensor(name, shape, dtype)
        t = cm.__enter__()
        self._cms = getattr(self, "_cms", {})
        self._cms[name] = cm
        self.ps[name] = t
        return t

    def free(self, *names):
        order = {n: i for i, n in enumerate(self._cms.keys())}
        for name in sorted(names, key=lambda n: -order[n]):
            self._cms.pop(name).__exit__(None, None, None)
            self.sb.pop(name, None)
            self.ps.pop(name, None)

    def din(self, name, shape, dtype):
        ap = self.nc.dram_tensor(name, shape, dtype, kind="ExternalInput").ap()
        self.dram[name] = ap
        return ap

    def dout(self, name, shape, dtype):
        ap = self.nc.dram_tensor(name, shape, dtype, kind="ExternalOutput").ap()
        self.dram[name] = ap
        return ap

    def dtmp(self, name, shape, dtype):
        ap = self.nc.dram_tensor(name, shape, dtype, kind="Internal").ap()
        self.dram[name] = ap
        if name in self.dump:
            self.dram[name + "_dump"] = self.nc.dram_tensor(
                name + "_dump", shape, dtype, kind="ExternalOutput").ap()
        return ap


def host_constants(core):
    c = {}
    ident = np.eye(128, dtype=np.float32)
    c["identb"] = ident.astype(ml_dtypes.bfloat16)
    k = np.arange(128)
    c["cumsum_lhsT"] = (((k[:, None] // CH) == (k[None, :] // CH)) &
                        (k[:, None] <= k[None, :])).astype(np.float32)
    selA = np.zeros((128, 128), np.float32); selA[63, :] = 1.0
    selB = np.zeros((128, 128), np.float32); selB[127, :] = 1.0
    selH = np.zeros((128, 128), np.float32); selH[63, 0:64] = 1.0; selH[127, 64:128] = 1.0
    c["glselA"] = selA
    c["glselB"] = selB
    c["glselH"] = selH
    tril_strict = ((k[:, None] // CH) == (k[None, :] // CH)) & (k[:, None] > k[None, :])
    triu_incl = ((k[:, None] // CH) == (k[None, :] // CH)) & (k[:, None] <= k[None, :])
    c["maskA4"] = np.tile(np.where(tril_strict, 0.0, NEG).astype(np.float32), (1, HEADS))
    c["maskT4"] = np.tile(np.where(triu_incl, 0.0, NEG).astype(np.float32), (1, HEADS))
    ql = np.arange(128)
    kl = np.arange(1024)
    c["attn_mask"] = np.where(kl[None, :] <= 128 * core + ql[:, None], 0.0, NEG).astype(np.float32)
    c["identf"] = ident
    c["blockones_f"] = (((k[:, None] // CH) == (k[None, :] // CH))).astype(np.float32)
    inblk = (k[:, None] // CH) == (k[None, :] // CH)
    strict_low = inblk & (k[:, None] > k[None, :])
    strict_up = inblk & (k[:, None] < k[None, :])
    incl_up = inblk & (k[:, None] <= k[None, :])
    c["maskA_n"] = np.tile(np.where(strict_low, 0.0, 1e30).astype(np.float32), (1, HEADS))
    c["maskAT"] = np.tile(np.where(strict_up, 0.0, -1e30).astype(np.float32), (1, HEADS))
    c["maskQT"] = np.tile(np.where(incl_up, 0.0, -1e30).astype(np.float32), (1, HEADS))
    return c


def stage_inputs(inp):
    x = np.ascontiguousarray(np.asarray(inp["L0_attn_hidden_states"])[0])
    xT = np.ascontiguousarray(x.T)
    wqkv = np.asarray(inp["L0_attn_W_qkv"])
    wz = np.asarray(inp["L0_attn_W_z"])
    wb = np.asarray(inp["L0_attn_W_b"])
    wa = np.asarray(inp["L0_attn_W_a"])
    convw = np.asarray(inp["L0_attn_conv_w"])
    base_wT = np.ascontiguousarray(np.asarray(inp["L0_ffn_base_w"]).T)
    router_wT = np.ascontiguousarray(np.asarray(inp["L0_ffn_router_w"]).T)
    lora_A = np.asarray(inp["L0_ffn_lora_A"])
    lora_AR = np.ascontiguousarray(lora_A.transpose(1, 0, 2).reshape(D, 128))
    lora_BS = np.ascontiguousarray(np.asarray(inp["L0_ffn_lora_B"]).reshape(128, D))
    maps = []
    for i in range(NCORES):
        own = np.concatenate([np.arange(c * 128, (c + 1) * 128) for c in range(i, NCHUNK, NCORES)])
        qcols = np.arange(256 * i, 256 * (i + 1))
        cc = np.concatenate([qcols, 2048 + qcols, 4096 + np.arange(512 * i, 512 * (i + 1))])
        bacols = np.arange(4 * i, 4 * (i + 1))
        m = {
            "xT": xT,
            "x_my": np.ascontiguousarray(x[own]),
            "xT_my": np.ascontiguousarray(xT[:, own]),
            "slabidx": ((np.arange(32, dtype=np.int32)[None, :] * 128 + np.arange(128, dtype=np.int32)[:, None]) * 8 + i).astype(np.int32),
            "wqkv_my": np.ascontiguousarray(wqkv[:, cc]),
            "convw_my": np.ascontiguousarray(convw[cc]),
            "wz_my": np.ascontiguousarray(wz[:, 512 * i:512 * (i + 1)]),
            "wb_my": np.ascontiguousarray(wb[:, bacols]),
            "wa_my": np.ascontiguousarray(wa[:, bacols]),
            "dtb_my": np.ascontiguousarray(np.asarray(inp["L0_attn_dt_bias"])[bacols]).reshape(1, 4),
            "alog_my": np.ascontiguousarray(np.asarray(inp["L0_attn_A_log"])[bacols]).reshape(1, 4),
            "normw": np.asarray(inp["L0_attn_norm_w"]).reshape(1, 128),
            "w_out": np.asarray(inp["L0_attn_W_out"]),
            "base_wT": base_wT,
            "base_b": np.asarray(inp["L0_ffn_base_b"]).reshape(1, D),
            "router_wT": router_wT,
            "router_b": np.asarray(inp["L0_ffn_router_b"]).reshape(1, 8),
            "lora_AR": lora_AR,
            "lora_BS": lora_BS,
            "c0": np.asarray(inp["L0_c"]),
            "wmod0_my": np.ascontiguousarray(np.asarray(inp["L0_wmod"])[:, 1536 * i:1536 * (i + 1)]),
            "c1": np.asarray(inp["L1_c"]),
            "wmod1_my": np.ascontiguousarray(np.asarray(inp["L1_wmod"])[:, 1536 * i:1536 * (i + 1)]),
            "qkv1": np.asarray(inp["L1_attn_qkv"]),
            "oproj1": np.asarray(inp["L1_attn_o_proj"]),
            "gamma1": np.asarray(inp["L1_attn_gamma"]).reshape(1, D),
            "beta1": np.asarray(inp["L1_attn_beta"]).reshape(1, D),
            "w1_my": np.ascontiguousarray(np.concatenate([np.asarray(inp["L1_ffn_weight1"])[W1ROWS * i:W1ROWS * (i + 1)], np.zeros((64, D), dtype=np.asarray(inp["L1_ffn_weight1"]).dtype)])),
            "w2_my": np.ascontiguousarray(np.concatenate([np.asarray(inp["L1_ffn_weight2"])[W1ROWS * i:W1ROWS * (i + 1)], np.zeros((64, D), dtype=np.asarray(inp["L1_ffn_weight2"]).dtype)])),
            "w3_my": np.ascontiguousarray(np.asarray(inp["L1_ffn_weight3"])[W3ROWS * i:W3ROWS * (i + 1)]),
        }
        m.update(host_constants(i))
        maps.append(m)
    return maps


def declare_io(p):
    d = p.din
    d("xT", [D, S], F32)
    d("x_my", [TOK, D], F32)
    d("xT_my", [D, TOK], F32)
    d("slabidx", [128, 32], mybir.dt.int32)
    d("wqkv_my", [D, 1024], F32)
    d("convw_my", [1024, 4], F32)
    d("wz_my", [D, ZC], F32)
    d("wb_my", [D, 4], F32)
    d("wa_my", [D, 4], F32)
    d("dtb_my", [1, 4], F32)
    d("alog_my", [1, 4], F32)
    d("normw", [1, 128], F32)
    d("w_out", [4096, D], F32)
    d("base_wT", [D, D], F32)
    d("base_b", [1, D], F32)
    d("router_wT", [D, 8], F32)
    d("router_b", [1, 8], F32)
    d("lora_AR", [D, 128], F32)
    d("lora_BS", [128, D], F32)
    d("c0", [1, D], F32)
    d("wmod0_my", [D, 1536], F32)
    d("c1", [1, D], F32)
    d("wmod1_my", [D, 1536], F32)
    d("qkv1", [D, 6144], F32)
    d("oproj1", [D, D], F32)
    d("gamma1", [1, D], F32)
    d("beta1", [1, D], F32)
    d("w1_my", [768, D], BF16)
    d("w2_my", [768, D], BF16)
    d("w3_my", [W3ROWS, D_FF], BF16)
    d("identb", [128, 128], BF16)
    d("identf", [128, 128], F32)
    d("blockones_f", [128, 128], F32)
    d("maskA_n", [128, 512], F32)
    d("maskAT", [128, 512], F32)
    d("maskQT", [128, 512], F32)
    d("cumsum_lhsT", [128, 128], F32)
    d("glselA", [128, 128], F32)
    d("glselB", [128, 128], F32)
    d("glselH", [128, 128], F32)
    d("maskA4", [128, 512], F32)
    d("maskT4", [128, 512], F32)
    d("attn_mask", [128, 1024], F32)
    p.dtmp("mod_in", [1, 1536], F32)
    p.dtmp("mod_all", [NCORES, 1536], F32)
    p.dtmp("stat_in", [2, TOK], F32)
    p.dtmp("stat_all", [NCORES, 2, TOK], F32)
    p.dtmp("mixc", [8, 128, S], BF16)
    p.dtmp("ztok", [S, ZC], BF16)
    p.dtmp("gbtok", [S, 8], F32)
    p.dtmp("rows0", [16, S], F32)
    for st in range(8):
        p.dtmp(f"slabi{st}", [ZC, 1024], BF16)
        p.dtmp(f"slaba{st}", [NCORES, ZC, 1024], BF16)
    p.dtmp("qkv1b", [D, 6144], BF16)
    p.dtmp("oproj1b", [D, D], BF16)
    p.dtmp("qTh", [D, TOK], BF16)
    p.dtmp("aoTh", [D, TOK], BF16)
    for sg in range(4):
        p.dtmp(f"kti{sg}", [D, 256], BF16)
        p.dtmp(f"kta{sg}", [NCORES, D, 256], BF16)
        p.dtmp(f"vti{sg}", [256, D], BF16)
        p.dtmp(f"vta{sg}", [NCORES, 256, D], BF16)
    p.dtmp("wq_in", [1, WQ_ELEMS], BF16)
    p.dtmp("wq_all", [NCORES, WQ_ELEMS], BF16)
    p.dtmp("nfsum_in", [1, 4], F32)
    p.dtmp("nfsum_all", [1, 4], F32)
    p.dout("y", [TOK, D], F32)




def ph0(p, block):
    dr = p.dram
    kcol = p.sbuf("kcol", [128, 8], F32)

    @block.vector
    def _(v):
        for val, idx in KCVALS.items():
            ins = v.memset(kcol[:, idx:idx + 1], float(val))
        p.inc(ins, "v", 1)

    modc = p.sbuf("modc", [128, 2, 16, 6], F32)
    p.sbuf("identb_s", [128, 128], BF16)
    p.sbuf("identf_s", [128, 128], F32)
    sc0 = p.sbuf("sc0", [128, 16], F32)
    sc1 = p.sbuf("sc1", [128, 16], F32)
    modps = p.psum("modps", [128, 12], F32)
    wmt = p.sbuf("wmt", [128, 16, 128], F32)
    modsb = p.sbuf("modsb", [128, 12], F32)
    tmp2 = p.sbuf("tmp2", [128, 2], F32)
    tmps = p.sbuf("tmps", [128, 16, 4], F32)

    @block.sync
    def _(s):
        p.inc(s.dma_start(out=sc0[:, :], in_=dr["c0"].rearrange("o (j pp) -> pp (o j)", pp=128)), "dm_c", 16)
        p.inc(s.dma_start(out=sc1[:, :], in_=dr["c1"].rearrange("o (j pp) -> pp (o j)", pp=128)), "dm_c", 16)

    @block.scalar
    def _(sc):
        sc.wait_ge(p.sem("dm_c"), p.tok("dm_c"))
        sc.activation(sc0[:, :], sc0[:, :], AF.Silu)
        p.inc(sc.activation(sc1[:, :], sc1[:, :], AF.Silu), "s", 1)

    for L, wn in enumerate(("wmod0_my", "wmod1_my")):
        rhs = (sc0, sc1)[L]
        for ct in range(12):
            @block.sync
            def _(s, wn=wn, ct=ct):
                s.wait_ge(p.sem("t"), p.tok("t"))
                p.inc(s.dma_start(
                    out=wmt[:, :, :],
                    in_=dr[wn].rearrange("(dt pp) c -> pp dt c", pp=128)[:, :, ct * 128:(ct + 1) * 128]),
                    "dm_wmt", 16)

            @block.tensor
            def _(t, L=L, ct=ct):
                t.wait_ge(p.sem("dm_wmt"), p.tok("dm_wmt"))
                t.wait_ge(p.sem("s"), 1)
                t.wait_ge(p.sem("v"), p.tok("v"))
                for dt in range(16):
                    ins = t.matmul(modps[:, ct:ct + 1], wmt[:, dt, :], rhs[:, dt:dt + 1],
                                   start=(dt == 0), stop=(dt == 15))
                p.inc(ins, "t", 1)

        @block.vector
        def _(v):
            v.wait_ge(p.sem("t"), p.tok("t"))
            v.wait_ge(p.sem("st_mod"), p.tok("st_mod"))
            p.inc(v.tensor_copy(modsb[:, :], modps[:, :]), "v", 1)

        @block.gpsimd
        def _(g, L=L):
            g.wait_ge(p.sem("v"), p.tok("v"))
            p.inc(g.dma_start(out=dr["mod_in"].rearrange("o (ct pp) -> pp (o ct)", pp=128),
                              in_=modsb[:, :]), "st_mod", 16)
            g.wait_ge(p.sem("st_mod"), p.tok("st_mod"))
            p.inc(g.collective_compute(
                "AllGather", OP.bypass, replica_groups=[list(range(NCORES))],
                ins=[dr["mod_in"][:, :]], outs=[dr["mod_all"][:, :]]), "cc_mod", 1)
            g.wait_ge(p.sem("cc_mod"), p.tok("cc_mod"))
            flat = dr["mod_all"].rearrange("a b -> (a b)").rearrange("(vv dt pp) -> vv dt pp", vv=6, pp=128)
            for dt in range(16):
                p.inc(g.dma_start(
                    out=modc[:, L, dt, :],
                    in_=flat[0:6, dt, :].rearrange("vv pp -> pp vv")), "dm_modc", 16)

        @block.vector
        def _(v, L=L):
            v.wait_ge(p.sem("dm_modc"), p.tok("dm_modc"))
            for dt in range(16):
                v.tensor_scalar(modc[:, L, dt, 1:2], modc[:, L, dt, 1:2], 1.0, None, OP.add)
                ins = v.tensor_scalar(modc[:, L, dt, 4:5], modc[:, L, dt, 4:5], 1.0, None, OP.add)
            for dt in range(16):
                for j, col in enumerate((2, 5)):
                    ins = v.tensor_scalar(tmps[:, dt, j:j + 1], modc[:, L, dt, col:col + 1], 0.0, 2.0, OP.is_ge, OP.mult)
            v.wait_ge(p.sem("v"), p.inc(ins, "v", 1))
            for dt in range(16):
                for j, col in enumerate((2, 5)):
                    ins = v.tensor_scalar(tmps[:, dt, j:j + 1], tmps[:, dt, j:j + 1], -1.0, None, OP.add)
            for dt in range(16):
                for j, col in enumerate((2, 5)):
                    ins = v.tensor_scalar(tmps[:, dt, 2 + j:3 + j], modc[:, L, dt, col:col + 1], -1.0, None, OP.mult)
            v.wait_ge(p.sem("v"), p.inc(ins, "v", 1))
            for dt in range(16):
                for j, col in enumerate((2, 5)):
                    ins = v.tensor_tensor(tmps[:, dt, 2 + j:3 + j], modc[:, L, dt, col:col + 1], tmps[:, dt, 2 + j:3 + j], OP.max)
            v.wait_ge(p.sem("v"), p.inc(ins, "v", 1))
            for dt in range(16):
                for j, col in enumerate((2, 5)):
                    ins = v.tensor_scalar(tmps[:, dt, 2 + j:3 + j], tmps[:, dt, 2 + j:3 + j], 0.25, None, OP.max)
            v.wait_ge(p.sem("v"), p.inc(ins, "v", 1))
            for dt in range(16):
                for j, col in enumerate((2, 5)):
                    ins = v.tensor_tensor(modc[:, L, dt, col:col + 1], tmps[:, dt, j:j + 1], tmps[:, dt, 2 + j:3 + j], OP.mult)
            p.inc(ins, "v", 1)


def ph1(p, block):
    dr = p.dram
    xrow = p.sbuf("xrow", [128, D], F32)
    sq = p.sbuf("sq", [128, D], F32)
    st = p.sbuf("st", [128, 8], F32)

    for cidx in range(8):
        @block.sync
        def _(s, cidx=cidx):
            s.wait_ge(p.sem("st_stat"), p.tok("st_stat"))
            p.inc(s.dma_start(out=xrow[:, :], in_=dr["x_my"][cidx * 128:(cidx + 1) * 128, :]), "dm_xrow", 16)

        @block.vector
        def _(v):
            v.wait_ge(p.sem("dm_xrow"), p.tok("dm_xrow"))
            v.reduce_sum(st[:, 0:1], xrow[:, :], axis=AX.X)
            v.tensor_tensor(sq[:, :], xrow[:, :], xrow[:, :], OP.mult)
            v.reduce_sum(st[:, 1:2], sq[:, :], axis=AX.X)
            v.tensor_scalar(st[:, 2:3], st[:, 0:1], 1.0 / D, None, OP.mult)
            v.tensor_scalar(st[:, 6:7], st[:, 1:2], 1.0 / D, None, OP.mult)
            v.tensor_scalar(st[:, 3:4], st[:, 2:3], st[:, 2:3], None, OP.mult)
            v.tensor_scalar(st[:, 5:6], st[:, 2:3], -1.0, None, OP.mult)
            p.inc(v.tensor_scalar(st[:, 3:4], st[:, 6:7], st[:, 3:4], None, OP.subtract), "v", 1)

        @block.scalar
        def _(sc):
            sc.wait_ge(p.sem("v"), p.tok("v"))
            sc.activation(st[:, 7:8], st[:, 3:4], AF.Ln, bias=p.kc(1e-5))
            p.inc(sc.activation(st[:, 4:5], st[:, 7:8], AF.Exp, scale=-0.5), "s", 1)

        @block.gpsimd
        def _(g, cidx=cidx):
            g.wait_ge(p.sem("s"), p.tok("s"))
            p.inc(g.dma_start(out=dr["stat_in"][0:1, cidx * 128:(cidx + 1) * 128]
                              .rearrange("o (t u) -> (o t) u", u=1), in_=st[:, 5:6]), "st_stat", 16)
            p.inc(g.dma_start(out=dr["stat_in"][1:2, cidx * 128:(cidx + 1) * 128]
                              .rearrange("o (t u) -> (o t) u", u=1), in_=st[:, 4:5]), "st_stat", 16)

    @block.gpsimd
    def _(g):
        g.wait_ge(p.sem("st_stat"), p.tok("st_stat"))
        p.inc(g.collective_compute(
            "AllGather", OP.bypass, replica_groups=[list(range(NCORES))],
            ins=[dr["stat_in"][:, :]], outs=[dr["stat_all"][:, :, :]]), "cc_stat", 1)


LN_EPS_L0 = 1e-5


def ph2(p, block):
    dr = p.dram
    nc = p.nc
    modc = p.sb["modc"]
    wqw = p.sbuf("wqw", [128, 16, 1024], BF16)
    wzw = p.sbuf("wzw", [128, 16, 520], BF16)
    onessa = p.sbuf("onessa", [128, 16, 4], BF16)
    sbq = p.sbuf("sbq", [2, 1024], F32)
    sbz = p.sbuf("sbz", [2, 520], F32)
    bqcol = p.sbuf("bqcol", [128, 8], F32)
    bzbc = p.sbuf("bzbc", [128, 520], F32)
    convsb = p.sbuf("convsb", [128, 8, 4], F32)
    dtbbc = p.sbuf("dtbbc", [128, 4], F32)
    badd = p.sbuf("badd", [128, 8], F32)
    nealog = p.sbuf("nealog", [128, 4], F32)
    xtb = [p.sbuf(f"xtb{i}", [128, 16, 512], BF16) for i in range(2)]
    rstdbc = p.sbuf("rstdbc", [128, 512], F32)
    negmur = p.sbuf("negmur", [1, 512], F32)
    rstdcol = p.sbuf("rstdcol", [128, 4], F32)
    convbuf = p.sbuf("convbuf", [128, 8, 515], BF16)
    cvtmp = p.sbuf("cvtmp", [128, 515], BF16)
    evtmp = p.sbuf("evtmp", [128, 512], F32)
    ztmp = p.sbuf("ztmp", [128, 512], BF16)
    gtmp = p.sbuf("gtmp", [128, 16], F32)
    mixout = p.sbuf("mixout", [128, 8, 512], BF16)
    pq = [p.psum("pq0", [128, 512], F32), p.psum("pq1", [128, 512], F32)]
    pz = p.psum("pz", [128, 512], F32)
    pba = p.psum("pba", [128, 16], F32)
    pr = p.psum("pr", [2, 1024], F32)

    @block.gpsimd
    def _(g):
        g.wait_ge(p.sem("cc_mod"), 2)
        for dt in range(16):
            p.inc(g.dma_start(out=wqw[:, dt, :], in_=dr["wqkv_my"][dt * 128:(dt + 1) * 128, :]), "dm_wprep", 16)
            p.inc(g.dma_start(out=wzw[:, dt, 0:512], in_=dr["wz_my"][dt * 128:(dt + 1) * 128, :]), "dm_wprep", 16)
            p.inc(g.dma_start(out=wzw[:, dt, 512:516], in_=dr["wb_my"][dt * 128:(dt + 1) * 128, :]), "dm_wprep", 16)
            p.inc(g.dma_start(out=wzw[:, dt, 516:520], in_=dr["wa_my"][dt * 128:(dt + 1) * 128, :]), "dm_wprep", 16)
        for a in range(8):
            p.inc(g.dma_start(out=convsb[:, a, :], in_=dr["convw_my"][a * 128:(a + 1) * 128, :]), "dm_wprep", 16)
        p.inc(g.dma_start(out=dtbbc[:, :], in_=dr["dtb_my"].broadcast_to([128, 4])), "dm_wprep", 16)
        p.inc(g.dma_start(out=nealog[:, :], in_=dr["alog_my"].broadcast_to([128, 4])), "dm_wprep", 16)

    @block.scalar
    def _(sc):
        sc.wait_ge(p.sem("dm_wprep"), p.tok("dm_wprep"))
        p.inc(sc.activation(nealog[:, :], nealog[:, :], AF.Exp, scale=1.0), "s", 1)

    @block.vector
    def _(v):
        v.wait_ge(p.sem("s"), p.tok("s"))
        v.tensor_scalar(nealog[:, :], nealog[:, :], -1.0, None, OP.mult)
        for dt in range(16):
            v.tensor_scalar(wqw[:, dt, :], wqw[:, dt, :], modc[:, 0, dt, 1:2], None, OP.mult)
            v.tensor_scalar(wzw[:, dt, :], wzw[:, dt, :], modc[:, 0, dt, 1:2], None, OP.mult)
            v.memset(onessa[:, dt, 0:1], 1.0)
            ins = v.tensor_copy(onessa[:, dt, 1:2], modc[:, 0, dt, 0:1])
        p.inc(ins, "v", 1)

    @block.tensor
    def _(t):
        t.wait_ge(p.sem("v"), p.tok("v"))
        for half in range(2):
            for dt in range(16):
                i1 = t.matmul(pr[:, half * 512:(half + 1) * 512], onessa[:, dt, 0:2],
                              wqw[:, dt, half * 512:(half + 1) * 512], start=(dt == 0), stop=(dt == 15))
        p.inc(i1, "t", 1)

    @block.vector
    def _(v):
        v.wait_ge(p.sem("t"), p.tok("t"))
        p.inc(v.tensor_copy(sbq[:, :], pr[:, :]), "v", 1)

    @block.tensor
    def _(t):
        t.wait_ge(p.sem("v"), p.tok("v"))
        for dt in range(16):
            t.matmul(pr[:, 0:512], onessa[:, dt, 0:2], wzw[:, dt, 0:512], start=(dt == 0), stop=(dt == 15))
        for dt in range(16):
            i1 = t.matmul(pr[:, 512:520], onessa[:, dt, 0:2], wzw[:, dt, 512:520], start=(dt == 0), stop=(dt == 15))
        p.inc(i1, "t", 1)

    @block.vector
    def _(v):
        v.wait_ge(p.sem("t"), p.tok("t"))
        p.inc(v.tensor_copy(sbz[:, :], pr[0:2, 0:520]), "v", 1)

    @block.gpsimd
    def _(g):
        g.wait_ge(p.sem("v"), p.tok("v"))
        p.inc(g.dma_start(out=dr["rows0"][2:3, 0:1024], in_=sbq[1:2, :]), "st_rows", 16)
        p.inc(g.dma_start(out=dr["rows0"][3:4, 0:520], in_=sbz[1:2, :]), "st_rows", 16)
        g.wait_ge(p.sem("st_rows"), p.tok("st_rows"))
        p.inc(g.dma_start(out=bqcol[:, :], in_=dr["rows0"][2:3, 0:1024].rearrange("o (ct pp) -> (o pp) ct", pp=128)), "dm_bq", 16)
        p.inc(g.dma_start(out=bzbc[:, :], in_=dr["rows0"][3:4, 0:520].broadcast_to([128, 520])), "dm_bq", 16)

    @block.vector
    def _(v):
        v.wait_ge(p.sem("dm_bq"), p.tok("dm_bq"))
        v.tensor_copy(badd[:, 0:4], bzbc[:, 512:516])
        p.inc(v.tensor_tensor(badd[:, 4:8], bzbc[:, 516:520], dtbbc[:, :], OP.add), "v", 1)

    @block.vector
    def _(v):
        p.inc(v.memset(convbuf[:, :, 0:3], 0.0), "v", 1)

    tok_cons = {}
    for tt in range(16):
        t0 = tt * 512
        xb = xtb[tt % 2]

        @block.gpsimd
        def _(g, tt=tt, t0=t0, xb=xb):
            g.wait_ge(p.sem("cc_stat"), 1)
            if tt >= 2:
                g.wait_ge(p.sem("t"), tok_cons[("xtb", tt - 2)])
            for dt in range(16):
                p.inc(g.dma_start(out=xb[:, dt, :], in_=dr["xT"][dt * 128:(dt + 1) * 128, t0:t0 + 512]),
                      "dm_xtb%d" % (tt % 2), 16)

        @block.sync
        def _(s, tt=tt, t0=t0):
            s.wait_ge(p.sem("cc_stat"), 1)
            if tt >= 1:
                s.wait_ge(p.sem("v"), tok_cons[("rows", tt - 1)])
            for off in range(0, 512, 128):
                gc = (t0 + off) // 128
                src = dr["stat_all"][gc % 8, 1:2, (gc // 8) * 128:(gc // 8) * 128 + 128]
                p.inc(s.dma_start(out=rstdbc[:, off:off + 128],
                                  in_=src.broadcast_to([128, 128])), "dm_srow", 16)
                src2 = dr["stat_all"][gc % 8, 0:1, (gc // 8) * 128:(gc // 8) * 128 + 128]
                p.inc(s.dma_start(out=negmur[0:1, off:off + 128],
                                  in_=src2), "dm_srow", 16)
                src3 = dr["stat_all"][gc % 8, 1:2, (gc // 8) * 128:(gc // 8) * 128 + 128]
                p.inc(s.dma_start(out=rstdcol[:, off // 128:off // 128 + 1],
                                  in_=src3.rearrange("o (t u) -> (o t) u", u=1)), "dm_srow", 16)

        for ct in range(8):
            ps = pq[ct % 2]

            @block.tensor
            def _(t, tt=tt, ct=ct, ps=ps, xb=xb):
                t.wait_ge(p.sem("dm_xtb%d" % (tt % 2)), p.tok("dm_xtb%d" % (tt % 2)))
                t.wait_ge(p.sem("dm_srow"), p.tok("dm_srow"))
                if tt == 0 and ct == 0:
                    t.wait_ge(p.sem("dm_bq"), p.tok("dm_bq"))
                key = ("pq", ct % 2)
                if key in tok_cons:
                    t.wait_ge(p.sem("v"), tok_cons[key])
                for dt in range(16):
                    t.matmul(ps[:, :], wqw[:, dt, ct * 128:(ct + 1) * 128], xb[:, dt, :],
                             start=(dt == 0), stop=False)
                ins = t.matmul(ps[:, :], sbq[0:1, ct * 128:(ct + 1) * 128], negmur[0:1, :],
                               start=False, stop=True)
                p.inc(ins, "t", 1)

            @block.vector
            def _(v, tt=tt, ct=ct, ps=ps):
                v.wait_ge(p.sem("t"), p.tok("t"))
                if tt > 0:
                    v.tensor_copy(convbuf[:, ct, 0:3], convbuf[:, ct, 512:515])
                v.tensor_tensor(evtmp[:, :], ps[:, :], rstdbc[:, :], OP.mult)
                key = ("pq", ct % 2)
                tok_cons[key] = p.inc(
                    v.tensor_scalar(convbuf[:, ct, 3:515], evtmp[:, :], bqcol[:, ct:ct + 1], None, OP.add),
                    "v", 1)
                v.tensor_scalar(cvtmp[:, 3:515], convbuf[:, ct, 0:512], convsb[:, ct, 0:1], None, OP.mult)
                v.scalar_tensor_tensor(cvtmp[:, 3:515], convbuf[:, ct, 1:513], convsb[:, ct, 1:2],
                                       cvtmp[:, 3:515], OP.mult, OP.add)
                v.scalar_tensor_tensor(cvtmp[:, 3:515], convbuf[:, ct, 2:514], convsb[:, ct, 2:3],
                                       cvtmp[:, 3:515], OP.mult, OP.add)
                ins = v.scalar_tensor_tensor(cvtmp[:, 3:515], convbuf[:, ct, 3:515], convsb[:, ct, 3:4],
                                             cvtmp[:, 3:515], OP.mult, OP.add)
                p.inc(ins, "v", 1)

            @block.scalar
            def _(sc, tt=tt, ct=ct):
                sc.wait_ge(p.sem("v"), p.tok("v"))
                if ("mixst", ct) in tok_cons:
                    sc.wait_ge(p.sem("st_mixc"), tok_cons[("mixst", ct)])
                p.inc(sc.activation(mixout[:, ct, :], cvtmp[:, 3:515], AF.Silu), "s", 1)

            @block.sync
            def _(s, tt=tt, ct=ct, t0=t0):
                s.wait_ge(p.sem("s"), p.tok("s"))
                tok_cons[("mixst", ct)] = p.inc(
                    s.dma_start(out=dr["mixc"][ct, :, t0:t0 + 512], in_=mixout[:, ct, :]), "st_mixc", 16)

        for tb in range(4):
            @block.tensor
            def _(t, tt=tt, tb=tb, xb=xb):
                key = ("pz",)
                if key in tok_cons:
                    t.wait_ge(p.sem("v"), tok_cons[key])
                for dt in range(16):
                    t.matmul(pz[:, :], xb[:, dt, tb * 128:(tb + 1) * 128], wzw[:, dt, 0:512],
                             start=(dt == 0), stop=False)
                t.matmul(pz[:, :], negmur[0:1, tb * 128:(tb + 1) * 128], sbz[0:1, 0:512],
                         start=False, stop=True)
                for dt in range(16):
                    t.matmul(pba[:, 0:8], xb[:, dt, tb * 128:(tb + 1) * 128], wzw[:, dt, 512:520],
                             start=(dt == 0), stop=False)
                ins = t.matmul(pba[:, 0:8], negmur[0:1, tb * 128:(tb + 1) * 128], sbz[0:1, 512:520],
                               start=False, stop=True)
                p.inc(ins, "t", 1)
                if tb == 3:
                    tok_cons[("xtb", tt)] = p.tok("t")

            @block.vector
            def _(v, tt=tt, tb=tb, t0=t0):
                v.wait_ge(p.sem("t"), p.tok("t"))
                if ("zst",) in tok_cons:
                    v.wait_ge(p.sem("st_z"), tok_cons[("zst",)])
                rc = rstdcol[:, tb:tb + 1]
                v.tensor_scalar(evtmp[:, :], pz[:, :], rc, None, OP.mult)
                v.tensor_scalar(gtmp[:, 0:8], pba[:, 0:8], rc, None, OP.mult)
                v.tensor_tensor(ztmp[:, :], evtmp[:, :], bzbc[:, 0:512], OP.add)
                tok_cons[("pz",)] = p.inc(
                    v.tensor_tensor(gtmp[:, 0:8], gtmp[:, 0:8], badd[:, :], OP.add), "v", 1)

            @block.scalar
            def _(sc):
                sc.wait_ge(p.sem("v"), p.tok("v"))
                sc.activation(gtmp[:, 12:16], gtmp[:, 4:8], AF.Exp)
                p.inc(sc.activation(gtmp[:, 12:16], gtmp[:, 12:16], AF.Ln, bias=p.kc(1.0)), "s", 1)

            @block.vector
            def _(v):
                v.wait_ge(p.sem("s"), p.tok("s"))
                p.inc(v.tensor_tensor(gtmp[:, 8:12], gtmp[:, 12:16], nealog[:, :], OP.mult), "v", 1)

            @block.sync
            def _(s, tt=tt, tb=tb, t0=t0):
                s.wait_ge(p.sem("v"), p.tok("v"))
                r0 = t0 + tb * 128
                p.inc(s.dma_start(out=dr["ztok"][r0:r0 + 128, :], in_=ztmp[:, :]), "st_z", 16)
                p.inc(s.dma_start(out=dr["gbtok"][r0:r0 + 128, 0:4], in_=gtmp[:, 8:12]), "st_z", 16)
                tok_cons[("zst",)] = p.inc(
                    s.dma_start(out=dr["gbtok"][r0:r0 + 128, 4:8], in_=gtmp[:, 0:4]), "st_z", 16)
                if tb == 3:
                    tok_cons[("rows", tt)] = p.tok("v")


def ph3(p, block):
    dr = p.dram
    p.free("wqw", "wzw", "xtb0", "xtb1", "convbuf", "cvtmp", "mixout", "evtmp", "ztmp",
           "rstdbc", "onessa", "bqcol", "bzbc", "gtmp", "rstdcol", "negmur", "sbq", "sbz",
           "badd", "dtbbc", "nealog", "convsb")
    p.free("pq0", "pq1", "pz", "pba", "pr", "modps")

    p.free("st", "sq", "xrow", "tmps", "tmp2", "modsb", "wmt", "sc1", "sc0")
    identb = p.sb["identb_s"]
    identf = p.sb["identf_s"]
    blockones = p.sbuf("blockones_s", [128, 128], F32)
    cum_l = p.sbuf("cum_l", [128, 128], F32)
    glselA = p.sbuf("glselA_s", [128, 128], F32)
    glselB = p.sbuf("glselB_s", [128, 128], F32)
    glselH = p.sbuf("glselH_s", [128, 128], F32)
    maskA = p.sbuf("maskA_s", [128, 512], F32)
    maskAT = p.sbuf("maskAT_s", [128, 512], F32)
    maskQT = p.sbuf("maskQT_s", [128, 512], F32)
    normbc = p.sbuf("normbc", [128, 128], F32)
    gb4 = p.sbuf("gb4", [128, NPAIR, 8], F32)
    beta4 = p.sbuf("beta4", [128, NPAIR, 4], F32)
    gc4 = p.sbuf("gc4", [128, NPAIR, 4], F32)
    egl4 = p.sbuf("egl4", [128, NPAIR, 8], F32)
    gl4 = p.sbuf("gl4", [128, NPAIR, 8], F32)
    glh4 = p.sbuf("glh4", [128, NPAIR, 4], F32)

    @block.sync
    def _(s):
        s.wait_ge(p.sem("st_z"), p.tok("st_z"))
        for nm, t in (("identb", identb), ("identf", identf), ("blockones_f", blockones),
                      ("cumsum_lhsT", cum_l), ("glselA", glselA), ("glselB", glselB),
                      ("glselH", glselH), ("maskA_n", maskA), ("maskAT", maskAT),
                      ("maskQT", maskQT)):
            p.inc(s.dma_start(out=t[:, :], in_=dr[nm][:, :]), "dm_p3c", 16)
        p.inc(s.dma_start(out=normbc[:, :], in_=dr["normw"].broadcast_to([128, 128])), "dm_p3c", 16)
        p.inc(s.dma_start(out=gb4[:, :, :],
                          in_=dr["gbtok"].rearrange("(P pp) c -> pp P c", pp=128)), "dm_p3c", 16)

    @block.scalar
    def _(sc):
        sc.wait_ge(p.sem("dm_p3c"), p.tok("dm_p3c"))
        p.inc(sc.activation(beta4[:, :, :], gb4[:, :, 4:8], AF.Sigmoid), "s", 1)

    pgc = p.psum("pgc", [128, 512], F32)
    for half in range(2):
        @block.tensor
        def _(t, half=half):
            t.wait_ge(p.sem("dm_p3c"), p.tok("dm_p3c"))
            t.wait_ge(p.sem("v"), p.tok("v"))
            for pr_ in range(32):
                P = half * 32 + pr_
                ins = t.matmul(pgc[:, pr_ * 16:pr_ * 16 + 4], cum_l[:, :], gb4[:, P, 0:4],
                               start=True, stop=True)
            p.inc(ins, "t", 1)

        @block.vector
        def _(v, half=half):
            v.wait_ge(p.sem("t"), p.tok("t"))
            for pr_ in range(32):
                P = half * 32 + pr_
                ins = v.tensor_copy(gc4[:, P, :], pgc[:, pr_ * 16:pr_ * 16 + 4])
            p.inc(ins, "v", 1)

    for half in range(2):
        @block.tensor
        def _(t, half=half):
            t.wait_ge(p.sem("v"), p.tok("v"))
            for pr_ in range(32):
                P = half * 32 + pr_
                t.matmul(pgc[:, pr_ * 16:pr_ * 16 + 4], glselA[:, :], gc4[:, P, 0:4],
                         start=True, stop=True)
                t.matmul(pgc[:, pr_ * 16 + 4:pr_ * 16 + 8], glselH[:, :], gc4[:, P, 0:4],
                         start=True, stop=True)
                ins = t.matmul(pgc[:, pr_ * 16 + 8:pr_ * 16 + 12], glselB[:, :], gc4[:, P, 0:4],
                               start=True, stop=True)
            p.inc(ins, "t", 1)

        @block.vector
        def _(v, half=half):
            v.wait_ge(p.sem("t"), p.tok("t"))
            for pr_ in range(32):
                P = half * 32 + pr_
                v.tensor_copy(gl4[:, P, 0:4], pgc[:, pr_ * 16:pr_ * 16 + 4])
                v.tensor_copy(glh4[:, P, :], pgc[:, pr_ * 16 + 4:pr_ * 16 + 8])
                ins = v.tensor_copy(gl4[:, P, 4:8], pgc[:, pr_ * 16 + 8:pr_ * 16 + 12])
            p.inc(ins, "v", 1)

    @block.scalar
    def _(sc):
        sc.wait_ge(p.sem("v"), p.tok("v"))
        p.inc(sc.activation(egl4[:, :, :], gl4[:, :, :], AF.Exp), "s", 1)
    p.free("pgc")

    chm = p.sbuf("chm", [128, 8, 128], BF16)
    tokt = p.sbuf("tokt", [128, 8, 128], BF16)
    lsum = p.sbuf("lsum", [128, 8], F32)
    sqs = p.sbuf("sqs", [128, 128], BF16)
    cols = p.sbuf("cols", [128, 24], F32)
    ecols = p.sbuf("ecols", [128, 12], F32)
    scol = p.sbuf("scol", [128, 12], F32)
    R4 = p.sbuf("R4", [128, 512], F32)
    dexp = p.sbuf("dexp", [128, 3, 512], F32)
    etmp = p.sbuf("etmp", [128, 512], F32)
    A4 = p.sbuf("A4", [128, 512], BF16)
    AT4 = p.sbuf("AT4", [128, 512], BF16)
    QT4 = p.sbuf("QT4", [128, 512], BF16)
    Ttl = p.sbuf("Ttl", [128, 4, 128], BF16)
    TtT = p.sbuf("TtT", [128, 4, 128], BF16)
    Xc = p.sbuf("Xc", [128, 4, 128], BF16)
    Yc = p.sbuf("Yc", [128, 4, 128], BF16)
    u4 = p.sbuf("u4", [128, 4, 128], BF16)
    wT4 = p.sbuf("wT4", [128, 4, 128], BF16)
    vb4 = p.sbuf("vb4", [128, 4, 128], BF16)
    kbg4 = p.sbuf("kbg4", [128, 4, 128], BF16)
    kd4 = p.sbuf("kd4", [128, 4, 128], BF16)
    S4 = p.sbuf("S4", [128, 4, 128], F32)
    S4b = p.sbuf("S4b", [128, 4, 128], BF16)
    vnew4 = p.sbuf("vnew4", [128, 4, 128], BF16)
    e14 = p.sbuf("e14", [128, 4, 128], F32)
    otile = p.sbuf("otile", [128, 4, 128], F32)
    zt3 = p.sbuf("zt3", [128, 512], BF16)
    gat = p.sbuf("gat", [128, 512], BF16)
    slabt = p.sbuf("slabt", [128, 4, 128], BF16)
    rsum = p.sbuf("rsum", [128, 12], F32)
    ptb = p.psum("ptb", [128, 1024], BF16)
    pA = p.psum("pA", [128, 512], F32)
    pQK = p.psum("pQK", [128, 512], F32)
    pT = p.psum("pT", [128, 1024], F32)
    pdec = pT
    psA2 = p.psum("psA2", [128, 512], F32)
    psB2 = p.psum("psB2", [128, 512], F32)
    pSS = p.psum("pSS", [128, 512], F32)

    @block.vector
    def _(v):
        v.memset(S4[:, :, :], 0.0)
        p.inc(v.memset(S4b[:, :, :], 0.0), "v", 1)

    LNDKH = float(np.log(DK ** -0.5))
    tokd = {}
    for P in range(PH3_PAIRS):
        t0 = P * 128

        @block.sync
        def _(g, P=P, t0=t0):
            if P > 0:
                g.wait_ge(p.sem("t"), tokd[("chmfree", P - 1)])
            else:
                g.wait_ge(p.sem("st_mixc"), p.tok("st_mixc"))
                g.wait_ge(p.sem("st_z"), p.tok("st_z"))
            for ct in range(8):
                p.inc(g.dma_start(out=chm[:, ct, :], in_=dr["mixc"][ct, :, t0:t0 + 128]), "dm_chm", 16)
            if P > 0:
                g.wait_ge(p.sem("v"), tokd[("ztfree", P - 1)])
            p.inc(g.dma_start(out=zt3[:, :], in_=dr["ztok"][t0:t0 + 128, :]), "dm_chm", 16)

        @block.tensor
        def _(t, P=P):
            t.wait_ge(p.sem("dm_chm"), p.tok("dm_chm"))
            if P == 0:
                t.wait_ge(p.sem("v"), p.tok("v"))
            if P > 0:
                t.wait_ge(p.sem("v"), tokd[("ttevac", P - 1)])
                t.wait_ge(p.sem("v"), tokd[("tokfree", P - 1)])
            for j in range(8):
                ins = t.transpose(ptb[:, j * 128:(j + 1) * 128], chm[:, j, :], identb[:, :])
            p.inc(ins, "t", 1)

        @block.vector
        def _(v, P=P):
            v.wait_ge(p.sem("t"), p.tok("t"))
            ins = v.tensor_copy(tokt[:, :, :].rearrange("pp a b -> pp (a b)"), ptb[:, 0:1024])
            tokd[("trdone", P)] = p.inc(ins, "v", 1)

        @block.scalar
        def _(sc, P=P):
            sc.wait_ge(p.sem("v"), p.tok("v"))
            sc.memzero(lsum[:, 0:4])
            for j in range(4):
                sc.activation(sqs[:, :], tokt[:, j, :], AF.Square, accum_out=lsum[:, j:j + 1])
            p.inc(sc.activation(lsum[:, 4:8], lsum[:, 0:4], AF.Ln, bias=p.kc(1e-6)), "s", 1)

        @block.vector
        def _(v, P=P):
            v.wait_ge(p.sem("s"), p.tok("s"))
            v.tensor_copy(cols[:, 0:2], lsum[:, 6:7].broadcast_to([128, 2]))
            v.tensor_copy(cols[:, 2:4], lsum[:, 7:8].broadcast_to([128, 2]))
            v.tensor_copy(scol[:, 8:10], lsum[:, 4:5].broadcast_to([128, 2]))
            v.tensor_copy(scol[:, 10:12], lsum[:, 5:6].broadcast_to([128, 2]))
            v.scalar_tensor_tensor(cols[:, 4:8], cols[:, 0:4], 0.5, gc4[:, P, :], OP.mult, OP.add)
            v.scalar_tensor_tensor(cols[:, 12:16], scol[:, 8:12], -0.5, gc4[:, P, :], OP.mult, OP.add)
            v.scalar_tensor_tensor(ecols[:, 0:4], cols[:, 0:4], -0.5, gc4[:, P, :], OP.mult, OP.add)
            v.tensor_copy(cols[:, 20:24], cols[:, 4:8])
            v.scalar_tensor_tensor(ecols[:, 4:8], cols[:, 0:4], -0.5, glh4[:, P, :], OP.mult, OP.add)
            v.tensor_scalar(cols[:, 12:16], cols[:, 12:16], LNDKH, None, OP.add)
            v.tensor_tensor(ecols[:, 4:8], ecols[:, 4:8], gc4[:, P, :], OP.subtract)
            p.inc(v.tensor_scalar(ecols[:, 8:12], gb4[:, P, 4:8], -1.0, None, OP.mult), "v", 1)

        @block.scalar
        def _(sc, P=P):
            sc.wait_ge(p.sem("v"), p.tok("v"))
            sc.activation(scol[:, 0:8], ecols[:, 0:8], AF.Exp)
            sc.activation(ecols[:, 8:12], ecols[:, 8:12], AF.Exp)
            sc.activation(ecols[:, 8:12], ecols[:, 8:12], AF.Ln, bias=p.kc(1.0))
            p.inc(sc.activation(scol[:, 8:12], cols[:, 12:16], AF.Exp), "s", 1)

        @block.vector
        def _(v, P=P):
            v.wait_ge(p.sem("s"), p.tok("s"))
            v.tensor_tensor(scol[:, 0:4], scol[:, 0:4], beta4[:, P, :], OP.mult)
            v.scalar_tensor_tensor(cols[:, 8:12], cols[:, 0:4], -0.5, gc4[:, P, :], OP.mult, OP.add)
            v.tensor_scalar(cols[:, 16:20], cols[:, 0:4], -0.5, None, OP.mult)
            v.tensor_tensor(cols[:, 8:12], cols[:, 8:12], ecols[:, 8:12], OP.subtract)
            p.inc(v.tensor_copy(scol[:, 4:8], scol[:, 4:8]), "v", 1)

        @block.scalar
        def _(sc, P=P):
            sc.wait_ge(p.sem("v"), p.tok("v"))
            p.inc(sc.activation(cols[:, 16:20], cols[:, 16:20], AF.Exp), "s", 1)

        @block.vector
        def _(v, P=P):
            v.wait_ge(p.sem("s"), p.tok("s"))
            v.tensor_tensor(cols[:, 16:20], cols[:, 16:20], beta4[:, P, :], OP.mult)
            p.inc(v.tensor_scalar(cols[:, 16:20], cols[:, 16:20], -1.0, None, OP.mult), "v", 1)

        for ci, colbase in ((0, 4), (1, 8), (2, 12)):
            @block.vector
            def _(v, P=P, ci=ci, colbase=colbase):
                v.wait_ge(p.sem("v"), p.tok("v"))
                if ci == 0:
                    v.wait_ge(p.sem("s"), p.tok("s"))
                for h in range(4):
                    ins = v.tensor_scalar(R4[:, h * 128:(h + 1) * 128], identf[:, :],
                                          cols[:, colbase + h:colbase + h + 1], None, OP.mult)
                p.inc(ins, "v", 1)

            @block.tensor
            def _(t, P=P, ci=ci):
                t.wait_ge(p.sem("v"), p.tok("v"))
                t.wait_ge(p.sem("s"), p.tok("s"))
                ins = t.matmul(pdec[:, 0:512], blockones[:, :], R4[:, :], start=True, stop=True)
                p.inc(ins, "t", 1)

            @block.vector
            def _(v, P=P, ci=ci):
                v.wait_ge(p.sem("t"), p.tok("t"))
                msk = (maskA, maskAT, maskQT)[ci]
                for h in range(4):
                    sub = gc4[:, P, h:h + 1] if ci == 0 else cols[:, 20 + h:21 + h]
                    ins = v.scalar_tensor_tensor(etmp[:, h * 128:(h + 1) * 128],
                                                 pdec[:, h * 128:(h + 1) * 128],
                                                 sub, msk[:, h * 128:(h + 1) * 128],
                                                 OP.subtract, OP.add)
                p.inc(ins, "v", 1)

            @block.scalar
            def _(sc, P=P, ci=ci):
                sc.wait_ge(p.sem("v"), p.tok("v"))
                scale = -1.0 if ci == 0 else 1.0
                p.inc(sc.activation(dexp[:, ci, :], etmp[:, :], AF.Exp, scale=scale), "s", 1)

        @block.tensor
        def _(t, P=P):
            t.wait_ge(p.sem("v"), tokd[("trdone", P)])
            if P > 0:
                t.wait_ge(p.sem("v"), tokd[("pAfree", P - 1)])
            for h in range(4):
                kc_ = chm[:, 2 + h // 2, :]
                t.matmul(pA[:, h * 128:(h + 1) * 128], kc_, kc_, start=True, stop=True)
            for h in range(4):
                ins = t.matmul(pQK[:, h * 128:(h + 1) * 128], chm[:, 2 + h // 2, :],
                               chm[:, h // 2, :], start=True, stop=True)
            p.inc(ins, "t", 1)

        @block.vector
        def _(v, P=P):
            v.wait_ge(p.sem("t"), p.tok("t"))
            v.wait_ge(p.sem("s"), p.tok("s"))
            if P > 0:
                v.wait_ge(p.sem("t"), tokd[("qtfree", P - 1)])
            v.tensor_tensor(etmp[:, :], pA[:, :], dexp[:, 0, :], OP.mult)
            v.scalar_tensor_tensor(AT4[:, :], pA[:, :], -1.0, dexp[:, 1, :], OP.mult, OP.mult)
            v.tensor_tensor(QT4[:, :], pQK[:, :], dexp[:, 2, :], OP.mult)
            for h in range(4):
                ins = v.tensor_scalar(A4[:, h * 128:(h + 1) * 128], etmp[:, h * 128:(h + 1) * 128],
                                      cols[:, 16 + h:17 + h], None, OP.mult)
            tokd[("pAfree", P)] = p.inc(ins, "v", 1)

        @block.vector
        def _(v, P=P):
            v.wait_ge(p.sem("v"), p.tok("v"))
            for h in range(4):
                v.tensor_scalar(vb4[:, h, :], tokt[:, 4 + h, :], beta4[:, P, h:h + 1], None, OP.mult)
            for h in range(4):
                v.tensor_scalar(kbg4[:, h, :], tokt[:, 2 + h // 2, :], scol[:, h:h + 1], None, OP.mult)
            for h in range(4):
                ins = v.tensor_scalar(kd4[:, h, :], tokt[:, 2 + h // 2, :], scol[:, 4 + h:5 + h], None, OP.mult)
            tokd[("tokfree", P)] = p.inc(ins, "v", 1)

        @block.vector
        def _(v, P=P):
            v.wait_ge(p.sem("v"), p.tok("v"))
            if P > 0:
                v.wait_ge(p.sem("s"), tokd[("uwevac", P - 1)])
            for h in range(4):
                v.tensor_copy(Xc[:, h, :], A4[:, h * 128:(h + 1) * 128])
            for h in range(4):
                v.tensor_copy(Yc[:, h, :], AT4[:, h * 128:(h + 1) * 128])
            for h in range(4):
                ins = v.tensor_tensor(Ttl[:, h, :], identb[:, :], A4[:, h * 128:(h + 1) * 128], OP.add)
            p.inc(ins, "v", 1)

        for it in range(5):
            @block.tensor
            def _(t, P=P, it=it):
                t.wait_ge(p.sem("v"), p.tok("v"))
                if it > 0:
                    t.wait_ge(p.sem("s"), p.tok("s"))
                for h in range(4):
                    t.matmul(pT[:, h * 128:(h + 1) * 128], Yc[:, h, :], Xc[:, h, :],
                             start=True, stop=True)
                for h in range(4):
                    ins = t.matmul(pT[:, 512 + h * 128:512 + (h + 1) * 128], Xc[:, h, :], Yc[:, h, :],
                                   start=True, stop=True)
                p.inc(ins, "t", 1)

            @block.scalar
            def _(sc, P=P, it=it):
                sc.wait_ge(p.sem("t"), p.tok("t"))
                if it < 4:
                    for h in range(4):
                        sc.copy(Xc[:, h, :], pT[:, h * 128:(h + 1) * 128])
                for h in range(4):
                    ins = sc.copy(Yc[:, h, :], pT[:, 512 + h * 128:512 + (h + 1) * 128])
                p.inc(ins, "s", 1)

            @block.tensor
            def _(t, P=P, it=it):
                t.wait_ge(p.sem("s"), p.tok("s"))
                for h in range(4):
                    ins = t.matmul(pT[:, h * 128:(h + 1) * 128], Yc[:, h, :], Ttl[:, h, :],
                                   start=True, stop=True)
                p.inc(ins, "t", 1)

            @block.vector
            def _(v, P=P, it=it):
                v.wait_ge(p.sem("t"), p.tok("t"))
                for h in range(4):
                    ins = v.tensor_tensor(Ttl[:, h, :], Ttl[:, h, :], pT[:, h * 128:(h + 1) * 128], OP.add)
                p.inc(ins, "v", 1)

        @block.tensor
        def _(t, P=P):
            t.wait_ge(p.sem("v"), p.tok("v"))
            for h in range(4):
                ins = t.transpose(ptb[:, h * 128:(h + 1) * 128], Ttl[:, h, :], identb[:, :])
            p.inc(ins, "t", 1)

        @block.vector
        def _(v, P=P):
            v.wait_ge(p.sem("t"), p.tok("t"))
            ins = v.tensor_copy(TtT[:, :, :].rearrange("pp a b -> pp (a b)"), ptb[:, 0:512])
            tokd[("ttevac", P)] = p.inc(ins, "v", 1)

        @block.tensor
        def _(t, P=P):
            t.wait_ge(p.sem("v"), p.tok("v"))
            for h in range(4):
                t.matmul(pT[:, h * 128:(h + 1) * 128], TtT[:, h, :], vb4[:, h, :],
                         start=True, stop=True)
            for h in range(4):
                ins = t.matmul(pT[:, 512 + h * 128:512 + (h + 1) * 128], kbg4[:, h, :], TtT[:, h, :],
                               start=True, stop=True)
            p.inc(ins, "t", 1)

        @block.scalar
        def _(sc, P=P):
            sc.wait_ge(p.sem("t"), p.tok("t"))
            for h in range(4):
                sc.copy(u4[:, h, :], pT[:, h * 128:(h + 1) * 128])
            for h in range(4):
                ins = sc.copy(wT4[:, h, :], pT[:, 512 + h * 128:512 + (h + 1) * 128])
            tokd[("uwevac", P)] = p.inc(ins, "s", 1)

        for half in range(2):
            lo, hi = half * 64, half * 64 + 64

            @block.tensor
            def _(t, P=P, half=half, lo=lo, hi=hi):
                t.wait_ge(p.sem("s"), p.tok("s"))
                t.wait_ge(p.sem("v"), p.tok("v"))
                for h in range(4):
                    t.matmul(psA2[lo:hi, h * 128:(h + 1) * 128], wT4[:, h, lo:hi], S4b[:, h, :],
                             start=True, stop=True, tile_position=(0, lo))
                for h in range(4):
                    ins = t.matmul(psB2[lo:hi, h * 128:(h + 1) * 128], chm[:, h // 2, lo:hi],
                                   S4b[:, h, :], start=True, stop=True, tile_position=(0, lo))
                tokd[("chmfree", P)] = p.inc(ins, "t", 1)

            @block.vector
            def _(v, P=P, half=half, lo=lo, hi=hi):
                v.wait_ge(p.sem("t"), p.tok("t"))
                for h in range(4):
                    ins = v.tensor_tensor(vnew4[lo:hi, h, :], u4[lo:hi, h, :],
                                          psA2[lo:hi, h * 128:(h + 1) * 128], OP.subtract)
                p.inc(ins, "v", 1)

            @block.tensor
            def _(t, P=P, half=half, lo=lo, hi=hi):
                t.wait_ge(p.sem("v"), p.tok("v"))
                for h in range(4):
                    t.matmul(psA2[lo:hi, h * 128:(h + 1) * 128],
                             QT4[lo:hi, h * 128 + lo:h * 128 + hi], vnew4[lo:hi, h, :],
                             start=True, stop=True, tile_position=(lo, lo))
                for h in range(4):
                    ins = t.matmul(pSS[:, h * 128:(h + 1) * 128], kd4[lo:hi, h, :], vnew4[lo:hi, h, :],
                                   start=True, stop=True, tile_position=(lo, 0))
                tokd[("qtfree", P)] = p.inc(ins, "t", 1)

            @block.vector
            def _(v, P=P, half=half, lo=lo, hi=hi):
                v.wait_ge(p.sem("t"), p.tok("t"))
                for h in range(4):
                    v.tensor_scalar(e14[lo:hi, h, :], psB2[lo:hi, h * 128:(h + 1) * 128],
                                    scol[lo:hi, 8 + h:9 + h], None, OP.mult)
                for h in range(4):
                    v.tensor_tensor(otile[lo:hi, h, :], e14[lo:hi, h, :],
                                    psA2[lo:hi, h * 128:(h + 1) * 128], OP.add)
                for h in range(4):
                    v.scalar_tensor_tensor(S4[:, h, :], S4[:, h, :],
                                           egl4[:, P, half * 4 + h:half * 4 + h + 1],
                                           pSS[:, h * 128:(h + 1) * 128], OP.mult, OP.add)
                for h in range(4):
                    ins = v.tensor_copy(S4b[:, h, :], S4[:, h, :])
                p.inc(ins, "v", 1)

        @block.scalar
        def _(sc, P=P):
            sc.wait_ge(p.sem("v"), p.tok("v"))
            sc.wait_ge(p.sem("dm_chm"), p.tok("dm_chm"))
            sc.memzero(rsum[:, 0:4])
            for h in range(4):
                sc.activation(sqs[:, :], otile[:, h, :], AF.Square, accum_out=rsum[:, h:h + 1])
            sc.activation(rsum[:, 4:8], rsum[:, 0:4], AF.Ln, scale=1.0 / DV, bias=p.kc(1e-6))
            sc.activation(rsum[:, 8:12], rsum[:, 4:8], AF.Exp, scale=-0.5)
            p.inc(sc.activation(gat[:, :], zt3[:, :], AF.Silu), "s", 1)

        @block.vector
        def _(v, P=P):
            v.wait_ge(p.sem("s"), p.tok("s"))
            for h in range(4):
                v.tensor_scalar(e14[:, h, :], otile[:, h, :], rsum[:, 8 + h:9 + h], None, OP.mult)
            for h in range(4):
                v.tensor_tensor(e14[:, h, :], e14[:, h, :], normbc[:, :], OP.mult)
            for h in range(4):
                ins = v.tensor_tensor(slabt[:, h, :], e14[:, h, :], gat[:, h * 128:(h + 1) * 128],
                                      OP.mult)
            tokd[("ztfree", P)] = p.inc(ins, "v", 1)

        @block.tensor
        def _(t, P=P):
            t.wait_ge(p.sem("v"), p.tok("v"))
            for h in range(4):
                ins = t.transpose(ptb[:, 512 + h * 128:512 + (h + 1) * 128], slabt[:, h, :],
                                  identb[:, :])
            p.inc(ins, "t", 1)

        @block.vector
        def _(v, P=P):
            v.wait_ge(p.sem("t"), p.tok("t"))
            if P > 0:
                v.wait_ge(p.sem("st_slab"), tokd[("slabst", P - 1)])
            ins = v.tensor_copy(gat[:, 0:512], ptb[:, 512:1024])
            p.inc(ins, "v", 1)

        @block.gpsimd
        def _(g, P=P, t0=t0):
            g.wait_ge(p.sem("v"), p.tok("v"))
            st = P // 8
            c0 = (P % 8) * 128
            for h in range(4):
                ins = g.dma_start(out=dr[f"slabi{st}"][h * 128:(h + 1) * 128, c0:c0 + 128],
                                  in_=gat[:, h * 128:(h + 1) * 128])
                p.inc(ins, "st_slab", 16)
            tokd[("slabst", P)] = p.tok("st_slab")
            if P % 8 == 7:
                g.wait_ge(p.sem("st_slab"), p.tok("st_slab"))
                p.inc(g.collective_compute(
                    "AllGather", OP.bypass, replica_groups=[list(range(NCORES))],
                    ins=[dr[f"slabi{st}"][:, :]], outs=[dr[f"slaba{st}"][:, :, :]]), f"cc_sl{st}", 1)


def ln_stats_own(p, block, src_tiles, nt, rowbase, kcol, bf=False):
    dr = p.dram
    sqt = p.sb["sq4"]
    onesl = p.sb["ones2"] if not bf else p.sb["onesb"]
    prr = p.ps["prr"]
    ps0, ps1 = p.ps["ps0"], p.ps["ps1"]
    srow = p.sb["srow"]
    mubc = p.sb["mubc"]
    rsbc = p.sb["rsbc"]

    for qq in range(nt // 256):
        @block.vector
        def _(v, qq=qq):
            v.wait_ge(p.sem("t"), p.tok("t"))
            v.wait_ge(p.sem("v"), p.tok("v"))
            for dt in range(16):
                ins = v.tensor_tensor(sqt[:, dt, :], src_tiles[:, dt, qq * 256:(qq + 1) * 256],
                                      src_tiles[:, dt, qq * 256:(qq + 1) * 256], OP.mult)
            p.inc(ins, "v", 1)

        @block.tensor
        def _(t, qq=qq):
            t.wait_ge(p.sem("v"), p.tok("v"))
            for dt in range(16):
                t.matmul(prr[0:1, qq * 256:(qq + 1) * 256], onesl[:, 0:1],
                         src_tiles[:, dt, qq * 256:(qq + 1) * 256],
                         start=(dt == 0), stop=(dt == 15))
            psh = (ps0, ps1)[qq // 2]
            col = (qq % 2) * 256
            for dt in range(16):
                ins = t.matmul(psh[0:1, col:col + 256], p.sb["ones2"][:, 0:1],
                               sqt[:, dt, :], start=(dt == 0), stop=(dt == 15))
            p.inc(ins, "t", 1)

    @block.vector
    def _(v):
        v.wait_ge(p.sem("t"), p.tok("t"))
        ins = v.tensor_scalar(srow[0:1, 0:nt], prr[0:1, 0:nt], 1.0 / D, None, OP.mult)
        v.wait_ge(p.sem("v"), p.inc(ins, "v", 1))
        ins = v.tensor_tensor(rsbc[0:1, 0:nt], srow[0:1, 0:nt], srow[0:1, 0:nt], OP.mult)
        v.wait_ge(p.sem("v"), p.inc(ins, "v", 1))
        v.tensor_scalar(mubc[0:1, 0:512], ps0[0:1, 0:512], 1.0 / D, None, OP.mult)
        ins = v.tensor_scalar(mubc[0:1, 512:1024], ps1[0:1, 0:512], 1.0 / D, None, OP.mult)
        v.wait_ge(p.sem("v"), p.inc(ins, "v", 1))
        ins = v.tensor_tensor(mubc[0:1, 0:nt], mubc[0:1, 0:nt], rsbc[0:1, 0:nt], OP.subtract)
        p.inc(ins, "v", 1)

    @block.scalar
    def _(sc):
        sc.wait_ge(p.sem("v"), p.tok("v"))
        sc.activation(rsbc[0:1, 0:nt], mubc[0:1, 0:nt], AF.Ln, bias=kcol)
        p.inc(sc.activation(mubc[0:1, 0:nt], rsbc[0:1, 0:nt], AF.Exp, scale=-0.5), "s", 1)

    @block.vector
    def _(v):
        v.wait_ge(p.sem("s"), p.tok("s"))
        ins = v.tensor_tensor(srow[0:1, 0:nt], srow[0:1, 0:nt], mubc[0:1, 0:nt], OP.mult)
        v.wait_ge(p.sem("v"), p.inc(ins, "v", 1))
        ins = v.tensor_scalar(srow[0:1, 0:nt], srow[0:1, 0:nt], -1.0, None, OP.mult)
        p.inc(ins, "v", 1)

    @block.gpsimd
    def _(g):
        g.wait_ge(p.sem("v"), p.tok("v"))
        g.wait_ge(p.sem("dm_lnbc"), p.tok("dm_lnbc"))
        p.inc(g.dma_start(out=dr["rows0"][rowbase:rowbase + 1, 0:nt], in_=srow[0:1, 0:nt]), "st_lnr", 16)
        p.inc(g.dma_start(out=dr["rows0"][rowbase + 1:rowbase + 2, 0:nt], in_=mubc[0:1, 0:nt]), "st_lnr", 16)


def ln_apply(p, block, src_tiles, dst_tiles, nt, rowbase, modslice, extra_gb=None):
    dr = p.dram
    mubc = p.sb["mubc"]
    rsbc = p.sb["rsbc"]
    modc = p.sb["modc"]

    @block.sync
    def _(s):
        s.wait_ge(p.sem("st_lnr"), p.tok("st_lnr"))
        s.wait_ge(p.sem("v"), p.tok("v"))
        p.inc(s.dma_start(out=mubc[:, 0:nt], in_=dr["rows0"][rowbase:rowbase + 1, 0:nt]
                          .broadcast_to([128, nt])), "dm_lnbc", 16)
        p.inc(s.dma_start(out=rsbc[:, 0:nt], in_=dr["rows0"][rowbase + 1:rowbase + 2, 0:nt]
                          .broadcast_to([128, nt])), "dm_lnbc", 16)

    @block.vector
    def _(v):
        v.wait_ge(p.sem("dm_lnbc"), p.tok("dm_lnbc"))
        for dt in range(16):
            ins = v.tensor_tensor(dst_tiles[:, dt, :], src_tiles[:, dt, :], rsbc[:, 0:nt], OP.mult)
        p.inc(ins, "v", 1)
        for dt in range(16):
            ins = v.tensor_tensor(dst_tiles[:, dt, :], dst_tiles[:, dt, :], mubc[:, 0:nt], OP.add)
        v.wait_ge(p.sem("v"), p.inc(ins, "v", 1))
        if extra_gb is None:
            L, c1, c2 = modslice
            for dt in range(16):
                ins = v.tensor_scalar(dst_tiles[:, dt, :], dst_tiles[:, dt, :],
                                      modc[:, L, dt, c1:c1 + 1], modc[:, L, dt, c2:c2 + 1],
                                      OP.mult, OP.add)
        else:
            gbc, bbc = extra_gb
            for dt in range(16):
                ins = v.tensor_scalar(dst_tiles[:, dt, :], dst_tiles[:, dt, :],
                                      gbc[:, dt:dt + 1], bbc[:, dt:dt + 1],
                                      OP.mult, OP.add)
        p.inc(ins, "v", 1)


def ph4(p, block):
    dr = p.dram
    modc = p.sb["modc"]
    kcol = p.sb["kcol"]
    identb = p.sb["identb_s"]
    p.free("chm", "tokt", "lsum", "sqs", "cols", "ecols", "scol", "R4", "dexp", "etmp",
           "A4", "AT4", "QT4", "Ttl", "TtT", "Xc", "Yc", "u4", "wT4", "vb4", "kbg4",
           "kd4", "S4", "S4b", "vnew4", "e14", "otile", "zt3", "gat", "slabt", "rsum",
           "gb4", "beta4", "gc4", "egl4", "gl4", "glh4", "normbc", "maskA_s", "maskAT_s",
           "maskQT_s", "glselA_s", "glselB_s", "glselH_s", "cum_l", "blockones_s")
    p.free("ptb", "pA", "pQK", "pT", "psA2", "psB2", "pSS")

    xT = p.sbuf("xTr", [128, 16, TOK], F32)
    sq4 = p.sbuf("sq4", [128, 16, 256], F32)
    ones2 = p.sbuf("ones2", [128, 2], F32)
    onesb = p.sbuf("onesb", [128, 2], BF16)
    srow = p.sbuf("srow", [6, TOK], F32)
    mubc = p.sbuf("mubc", [128, TOK], F32)
    rsbc = p.sbuf("rsbc", [128, TOK], F32)
    slabix = p.sbuf("slabix", [128, 32], mybir.dt.int32)
    act4 = p.sbuf("act4", [128, 32, 256], BF16)
    w_t = p.sbuf("wtile0", [128, 32, 128], BF16)
    wtile = [w_t, w_t]
    p.sbuf("gbbc", [128, 32], F32)
    xn = p.sbuf("xn", [128, 16, TOK], BF16)
    h2 = p.sbuf("h2", [128, 16, TOK], BF16)
    loraA = p.sbuf("loraA", [128, 16, 128], BF16)
    loraB = p.sbuf("loraB", [128, 16, 128], BF16)
    rwt = p.sbuf("rwt", [128, 16, 8], BF16)
    rbbc = p.sbuf("rbbc", [128, 8], F32)
    gcols = p.sbuf("gcols", [128, 16], F32)
    midg = p.sbuf("midg", [128, 128], BF16)
    midT = p.sbuf("midT", [128, 128], BF16)
    bb2 = p.sbuf("bb2", [128, 16, 2], F32)
    ps0 = p.psum("ps0", [128, 512], F32)
    ps1 = p.psum("ps1", [128, 512], F32)
    prr = p.psum("prr", [64, 1024], F32)
    pmid = p.psum("pmid", [128, 128], F32)
    plog = p.psum("plog", [128, 16], F32)
    ptr4 = p.psum("ptr4", [128, 128], BF16)

    @block.sync
    def _(s):
        s.wait_ge(p.sem("v"), p.tok("v"))
        s.wait_ge(p.sem("st_slab"), p.tok("st_slab"))
        for dt in range(16):
            p.inc(s.dma_start(out=xT[:, dt, :], in_=dr["xT_my"][dt * 128:(dt + 1) * 128, :]), "dm_xres", 16)
        p.inc(s.dma_start(out=slabix[:, :], in_=dr["slabidx"][:, :]), "dm_xres", 16)

    @block.gpsimd
    def _(g):
        g.wait_ge(p.sem("v"), p.tok("v"))
        g.wait_ge(p.sem("st_slab"), p.tok("st_slab"))
        for dt in range(16):
            p.inc(g.dma_start(out=loraA[:, dt, :], in_=dr["lora_AR"][dt * 128:(dt + 1) * 128, :]), "dm_ffnw", 16)
            p.inc(g.dma_start(out=rwt[:, dt, :], in_=dr["router_wT"][dt * 128:(dt + 1) * 128, :]), "dm_ffnw", 16)
        p.inc(g.dma_start(out=loraB[:, :, :].rearrange("pp a b -> pp (a b)"),
                          in_=dr["lora_BS"][:, :]), "dm_ffnw", 16)
        p.inc(g.dma_start(out=rbbc[:, :], in_=dr["router_b"].broadcast_to([128, 8])), "dm_ffnw", 16)
        for dt in range(16):
            p.inc(g.dma_start(out=bb2[:, dt, 0:1],
                              in_=dr["base_b"].rearrange("o (dt pp) -> (o pp) dt", pp=128)[:, dt:dt + 1]),
                  "dm_ffnw", 16)

    @block.vector
    def _(v):
        v.wait_ge(p.sem("dm_ffnw"), p.tok("dm_ffnw"))
        v.memset(ones2[:, :], 1.0)
        v.memset(onesb[:, :], 1.0)
        for dt in range(16):
            ins = v.tensor_scalar(bb2[:, dt, 1:2], bb2[:, dt, 0:1], modc[:, 0, dt, 5:6], None, OP.mult)
        p.inc(ins, "v", 1)

    tokp = {}
    for tg in range(4):
        @block.gpsimd
        def _(g, tg=tg):
            g.wait_ge(p.sem("dm_xres"), p.tok("dm_xres"))
            for oc in range(tg * 2, tg * 2 + 2):
                g.wait_ge(p.sem(f"cc_sl{oc}"), 1)
            if tg > 0:
                g.wait_ge(p.sem("t"), tokp[("act", tg - 1)])
            for ct in range(32):
                for j, oc in enumerate(range(tg * 2, tg * 2 + 2)):
                    strip = dr[f"slaba{oc}"].rearrange("a b c -> (a b c)") \
                        .rearrange("(r e) -> r e", e=128)
                    ins = g.indirect_dma_start(
                        act4[:, ct, j * 128:(j + 1) * 128], None,
                        strip[:, :],
                        bass.IndirectOffsetOnAxis(slabix[:, ct:ct + 1], 0))
                    p.inc(ins, "dm_act", 16)

        for dt in range(16):
            @block.gpsimd
            def _(g, tg=tg, dt=dt):
                if (tg, dt) not in ((0, 0),):
                    g.wait_ge(p.sem("t"), p.tok("t"))
                p.inc(g.dma_start(
                    out=wtile[dt % 2][:, :, :],
                    in_=dr["w_out"].rearrange("(ct pp) d -> pp ct d", pp=128)[:, :, dt * 128:(dt + 1) * 128]),
                    "dm_wo%d" % (dt % 2), 16)

            @block.tensor
            def _(t, tg=tg, dt=dt):
                t.wait_ge(p.sem("dm_act"), p.tok("dm_act"))
                t.wait_ge(p.sem("dm_wo%d" % (dt % 2)), p.tok("dm_wo%d" % (dt % 2)))
                if tg == 0 and dt == 0:
                    t.wait_ge(p.sem("v"), p.tok("v"))
                if ("ps", dt % 2) in tokp:
                    t.wait_ge(p.sem("v"), tokp[("ps", dt % 2)])
                ps = (ps0, ps1)[dt % 2]
                for ct in range(32):
                    ins = t.matmul(ps[:, 0:256], wtile[dt % 2][:, ct, :], act4[:, ct, :],
                                   start=(ct == 0), stop=(ct == 31))
                tokp[("act", tg)] = p.inc(ins, "t", 1)

            @block.vector
            def _(v, tg=tg, dt=dt):
                v.wait_ge(p.sem("t"), p.tok("t"))
                ps = (ps0, ps1)[dt % 2]
                ins = v.scalar_tensor_tensor(xT[:, dt, tg * 256:(tg + 1) * 256], ps[:, 0:256],
                                             modc[:, 0, dt, 2:3], xT[:, dt, tg * 256:(tg + 1) * 256],
                                             OP.mult, OP.add)
                tokp[("ps", dt % 2)] = p.inc(ins, "v", 1)

    ln_stats_own(p, block, xT, TOK, 6, kcol[0:1, KCVALS[1e-5]:KCVALS[1e-5] + 1])
    ln_apply(p, block, xT, h2, TOK, 6, (0, 4, 3))

    for oc in range(8):
        @block.tensor
        def _(t, oc=oc):
            t.wait_ge(p.sem("v"), p.tok("v"))
            if oc > 0:
                t.wait_ge(p.sem("v"), tokp[("mid", oc - 1)])
            for dt in range(16):
                t.matmul(pmid[:, :], h2[:, dt, oc * 128:(oc + 1) * 128], loraA[:, dt, :],
                         start=(dt == 0), stop=(dt == 15))
            for dt in range(16):
                ins = t.matmul(plog[:, 0:8], h2[:, dt, oc * 128:(oc + 1) * 128], rwt[:, dt, :],
                               start=(dt == 0), stop=(dt == 15))
            p.inc(ins, "t", 1)

        @block.vector
        def _(v, oc=oc):
            v.wait_ge(p.sem("t"), p.tok("t"))
            v.tensor_tensor(gcols[:, 8:16], plog[:, 0:8], rbbc[:, :], OP.add)
            v.wait_ge(p.sem("v"), p.inc(v.memset(gcols[:, 0:1], 0.0), "v", 1))
            ins = v.reduce_max(gcols[:, 0:1], gcols[:, 8:16], axis=AX.X)
            v.wait_ge(p.sem("v"), p.inc(ins, "v", 1))
            v.tensor_scalar(gcols[:, 2:3], gcols[:, 0:1], -1.0, None, OP.mult)
            ins = v.tensor_scalar(midg[:, 0:8], gcols[:, 8:16], gcols[:, 0:1], None, OP.is_ge)
            v.wait_ge(p.sem("v"), p.inc(ins, "v", 1))
            ins = v.scalar_tensor_tensor(gcols[:, 8:16], midg[:, 0:8], NEG, gcols[:, 8:16], OP.mult, OP.add)
            v.wait_ge(p.sem("v"), p.inc(ins, "v", 1))
            ins = v.reduce_max(gcols[:, 1:2], gcols[:, 8:16], axis=AX.X)
            v.wait_ge(p.sem("v"), p.inc(ins, "v", 1))
            v.tensor_scalar(gcols[:, 3:4], gcols[:, 2:3], gcols[:, 1:2], None, OP.add)
            ins = v.tensor_copy(gcols[:, 5:6], gcols[:, 1:2])
            v.wait_ge(p.sem("v"), p.inc(ins, "v", 1))
            ins = v.tensor_scalar(gcols[:, 4:5], gcols[:, 3:4], -1.0, None, OP.mult)
            p.inc(ins, "v", 1)

        @block.scalar
        def _(sc, oc=oc):
            sc.wait_ge(p.sem("v"), p.tok("v"))
            sc.activation(gcols[:, 6:7], gcols[:, 4:5], AF.Sigmoid)
            p.inc(sc.activation(gcols[:, 7:8], gcols[:, 3:4], AF.Sigmoid), "s", 1)

        @block.vector
        def _(v, oc=oc):
            v.wait_ge(p.sem("s"), p.tok("s"))
            v.tensor_tensor(gcols[:, 8:16], plog[:, 0:8], rbbc[:, :], OP.add)
            v.tensor_scalar(gcols[:, 6:7], gcols[:, 6:7], 2.0, None, OP.mult)
            ins = v.tensor_scalar(gcols[:, 7:8], gcols[:, 7:8], 2.0, None, OP.mult)
            v.wait_ge(p.sem("v"), p.inc(ins, "v", 1))
            v.tensor_scalar(midg[:, 0:8], gcols[:, 8:16], gcols[:, 0:1], None, OP.is_ge)
            ins = v.tensor_scalar(midg[:, 8:16], gcols[:, 8:16], gcols[:, 1:2], None, OP.is_ge)
            v.wait_ge(p.sem("v"), p.inc(ins, "v", 1))
            v.tensor_tensor(midg[:, 8:16], midg[:, 8:16], midg[:, 0:8], OP.subtract)
            v.tensor_scalar(gcols[:, 8:16], midg[:, 0:8], gcols[:, 6:7], None, OP.mult)
            v.wait_ge(p.sem("v"), p.inc(v.memset(gcols[:, 2:3], 0.0), "v", 1))
            ins = v.scalar_tensor_tensor(gcols[:, 8:16], midg[:, 8:16], gcols[:, 7:8],
                                         gcols[:, 8:16], OP.mult, OP.add)
            v.wait_ge(p.sem("v"), p.inc(ins, "v", 1))
            for e in range(8):
                ins = v.tensor_scalar(midg[:, e * 16:(e + 1) * 16], pmid[:, e * 16:(e + 1) * 16],
                                      gcols[:, 8 + e:9 + e], None, OP.mult)
            tokp[("mid", oc)] = p.inc(ins, "v", 1)

        @block.tensor
        def _(t, oc=oc):
            t.wait_ge(p.sem("v"), p.tok("v"))
            ins = t.transpose(ptr4[:, :], midg[:, 0:128], identb[:, :])
            p.inc(ins, "t", 1)

        @block.vector
        def _(v, oc=oc):
            v.wait_ge(p.sem("t"), p.tok("t"))
            ins = v.tensor_copy(midT[:, :], ptr4[:, :])
            tokp[("midT", oc)] = p.inc(ins, "v", 1)

        for dto in range(16):
            @block.gpsimd
            def _(g, oc=oc, dto=dto):
                g.wait_ge(p.sem("t"), p.tok("t"))
                p.inc(g.dma_start(
                    out=wtile[dto % 2][:, 0:16, :],
                    in_=dr["base_wT"].rearrange("(dt pp) d -> pp dt d", pp=128)[:, :, dto * 128:(dto + 1) * 128]),
                    "dm_wo%d" % (dto % 2), 16)

            @block.tensor
            def _(t, oc=oc, dto=dto):
                t.wait_ge(p.sem("dm_wo%d" % (dto % 2)), p.tok("dm_wo%d" % (dto % 2)))
                t.wait_ge(p.sem("v"), tokp[("midT", oc)])
                if ("fps", dto % 2) in tokp:
                    t.wait_ge(p.sem("v"), tokp[("fps", dto % 2)])
                ps = (ps0, ps1)[dto % 2]
                for dt in range(16):
                    t.matmul(ps[:, 0:128], wtile[dto % 2][:, dt, :], h2[:, dt, oc * 128:(oc + 1) * 128],
                             start=(dt == 0), stop=False)
                ins = t.matmul(ps[:, 0:128], loraB[:, dto, :], midT[:, :], start=False, stop=True)
                p.inc(ins, "t", 1)

            @block.vector
            def _(v, oc=oc, dto=dto):
                v.wait_ge(p.sem("t"), p.tok("t"))
                ps = (ps0, ps1)[dto % 2]
                v.scalar_tensor_tensor(xT[:, dto, oc * 128:(oc + 1) * 128], ps[:, 0:128],
                                       modc[:, 0, dto, 5:6], xT[:, dto, oc * 128:(oc + 1) * 128],
                                       OP.mult, OP.add)
                ins = v.tensor_scalar(xT[:, dto, oc * 128:(oc + 1) * 128],
                                      xT[:, dto, oc * 128:(oc + 1) * 128],
                                      bb2[:, dto, 1:2], None, OP.add)
                tokp[("fps", dto % 2)] = p.inc(ins, "v", 1)


    pcst = p.sbuf("pcst", [128, 1024], F32)
    pcsb = p.sbuf("pcsb", [128, 1024], BF16)
    for i in range(128):
        if i < 96:
            pdt, pch = i // 6, i % 6
            srcv, dstv = dr["qkv1"], dr["qkv1b"]
        else:
            pdt, pch = (i - 96) // 2, (i - 96) % 2
            srcv, dstv = dr["oproj1"], dr["oproj1b"]

        @block.sync
        def _(s, i=i, pdt=pdt, pch=pch, srcv=srcv):
            if ("pc_cp",) in tokp:
                s.wait_ge(p.sem("v"), tokp[("pc_cp",)])
            ins = s.dma_start(out=pcst[:, :],
                              in_=srcv[pdt * 128:(pdt + 1) * 128, pch * 1024:(pch + 1) * 1024])
            p.inc(ins, "dm_pc", 16)
            tokp[("pc_ld",)] = p.tok("dm_pc")

        @block.vector
        def _(v, i=i):
            v.wait_ge(p.sem("dm_pc"), tokp[("pc_ld",)])
            if ("pc_st",) in tokp:
                v.wait_ge(p.sem("st_pc"), tokp[("pc_st",)])
            ins = v.tensor_copy(pcsb[:, :], pcst[:, :])
            tokp[("pc_cp",)] = p.inc(ins, "v", 1)

        @block.sync
        def _(s, i=i, pdt=pdt, pch=pch, dstv=dstv):
            s.wait_ge(p.sem("v"), tokp[("pc_cp",)])
            ins = s.dma_start(out=dstv[pdt * 128:(pdt + 1) * 128, pch * 1024:(pch + 1) * 1024],
                              in_=pcsb[:, :])
            tokp[("pc_st",)] = p.inc(ins, "st_pc", 16)


def ph5(p, block):
    dr = p.dram
    modc = p.sb["modc"]
    kcol = p.sb["kcol"]
    identb = p.sb["identb_s"]
    xT = p.sb["xTr"]
    tokq = {}
    h2 = p.sb["h2"]
    srow = p.sb["srow"]
    mubc = p.sb["mubc"]
    rsbc = p.sb["rsbc"]
    w_t = p.sb["wtile0"]
    act4 = p.sb["act4"]
    ps0, ps1, prr = p.ps["ps0"], p.ps["ps1"], p.ps["prr"]
    p.free("pcsb", "pcst")
    p.free("bb2", "midT", "midg", "gcols", "rbbc", "rwt", "loraB", "loraA")
    p.free("ptr4", "plog", "pmid")
    xn = p.sb["xn"]
    gbbc = p.sb["gbbc"]
    pav = p.psum("pav", [128, 1024], F32)
    ptrb = p.psum("ptrb", [128, 512], BF16)

    @block.gpsimd
    def _(g):
        g.wait_ge(p.sem("v"), p.tok("v"))
        for dt in range(16):
            p.inc(g.dma_start(out=gbbc[:, dt:dt + 1],
                              in_=dr["gamma1"].rearrange("o (dt pp) -> (o pp) dt", pp=128)[:, dt:dt + 1]),
                  "dm_gb1", 16)
            p.inc(g.dma_start(out=gbbc[:, 16 + dt:17 + dt],
                              in_=dr["beta1"].rearrange("o (dt pp) -> (o pp) dt", pp=128)[:, dt:dt + 1]),
                  "dm_gb1", 16)
    ln_stats_own(p, block, xT, TOK, 8, kcol[0:1, KCVALS[1e-5]:KCVALS[1e-5] + 1])
    ln_apply(p, block, xT, h2, TOK, 8, (1, 1, 0))
    ln_stats_own(p, block, h2, TOK, 10, kcol[0:1, KCVALS[1e-5]:KCVALS[1e-5] + 1], bf=True)

    @block.vector
    def _(v):
        v.wait_ge(p.sem("dm_gb1"), p.tok("dm_gb1"))
        p.inc(v.tensor_copy(gbbc[:, 0:1], gbbc[:, 0:1]), "v", 1)
    ln_apply(p, block, h2, xn, TOK, 10, None, extra_gb=(gbbc[:, 0:16], gbbc[:, 16:32]))
    p.free("h2")
    macc = p.sbuf("macc", [128, 16], F32)
    vstg = p.sbuf("vstg", [128, 512], BF16)
    qkstg = p.sbuf("qkstg", [128, 2, 512], BF16)
    amask = p.sbuf("amask", [128, 1024], F32)

    @block.gpsimd
    def _(g):
        g.wait_ge(p.sem("v"), p.tok("v"))
        g.wait_ge(p.sem("t"), p.tok("t"))
        g.wait_ge(p.sem("st_pc"), p.tok("st_pc"))
        p.inc(g.dma_start(out=amask[:, :], in_=dr["attn_mask"][:, :]), "dm_gb1", 16)

    for pc in range(2):
        for dto in range(16):
            @block.sync
            def _(g, pc=pc, dto=dto):
                g.wait_ge(p.sem("t"), p.tok("t"))
                if pc == 0 and dto == 0:
                    g.wait_ge(p.sem("st_pc"), p.tok("st_pc"))
                p.inc(g.dma_start(
                    out=w_t[:, 0:16, :],
                    in_=dr["qkv1b"].rearrange("(dt pp) d -> pp dt d", pp=128)
                    [:, :, pc * 2048 + dto * 128:pc * 2048 + (dto + 1) * 128]), "dm_wq5", 16)

            @block.tensor
            def _(t, pc=pc, dto=dto):
                t.wait_ge(p.sem("dm_wq5"), p.tok("dm_wq5"))
                t.wait_ge(p.sem("v"), p.tok("v"))
                if ("spill",) in tokq:
                    t.wait_ge(p.sem("st_kv"), tokq[("spill",)])
                for tg in range(2):
                    for dt in range(16):
                        ins = t.matmul((ps0, ps1)[tg][:, :], w_t[:, dt, :],
                                       xn[:, dt, tg * 512:(tg + 1) * 512],
                                       start=(dt == 0), stop=(dt == 15))
                p.inc(ins, "t", 1)

            @block.vector
            def _(v, pc=pc, dto=dto):
                v.wait_ge(p.sem("t"), p.tok("t"))
                if ("qkst",) in tokq:
                    v.wait_ge(p.sem("st_kv"), tokq[("qkst",)])
                sc_ = (2048.0 ** -0.5) if pc == 0 else 1.0
                for tg in range(2):
                    ins = v.tensor_scalar(qkstg[:, tg, :], (ps0, ps1)[tg][:, :],
                                          sc_, None, OP.mult)
                p.inc(ins, "v", 1)

            @block.sync
            def _(g, pc=pc, dto=dto):
                g.wait_ge(p.sem("v"), p.tok("v"))
                for oc in range(8):
                    tg, col = oc // 4, (oc % 4) * 128
                    if pc == 0:
                        ins = g.dma_start(out=dr["qTh"][dto * 128:(dto + 1) * 128,
                                                        oc * 128:(oc + 1) * 128],
                                          in_=qkstg[:, tg, col:col + 128])
                    else:
                        sg, half = oc // 2, oc % 2
                        ins = g.dma_start(out=dr[f"kti{sg}"][dto * 128:(dto + 1) * 128,
                                                             half * 128:(half + 1) * 128],
                                          in_=qkstg[:, tg, col:col + 128])
                    p.inc(ins, "st_kv", 16)
                tokq[("qkst",)] = p.tok("st_kv")
        tokq[("spill",)] = p.tok("st_kv")

    wv = p.sbuf("wv", [128, 16, 512], BF16)
    for dvg in range(4):
        @block.sync
        def _(g, dvg=dvg):
            g.wait_ge(p.sem("t"), p.tok("t"))
            for dt in range(16):
                p.inc(g.dma_start(out=wv[:, dt, :],
                                  in_=dr["qkv1b"][dt * 128:(dt + 1) * 128,
                                                  4096 + dvg * 512:4096 + dvg * 512 + 512]),
                      "dm_wv", 16)

        for oc in range(8):
            @block.tensor
            def _(t, dvg=dvg, oc=oc):
                t.wait_ge(p.sem("dm_wv"), p.tok("dm_wv"))
                if ("vps",) in tokq:
                    t.wait_ge(p.sem("st_kv"), tokq[("vps",)])
                for dt in range(16):
                    ins = t.matmul(ps0[:, :], xn[:, dt, oc * 128:(oc + 1) * 128], wv[:, dt, :],
                                   start=(dt == 0), stop=(dt == 15))
                p.inc(ins, "t", 1)

            @block.vector
            def _(v, dvg=dvg, oc=oc):
                v.wait_ge(p.sem("t"), p.tok("t"))
                p.inc(v.tensor_copy(vstg[:, :], ps0[:, :]), "v", 1)

            @block.sync
            def _(g, dvg=dvg, oc=oc):
                g.wait_ge(p.sem("v"), p.tok("v"))
                sg, half = oc // 2, oc % 2
                ins = g.dma_start(out=dr[f"vti{sg}"][half * 128:(half + 1) * 128,
                                                     dvg * 512:(dvg + 1) * 512],
                                  in_=vstg[:, :])
                tokq[("vps",)] = p.inc(ins, "st_kv", 16)

    @block.gpsimd
    def _(g):
        g.wait_ge(p.sem("st_kv"), p.tok("st_kv"))
        for sg in range(4):
            p.inc(g.collective_compute(
                "AllGather", OP.bypass, replica_groups=[list(range(NCORES))],
                ins=[dr[f"kti{sg}"][:, :]], outs=[dr[f"kta{sg}"][:, :, :]]), f"cc_kv{sg}", 1)
            p.inc(g.collective_compute(
                "AllGather", OP.bypass, replica_groups=[list(range(NCORES))],
                ins=[dr[f"vti{sg}"][:, :]], outs=[dr[f"vta{sg}"][:, :, :]]), f"cc_kv{sg}", 1)

    p.free("wv")
    accf = p.sbuf("accf", [128, 2048], F32)
    ptt = p.sbuf("ptt", [128, 512], BF16)
    scf = p.sbuf("scf", [128, 512], F32)
    qsl = p.sbuf("qsl", [128, 16, 128], BF16)
    ktile = p.sbuf("ktile", [128, 16, 512], BF16)
    vtile = p.sbuf("vtile", [128, 4, 2048], BF16)

    for s in range(8):
        @block.sync
        def _(g, s=s):
            if ("qfree",) in tokq:
                g.wait_ge(p.sem("t"), tokq[("qfree",)])
            for dt in range(16):
                p.inc(g.dma_start(out=qsl[:, dt, :],
                                  in_=dr["qTh"][dt * 128:(dt + 1) * 128, s * 128:(s + 1) * 128]),
                      "dm_q", 16)

        @block.vector
        def _(v, s=s):
            v.wait_ge(p.sem("v"), p.tok("v"))
            if ("accread",) in tokq:
                v.wait_ge(p.sem("v"), tokq[("accread",)])
            v.memset(accf[:, :], 0.0)
            v.memset(macc[:, 1:2], 0.0)
            ins = v.memset(macc[:, 0:1], -1e30)
            p.inc(ins, "v", 1)

        nkv = 8 * (s + 1)
        for kt0 in range(0, nkv, 4):
            last_sb = (kt0 + 4 == nkv)

            @block.sync
            def _(g, s=s, kt0=kt0):
                for j in range(4):
                    c = kt0 + j
                    g_, snd, hf = c // 16, c % 8, (c // 8) % 2
                    g.wait_ge(p.sem(f"cc_kv{g_}"), 2)
                    if ("kvfree",) in tokq:
                        g.wait_ge(p.sem("t"), tokq[("kvfree",)])
                    for dt in range(16):
                        p.inc(g.dma_start(out=ktile[:, dt, j * 128:(j + 1) * 128],
                                          in_=dr[f"kta{g_}"][snd, dt * 128:(dt + 1) * 128,
                                                             hf * 128:(hf + 1) * 128]), "dm_kv", 16)
                    p.inc(g.dma_start(out=vtile[:, j, :],
                                      in_=dr[f"vta{g_}"][snd, hf * 128:(hf + 1) * 128, :]), "dm_kv", 16)

            @block.tensor
            def _(t, s=s, kt0=kt0):
                t.wait_ge(p.sem("dm_q"), p.tok("dm_q"))
                t.wait_ge(p.sem("dm_kv"), p.tok("dm_kv"))
                t.wait_ge(p.sem("v"), p.tok("v"))
                for dt in range(16):
                    ins = t.matmul(ps0[:, :], qsl[:, dt, :], ktile[:, dt, :],
                                   start=(dt == 0), stop=(dt == 15))
                p.inc(ins, "t", 1)

            @block.vector
            def _(v, s=s, kt0=kt0, last_sb=last_sb):
                v.wait_ge(p.sem("t"), p.tok("t"))
                if last_sb:
                    off = (kt0 % 8) * 128
                    ins = v.tensor_tensor(scf[:, :], ps0[:, :], amask[:, off:off + 512], OP.add)
                else:
                    ins = v.tensor_copy(scf[:, :], ps0[:, :])
                v.wait_ge(p.sem("v"), p.inc(ins, "v", 1))
                ins = v.reduce_max(macc[:, 2:3], scf[:, :], axis=AX.X)
                v.wait_ge(p.sem("v"), p.inc(ins, "v", 1))
                v.tensor_scalar(macc[:, 3:4], macc[:, 2:3], macc[:, 0:1], None, OP.max)
                v.wait_ge(p.sem("v"), p.inc(v.tensor_copy(macc[:, 6:7], macc[:, 0:1]), "v", 1))
                v.tensor_scalar(macc[:, 4:5], macc[:, 3:4], -1.0, None, OP.mult)
                ins = v.tensor_scalar(macc[:, 5:6], macc[:, 6:7], macc[:, 3:4], None, OP.subtract)
                v.wait_ge(p.sem("v"), p.inc(ins, "v", 1))
                p.inc(v.tensor_copy(macc[:, 0:1], macc[:, 3:4]), "v", 1)

            @block.scalar
            def _(sc, s=s, kt0=kt0):
                sc.wait_ge(p.sem("v"), p.tok("v"))
                sc.memzero(macc[:, 7:8])
                sc.activation(ptile[:, :] if False else ptt[:, :], scf[:, :], AF.Exp,
                              bias=macc[:, 4:5], accum_out=macc[:, 7:8])
                p.inc(sc.activation(macc[:, 8:9], macc[:, 5:6], AF.Exp), "s", 1)

            @block.vector
            def _(v, s=s, kt0=kt0):
                v.wait_ge(p.sem("s"), p.tok("s"))
                ins = v.scalar_tensor_tensor(macc[:, 1:2], macc[:, 1:2], macc[:, 8:9],
                                             macc[:, 7:8], OP.mult, OP.add)
                p.inc(ins, "v", 1)

            @block.tensor
            def _(t, s=s, kt0=kt0):
                t.wait_ge(p.sem("s"), p.tok("s"))
                for j in range(4):
                    ins = t.transpose(ptrb[:, j * 128:(j + 1) * 128], ptt[:, j * 128:(j + 1) * 128],
                                      identb[:, :])
                p.inc(ins, "t", 1)

            @block.vector
            def _(v, s=s, kt0=kt0):
                v.wait_ge(p.sem("t"), p.tok("t"))
                ins = v.tensor_copy(ptt[:, :], ptrb[:, :])
                p.inc(ins, "v", 1)

            for dh in range(2):
                @block.tensor
                def _(t, s=s, kt0=kt0, dh=dh):
                    t.wait_ge(p.sem("v"), p.tok("v"))
                    for dvq in range(2):
                        for j in range(4):
                            ins = t.matmul(pav[:, dvq * 512:(dvq + 1) * 512], ptt[:, j * 128:(j + 1) * 128],
                                           vtile[:, j, dh * 1024 + dvq * 512:dh * 1024 + (dvq + 1) * 512],
                                           start=(j == 0), stop=(j == 3))
                    tokq[("kvfree",)] = p.inc(ins, "t", 1)

                @block.vector
                def _(v, s=s, kt0=kt0, dh=dh):
                    v.wait_ge(p.sem("t"), p.tok("t"))
                    for dvq in range(2):
                        ins = v.scalar_tensor_tensor(accf[:, dh * 1024 + dvq * 512:dh * 1024 + (dvq + 1) * 512],
                                                     accf[:, dh * 1024 + dvq * 512:dh * 1024 + (dvq + 1) * 512],
                                                     macc[:, 8:9], pav[:, dvq * 512:(dvq + 1) * 512],
                                                     OP.mult, OP.add)
                    p.inc(ins, "v", 1)

        @block.vector
        def _(v, s=s):
            v.wait_ge(p.sem("v"), p.tok("v"))
            ins = v.reciprocal(macc[:, 9:10], macc[:, 1:2])
            v.wait_ge(p.sem("v"), p.inc(ins, "v", 1))

        for dvq in range(4):
            @block.vector
            def _(v, s=s, dvq=dvq):
                if ("epist",) in tokq:
                    v.wait_ge(p.sem("st_kv"), tokq[("epist",)])
                ins = v.tensor_scalar(ptt[:, :], accf[:, dvq * 512:(dvq + 1) * 512],
                                      macc[:, 9:10], None, OP.mult)
                p.inc(ins, "v", 1)
                if dvq == 3:
                    tokq[("accread",)] = p.tok("v")

            @block.tensor
            def _(t, s=s, dvq=dvq):
                t.wait_ge(p.sem("v"), p.tok("v"))
                for j in range(4):
                    ins = t.transpose(ptrb[:, j * 128:(j + 1) * 128], ptt[:, j * 128:(j + 1) * 128],
                                      identb[:, :])
                p.inc(ins, "t", 1)

            @block.vector
            def _(v, s=s, dvq=dvq):
                v.wait_ge(p.sem("t"), p.tok("t"))
                ins = v.tensor_copy(qkstg[:, 0, :], ptrb[:, :])
                p.inc(ins, "v", 1)

            @block.sync
            def _(g, s=s, dvq=dvq):
                g.wait_ge(p.sem("v"), p.tok("v"))
                for j in range(4):
                    r0 = (dvq * 4 + j) * 128
                    ins = g.dma_start(out=dr["aoTh"][r0:r0 + 128, s * 128:(s + 1) * 128],
                                      in_=qkstg[:, 0, j * 128:(j + 1) * 128])
                    p.inc(ins, "st_kv", 16)
                tokq[("epist",)] = p.tok("st_kv")
        tokq[("qfree",)] = p.tok("t")

    @block.sync
    def _(s):
        s.wait_ge(p.sem("v"), p.tok("v"))
        p.inc(s.dma_start(out=mubc[:, :], in_=dr["rows0"][8:9, 0:TOK]
                          .broadcast_to([128, TOK])), "dm_lnbc", 16)
        p.inc(s.dma_start(out=rsbc[:, :], in_=dr["rows0"][9:10, 0:TOK]
                          .broadcast_to([128, TOK])), "dm_lnbc", 16)

    for tg in range(2):
        @block.sync
        def _(g, tg=tg):
            g.wait_ge(p.sem("st_kv"), p.tok("st_kv"))
            g.wait_ge(p.sem("t"), p.tok("t"))
            for dt in range(16):
                p.inc(g.dma_start(out=ktile[:, dt, :],
                                  in_=dr["aoTh"][dt * 128:(dt + 1) * 128, tg * 512:(tg + 1) * 512]),
                      "dm_ao%d" % tg, 16)

        for dto in range(16):
            @block.sync
            def _(g, tg=tg, dto=dto):
                g.wait_ge(p.sem("t"), p.tok("t"))
                p.inc(g.dma_start(
                    out=w_t[:, 0:16, :],
                    in_=dr["oproj1b"].rearrange("(dt pp) d -> pp dt d", pp=128)
                    [:, :, dto * 128:(dto + 1) * 128]), "dm_wq5", 16)

            @block.tensor
            def _(t, tg=tg, dto=dto):
                t.wait_ge(p.sem("dm_wq5"), p.tok("dm_wq5"))
                t.wait_ge(p.sem("dm_ao%d" % tg), p.tok("dm_ao%d" % tg))
                if ("ops",) in tokq:
                    t.wait_ge(p.sem("v"), tokq[("ops",)])
                for dt in range(16):
                    ins = t.matmul(ps0[:, :], w_t[:, dt, :], ktile[:, dt, :],
                                   start=(dt == 0), stop=(dt == 15))
                p.inc(ins, "t", 1)

            @block.vector
            def _(v, tg=tg, dto=dto):
                v.wait_ge(p.sem("t"), p.tok("t"))
                v.wait_ge(p.sem("dm_lnbc"), p.tok("dm_lnbc"))
                sl = slice(tg * 512, (tg + 1) * 512)
                v.tensor_tensor(scf[:, :], xT[:, dto, sl], rsbc[:, sl], OP.mult)
                v.wait_ge(p.sem("v"), p.inc(v.memset(macc[:, 10:11], 0.0), "v", 1))
                ins = v.tensor_tensor(scf[:, :], scf[:, :], mubc[:, sl], OP.add)
                v.wait_ge(p.sem("v"), p.inc(ins, "v", 1))
                ins = v.tensor_scalar(scf[:, :], scf[:, :], modc[:, 1, dto, 1:2],
                                      modc[:, 1, dto, 0:1], OP.mult, OP.add)
                v.wait_ge(p.sem("v"), p.inc(ins, "v", 1))
                ins = v.tensor_tensor(scf[:, :], ps0[:, :], scf[:, :], OP.add)
                v.wait_ge(p.sem("v"), p.inc(ins, "v", 1))
                ins = v.scalar_tensor_tensor(xT[:, dto, sl], scf[:, :],
                                             modc[:, 1, dto, 2:3], xT[:, dto, sl],
                                             OP.mult, OP.add)
                tokq[("ops",)] = p.inc(ins, "v", 1)

    ln_stats_own(p, block, xT, TOK, 12, kcol[0:1, KCVALS[1e-5]:KCVALS[1e-5] + 1])
    ln_apply(p, block, xT, xn, TOK, 12, (1, 4, 3))



def ph6(p, block):
    dr = p.dram
    modc = p.sb["modc"]
    identb = p.sb["identb_s"]
    identf = p.sb["identf_s"]
    xT = p.sb["xTr"]
    h4 = p.sb["xn"]
    ps0, ps1, prr = p.ps["ps0"], p.ps["ps1"], p.ps["prr"]
    p.free("vtile", "ktile", "qsl", "scf", "ptt", "accf",
           "amask", "qkstg", "vstg", "macc")
    p.free("pav", "ptrb")

    scw = p.sbuf("scw", [128, 16, 44], F32)
    rcol = p.sbuf("rcol", [128, 10], F32)
    rrow = p.sbuf("rrow", [1, 512], F32)
    bct = p.sbuf("bct", [128, 44], F32)
    cen = p.sbuf("cen", [128, 44], BF16)
    qsc = p.sbuf("qsc", [128, 44], F32)
    deqc = p.sbuf("deqc", [128, 44], BF16)
    deqf = p.sbuf("deqf", [128, 44], F32)
    wstg = p.sbuf("wstg", [128, 128], BF16)
    wtl = p.sbuf("wtl", [128, 2816], BF16)
    ttl = p.sbuf("ttl", [128, 2816], BF16)
    Nacc = p.sbuf("Nacc", [128, 2816], BF16)
    Pacc = p.sbuf("Pacc", [128, 2816], BF16)
    pout = p.psum("pout", [128, 256], F32)
    ptr6 = p.psum("ptr6", [128, 128], F32)
    ptb6 = p.psum("ptb6", [128, 128], BF16)
    tok6 = {}
    BIG = 2.0 ** 100
    W1TSH = 2048 * 768
    W3OFF = 2 * W1TSH
    items = [(0, ti) for ti in range(6)] + [(1, ti) for ti in range(6)] + \
            [(2, ti) for ti in range(4)]

    def load_tile(s, wi, ti, width):
        if wi < 2:
            nm = ("w1_my", "w2_my")[wi]
            return s.dma_start(out=wtl[:, 0:width], in_=dr[nm][ti * 128:(ti + 1) * 128, :])
        flat = dr["w3_my"].rearrange("a b -> (a b)").rearrange(
            "(g q pp e) -> g q pp e", q=2, pp=128, e=64)
        return s.dma_start(out=wtl[:, 0:width].rearrange("pp (gq e) -> pp gq e", e=64),
                           in_=flat[ti * 22:(ti + 1) * 22].rearrange("g q pp e -> pp (g q) e"))

    for wi, ti in items:
        nseg = 32 if wi < 2 else 44
        width = nseg * 64
        slot = wi * 6 + ti

        @block.sync
        def _(s, wi=wi, ti=ti, width=width):
            s.wait_ge(p.sem("v"), p.tok("v"))
            p.inc(load_tile(s, wi, ti, width), "dm_nf", 16)

        @block.vector
        def _(v, wi=wi, ti=ti, slot=slot, nseg=nseg, width=width):
            v.wait_ge(p.sem("dm_nf"), p.tok("dm_nf"))
            v.tensor_scalar(ttl[:, 0:width], wtl[:, 0:width], -1.0, None, OP.mult)
            ins = v.tensor_tensor(ttl[:, 0:width], ttl[:, 0:width], wtl[:, 0:width], OP.max)
            v.wait_ge(p.sem("v"), p.inc(ins, "v", 1))
            for j in range(nseg):
                ins = v.reduce_max(scw[:, slot, j:j + 1], ttl[:, j * 64:(j + 1) * 64], axis=AX.X)
            v.wait_ge(p.sem("v"), p.inc(ins, "v", 1))
            ins = v.reduce_sum(rcol[:, wi:wi + 1], scw[:, slot, 0:nseg], axis=AX.X)
            p.inc(ins, "v", 1)

        @block.tensor
        def _(t, wi=wi, ti=ti):
            t.wait_ge(p.sem("v"), p.tok("v"))
            ins = t.matmul(prr[0:1, 8 + wi:9 + wi], p.sb["ones2"][:, 0:1], rcol[:, wi:wi + 1],
                           start=(ti == 0), stop=(ti == (5 if wi < 2 else 3)))
            p.inc(ins, "t", 1)

    @block.vector
    def _(v):
        v.wait_ge(p.sem("t"), p.tok("t"))
        ins = v.memset(rrow[0:1, 0:4], 0.0)
        v.wait_ge(p.sem("v"), p.inc(ins, "v", 1))
        p.inc(v.tensor_copy(rrow[0:1, 0:3], prr[0:1, 8:11]), "v", 1)

    @block.gpsimd
    def _(g):
        g.wait_ge(p.sem("v"), p.tok("v"))
        p.inc(g.dma_start(out=dr["nfsum_in"][:, :], in_=rrow[0:1, 0:4]), "st_nf3", 16)
        g.wait_ge(p.sem("st_nf3"), p.tok("st_nf3"))
        p.inc(g.collective_compute(
            "AllReduce", OP.add, replica_groups=[list(range(NCORES))],
            ins=[dr["nfsum_in"][:, :]], outs=[dr["nfsum_all"][:, :]]), "cc_nf", 1)
        g.wait_ge(p.sem("cc_nf"), p.tok("cc_nf"))
        p.inc(g.dma_start(out=rcol[:, 4:7],
                          in_=dr["nfsum_all"][0:1, 0:3].broadcast_to([128, 3])), "dm_nf3", 16)

    @block.vector
    def _(v):
        v.wait_ge(p.sem("dm_nf3"), p.tok("dm_nf3"))
        v.tensor_scalar(rcol[:, 4:5], rcol[:, 4:5], 1.0 / (NCORES * 704 * 32), None, OP.mult)
        v.tensor_scalar(rcol[:, 5:6], rcol[:, 5:6], 1.0 / (NCORES * 704 * 32), None, OP.mult)
        ins = v.tensor_scalar(rcol[:, 6:7], rcol[:, 6:7], 1.0 / (NCORES * 22528), None, OP.mult)
        v.wait_ge(p.sem("v"), p.inc(ins, "v", 1))
        ins = v.tensor_copy(cen[:, 0:3], rcol[:, 4:7])
        v.wait_ge(p.sem("v"), p.inc(ins, "v", 1))
        p.inc(v.tensor_copy(rcol[:, 4:7], cen[:, 0:3]), "v", 1)

    def ladder(v, width):
        v.tensor_scalar(Nacc[:, 0:width], wtl[:, 0:width], float(NF4_MID[0]), BIG,
                        OP.subtract, OP.mult)
        v.tensor_scalar(Nacc[:, 0:width], Nacc[:, 0:width], float(NF4[0]), None, OP.max)
        for j in range(2, 9):
            v.tensor_scalar(ttl[:, 0:width], wtl[:, 0:width], float(NF4_MID[j - 1]), BIG,
                            OP.subtract, OP.mult)
            v.scalar_tensor_tensor(Nacc[:, 0:width], ttl[:, 0:width], float(NF4[j - 1]),
                                   Nacc[:, 0:width], OP.max, OP.min)
        v.tensor_scalar(ttl[:, 0:width], wtl[:, 0:width], float(NF4_MID[8]), BIG,
                        OP.subtract, OP.mult)
        v.tensor_scalar(Pacc[:, 0:width], ttl[:, 0:width], float(NF4[9]), float(NF4[8]),
                        OP.min, OP.max)
        for j in range(10, 16):
            v.tensor_scalar(ttl[:, 0:width], wtl[:, 0:width], float(NF4_MID[j - 1]), BIG,
                            OP.subtract, OP.mult)
            v.scalar_tensor_tensor(Pacc[:, 0:width], ttl[:, 0:width], float(NF4[j]),
                                   Pacc[:, 0:width], OP.min, OP.max)
        ins = v.tensor_tensor(Nacc[:, 0:width], Nacc[:, 0:width], Pacc[:, 0:width], OP.min)
        v.wait_ge(p.sem("v"), p.inc(ins, "v", 1))

    for wi, ti in items:
        nseg = 32 if wi < 2 else 44
        width = nseg * 64
        slot = wi * 6 + ti

        @block.sync
        def _(s, wi=wi, ti=ti, width=width):
            s.wait_ge(p.sem("v"), p.tok("v"))
            s.wait_ge(p.sem("st_wq"), p.tok("st_wq"))
            p.inc(load_tile(s, wi, ti, width), "dm_nf", 16)

        @block.vector
        def _(v, wi=wi, ti=ti, slot=slot, nseg=nseg):
            def rt(ins):
                v.wait_ge(p.sem("v"), p.inc(ins, "v", 1))
            v.wait_ge(p.sem("dm_nf"), p.tok("dm_nf"))
            rt(v.tensor_scalar(bct[:, 0:nseg], scw[:, slot, 0:nseg], 1e-30, None, OP.max))
            rt(v.reciprocal(bct[:, 0:nseg], bct[:, 0:nseg]))
            rt(v.tensor_scalar(cen[:, 0:nseg], scw[:, slot, 0:nseg], rcol[:, 4 + wi:5 + wi],
                               None, OP.subtract))
            rt(v.tensor_copy(qsc[:, 0:nseg], cen[:, 0:nseg]))
            rt(v.tensor_scalar(deqf[:, 0:nseg], qsc[:, 0:nseg], -1.0, None, OP.mult))
            rt(v.tensor_tensor(qsc[:, 0:nseg], qsc[:, 0:nseg], deqf[:, 0:nseg], OP.max))
            if wi < 2:
                rt(v.reduce_max(rcol[:, 3:4], qsc[:, 0:nseg], axis=AX.X))
            else:
                a = qsc[:, 0:44].rearrange("pp (g u) -> pp g u", u=2)
                rt(v.tensor_tensor(a[:, :, 0], a[:, :, 0], a[:, :, 1], OP.max))

        @block.tensor
        def _(t, wi=wi, ti=ti):
            t.wait_ge(p.sem("v"), p.tok("v"))
            if wi < 2:
                ins = t.transpose(ptr6[0:1, 0:128], rcol[:, 3:4], identf[:, :])
            else:
                ins = t.transpose(ptr6[0:44, 0:128], qsc[:, 0:44], identf[:, :])
            p.inc(ins, "t", 1)

        @block.vector
        def _(v, wi=wi, ti=ti):
            def rt(ins):
                v.wait_ge(p.sem("v"), p.inc(ins, "v", 1))
            v.wait_ge(p.sem("t"), p.tok("t"))
            if wi < 2:
                rt(v.tensor_copy(rrow[0:1, 0:128], ptr6[0:1, 0:128]))
                a = rrow[0:1, 0:128].rearrange("o (g k) -> o g k", k=8)
                rt(v.tensor_tensor(a[:, :, 0:4], a[:, :, 0:4], a[:, :, 4:8], OP.max))
                rt(v.tensor_tensor(a[:, :, 0:2], a[:, :, 0:2], a[:, :, 2:4], OP.max))
                rt(v.tensor_tensor(a[:, :, 0:1], a[:, :, 0:1], a[:, :, 1:2], OP.max))
                for k in range(1, 8):
                    ins = v.tensor_copy(a[:, :, k:k + 1], a[:, :, 0:1])
                rt(ins)
            else:
                rt(v.reduce_max(rcol[0:44, 3:4], ptr6[0:44, 0:128], axis=AX.X))

        if wi == 2:
            @block.tensor
            def _(t, wi=wi, ti=ti):
                t.wait_ge(p.sem("v"), p.tok("v"))
                ins = t.transpose(ptr6[0:1, 0:44], rcol[0:44, 3:4], identf[0:44, 0:44])
                p.inc(ins, "t", 1)

            @block.vector
            def _(v, wi=wi, ti=ti):
                def rt(ins):
                    v.wait_ge(p.sem("v"), p.inc(ins, "v", 1))
                v.wait_ge(p.sem("t"), p.tok("t"))
                rt(v.tensor_copy(rrow[0:1, 0:44], ptr6[0:1, 0:44]))
                b = rrow[0:1, 0:44].rearrange("o (g u) -> o g u", u=2)
                rt(v.tensor_copy(b[:, :, 1:2], b[:, :, 0:1]))

        nw = 128 if wi < 2 else 44

        @block.vector
        def _(v, wi=wi, ti=ti, nw=nw):
            def rt(ins):
                v.wait_ge(p.sem("v"), p.inc(ins, "v", 1))
            rt(v.tensor_scalar(rrow[0:1, 128:128 + nw], rrow[0:1, 0:nw], 2.0, None, OP.mult))
            rt(v.reciprocal(rrow[0:1, 128:128 + nw], rrow[0:1, 128:128 + nw]))
            rt(v.tensor_scalar(rrow[0:1, 128:128 + nw], rrow[0:1, 128:128 + nw], 256.0,
                               None, OP.mult))
            rt(v.tensor_copy(wstg[0:1, 0:nw], rrow[0:1, 128:128 + nw]))
            rt(v.tensor_copy(rrow[0:1, 128:128 + nw], wstg[0:1, 0:nw]))
            rt(v.reciprocal(rrow[0:1, 256:256 + nw], rrow[0:1, 128:128 + nw]))

        if wi < 2:
            @block.tensor
            def _(t, wi=wi, ti=ti):
                t.wait_ge(p.sem("v"), p.tok("v"))
                t.transpose(ptr6[:, 0:1], rrow[0:1, 128:256], identf[0:1, 0:1])
                ins = t.transpose(ptr6[:, 1:2], rrow[0:1, 256:384], identf[0:1, 0:1])
                p.inc(ins, "t", 1)

            @block.vector
            def _(v, wi=wi, ti=ti):
                v.wait_ge(p.sem("t"), p.tok("t"))
                ins = v.tensor_copy(rcol[:, 8:10], ptr6[:, 0:2])
                v.wait_ge(p.sem("v"), p.inc(ins, "v", 1))

            @block.vector
            def _(v, wi=wi, ti=ti, nseg=nseg):
                def rt(ins):
                    v.wait_ge(p.sem("v"), p.inc(ins, "v", 1))
                v.wait_ge(p.sem("dm_nf"), p.tok("dm_nf"))
                rt(v.tensor_scalar(cen[:, 0:nseg], cen[:, 0:nseg], rcol[:, 8:9], None, OP.mult))
                rt(v.tensor_copy(qsc[:, 0:nseg], cen[:, 0:nseg]))
                rt(v.tensor_scalar(qsc[:, 0:nseg], qsc[:, 0:nseg], 8388608.0, 8388608.0,
                                   OP.add, OP.subtract))
                rt(v.tensor_scalar(qsc[:, 0:nseg], qsc[:, 0:nseg], 127.0, -128.0,
                                   OP.min, OP.max))
                rt(v.tensor_scalar(deqc[:, 0:nseg], qsc[:, 0:nseg], rcol[:, 9:10], None, OP.mult))
                rt(v.tensor_scalar(deqc[:, 0:nseg], deqc[:, 0:nseg], rcol[:, 4 + wi:5 + wi],
                                   None, OP.add))
                rt(v.tensor_copy(qsc[:, 0:nseg], deqc[:, 0:nseg]))
        else:
            @block.sync
            def _(s, wi=wi, ti=ti):
                s.wait_ge(p.sem("v"), p.tok("v"))
                p.inc(s.dma_start(out=dr["rows0"][14:15, 0:44], in_=rrow[0:1, 128:172]), "st_nf", 16)
                p.inc(s.dma_start(out=dr["rows0"][15:16, 0:44], in_=rrow[0:1, 256:300]), "st_nf", 16)
                s.wait_ge(p.sem("st_nf"), p.tok("st_nf"))
                p.inc(s.dma_start(out=deqf[:, 0:44],
                                  in_=dr["rows0"][15:16, 0:44].broadcast_to([128, 44])),
                      "dm_nf", 16)

            @block.gpsimd
            def _(g, wi=wi, ti=ti):
                g.wait_ge(p.sem("st_nf"), p.tok("st_nf"))
                p.inc(g.dma_start(out=deqc[:, 0:44],
                                  in_=dr["rows0"][14:15, 0:44].broadcast_to([128, 44])),
                      "dm_nf3", 16)

            @block.vector
            def _(v, wi=wi, ti=ti):
                def rt(ins):
                    v.wait_ge(p.sem("v"), p.inc(ins, "v", 1))
                v.wait_ge(p.sem("dm_nf"), p.tok("dm_nf"))
                v.wait_ge(p.sem("dm_nf3"), p.tok("dm_nf3"))
                rt(v.tensor_tensor(cen[:, 0:44], cen[:, 0:44], deqc[:, 0:44], OP.mult))
                rt(v.tensor_copy(qsc[:, 0:44], cen[:, 0:44]))
                rt(v.tensor_scalar(qsc[:, 0:44], qsc[:, 0:44], 8388608.0, 8388608.0,
                                   OP.add, OP.subtract))
                rt(v.tensor_scalar(qsc[:, 0:44], qsc[:, 0:44], 127.0, -128.0, OP.min, OP.max))
                rt(v.tensor_tensor(qsc[:, 0:44], qsc[:, 0:44], deqf[:, 0:44], OP.mult))
                rt(v.tensor_copy(deqc[:, 0:44], qsc[:, 0:44]))
                rt(v.tensor_scalar(deqc[:, 0:44], deqc[:, 0:44], rcol[:, 6:7], None, OP.add))
                rt(v.tensor_copy(qsc[:, 0:44], deqc[:, 0:44]))

        @block.vector
        def _(v, wi=wi, ti=ti, nseg=nseg, width=width):
            v.wait_ge(p.sem("v"), p.tok("v"))
            for j in range(nseg):
                ins = v.tensor_scalar(wtl[:, j * 64:(j + 1) * 64], wtl[:, j * 64:(j + 1) * 64],
                                      bct[:, j:j + 1], None, OP.mult)
            v.wait_ge(p.sem("v"), p.inc(ins, "v", 1))
            ladder(v, width)
            for j in range(nseg):
                ins = v.tensor_scalar(wtl[:, j * 64:(j + 1) * 64], Nacc[:, j * 64:(j + 1) * 64],
                                      qsc[:, j:j + 1], None, OP.mult)
            p.inc(ins, "v", 1)

        if wi < 2:
            for dt in range(16):
                @block.tensor
                def _(t, wi=wi, ti=ti, dt=dt):
                    t.wait_ge(p.sem("v"), p.tok("v"))
                    ins = t.transpose(ptb6[:, 0:128], wtl[:, dt * 128:(dt + 1) * 128], identb[:, :])
                    p.inc(ins, "t", 1)

                @block.vector
                def _(v, wi=wi, ti=ti, dt=dt):
                    v.wait_ge(p.sem("t"), p.tok("t"))
                    if ("wst",) in tok6:
                        v.wait_ge(p.sem("st_wq"), tok6[("wst",)])
                    ins = v.tensor_copy(wstg[:, :], ptb6[:, 0:128])
                    p.inc(ins, "v", 1)

                @block.sync
                def _(s, wi=wi, ti=ti, dt=dt):
                    s.wait_ge(p.sem("v"), p.tok("v"))
                    vw = dr["wq_in"][0, wi * W1TSH:(wi + 1) * W1TSH] \
                        .rearrange("(d f) -> d f", f=768)
                    ins = s.dma_start(out=vw[dt * 128:(dt + 1) * 128, ti * 128:(ti + 1) * 128],
                                      in_=wstg[:, :])
                    tok6[("wst",)] = p.inc(ins, "st_wq", 16)
        else:
            @block.sync
            def _(s, wi=wi, ti=ti):
                s.wait_ge(p.sem("v"), p.tok("v"))
                base = W3OFF + ti * 2816 * 128
                ins = s.dma_start(out=dr["wq_in"][0, base:base + 2816 * 128]
                                  .rearrange("(gq pp e) -> gq pp e", pp=128, e=64)
                                  .rearrange("gq pp e -> pp gq e"),
                                  in_=wtl[:, 0:2816].rearrange("pp (gq e) -> pp gq e", e=64))
                tok6[("wst",)] = p.inc(ins, "st_wq", 16)

    @block.gpsimd
    def _(g):
        g.wait_ge(p.sem("st_wq"), p.tok("st_wq"))
        p.inc(g.collective_compute(
            "AllGather", OP.bypass, replica_groups=[list(range(NCORES))],
            ins=[dr["wq_in"][:, :]], outs=[dr["wq_all"][:, :]]), "cc_wq", 1)

    p.free("Pacc", "Nacc", "ttl", "wtl")
    p.free("ptb6", "ptr6", "pout", "prr")
    hh = p.sbuf("hh", [128, 48, 256], BF16)
    wf = p.sbuf("wf", [128, 2, 16, 128], BF16)
    wf2 = p.sbuf("wf2", [128, 2, 16, 128], BF16)
    wt3 = p.sbuf("wt3", [128, 2, 128], BF16)
    tl3 = p.sbuf("tl3", [128, 2, 128], BF16)
    sstg = p.sbuf("sstg", [128, 256], F32)
    ystg = p.sbuf("ystg", [128, 2, 128], F32)
    poutA = p.psum("poutA", [128, 512], F32)
    poutB = p.psum("poutB", [128, 512], F32)
    ptrY0 = p.psum("ptrY0", [128, 128], F32)
    ptrY1 = p.psum("ptrY1", [128, 128], F32)
    ptbA = p.psum("ptbA", [128, 128], BF16)
    ptbB = p.psum("ptbB", [128, 128], BF16)

    for tg in range(4):
        for fb in range(48):
            par = fb % 2
            sh, b = fb // 6, fb % 6

            @block.sync
            def _(s, tg=tg, fb=fb, sh=sh, b=b, par=par):
                s.wait_ge(p.sem("cc_wq"), p.tok("cc_wq"))
                if ("ffw", par) in tok6:
                    s.wait_ge(p.sem("t"), tok6[("ffw", par)])
                v1 = dr["wq_all"][sh, 0:W1TSH].rearrange("(dt pp f) -> pp dt f", pp=128, f=768)
                v2 = dr["wq_all"][sh, W1TSH:2 * W1TSH].rearrange("(dt pp f) -> pp dt f",
                                                                 pp=128, f=768)
                p.inc(s.dma_start(out=wf[:, par, :, :], in_=v1[:, :, b * 128:(b + 1) * 128]),
                      "dm_ff%d" % par, 16)
                p.inc(s.dma_start(out=wf2[:, par, :, :], in_=v2[:, :, b * 128:(b + 1) * 128]),
                      "dm_ff%d" % par, 16)
                tok6[("ffld", par)] = p.tok("dm_ff%d" % par)

            @block.tensor
            def _(t, tg=tg, fb=fb, par=par):
                t.wait_ge(p.sem("dm_ff%d" % par), tok6[("ffld", par)])
                if ("ffps",) in tok6:
                    t.wait_ge(p.sem("v"), tok6[("ffps",)])
                for dt in range(16):
                    t.matmul(ps0[:, 0:256], wf[:, par, dt, :], h4[:, dt, tg * 256:(tg + 1) * 256],
                             start=(dt == 0), stop=(dt == 15))
                for dt in range(16):
                    ins = t.matmul(ps1[:, 0:256], wf2[:, par, dt, :],
                                   h4[:, dt, tg * 256:(tg + 1) * 256],
                                   start=(dt == 0), stop=(dt == 15))
                tok6[("ffw", par)] = p.inc(ins, "t", 1)

            @block.scalar
            def _(sc, tg=tg, fb=fb):
                sc.wait_ge(p.sem("t"), p.tok("t"))
                p.inc(sc.activation(sstg[:, 0:256], ps0[:, 0:256], AF.Silu), "s", 1)

            @block.vector
            def _(v, tg=tg, fb=fb):
                v.wait_ge(p.sem("s"), p.tok("s"))
                v.wait_ge(p.sem("t"), p.tok("t"))
                ins = v.tensor_tensor(hh[:, fb, :], sstg[:, 0:256], ps1[:, 0:256], OP.mult)
                tok6[("ffps",)] = p.inc(ins, "v", 1)

        for dto in range(16):
            dpar = dto % 2
            pout = (poutA, poutB)[dpar]
            for fb in range(48):
                wpar = fb % 2
                ptb = (ptbA, ptbB)[wpar]
                sh, b = fb // 6, fb % 6
                kreal = 64 if b == 5 else 128
                fcol = 704 * sh + 128 * b

                @block.sync
                def _(s, tg=tg, dto=dto, fb=fb, wpar=wpar, kreal=kreal, fcol=fcol):
                    if ("w3tr", wpar) in tok6:
                        s.wait_ge(p.sem("t"), tok6[("w3tr", wpar)])
                    w3v = dr["wq_all"][dto // 2, W3OFF:W3OFF + 256 * 5632] \
                        .rearrange("(d f) -> d f", f=5632)
                    ins = s.dma_start(out=tl3[:, wpar, 0:kreal],
                                      in_=w3v[(dto % 2) * 128:(dto % 2) * 128 + 128,
                                              fcol:fcol + kreal])
                    p.inc(ins, "dm_f3%d" % wpar, 16)
                    tok6[("f3ld", wpar)] = p.tok("dm_f3%d" % wpar)

                @block.tensor
                def _(t, tg=tg, dto=dto, fb=fb, wpar=wpar, kreal=kreal, ptb=ptb):
                    t.wait_ge(p.sem("dm_f3%d" % wpar), tok6[("f3ld", wpar)])
                    if ("w3ev", wpar) in tok6:
                        t.wait_ge(p.sem("v"), tok6[("w3ev", wpar)])
                    ins = t.transpose(ptb[0:kreal, 0:128], tl3[:, wpar, 0:kreal], identb[:, :])
                    tok6[("w3tr", wpar)] = p.inc(ins, "t", 1)

                @block.vector
                def _(v, tg=tg, dto=dto, fb=fb, wpar=wpar, kreal=kreal, ptb=ptb):
                    v.wait_ge(p.sem("t"), tok6[("w3tr", wpar)])
                    ins = v.tensor_copy(wt3[0:kreal, wpar, :], ptb[0:kreal, 0:128])
                    tok6[("w3ev", wpar)] = p.inc(ins, "v", 1)

                @block.tensor
                def _(t, tg=tg, dto=dto, fb=fb, wpar=wpar, kreal=kreal, dpar=dpar, pout=pout):
                    t.wait_ge(p.sem("v"), tok6[("w3ev", wpar)])
                    if fb == 0 and ("pev", dpar) in tok6:
                        t.wait_ge(p.sem("v"), tok6[("pev", dpar)])
                    ins = t.matmul(pout[:, 0:256], wt3[0:kreal, wpar, :], hh[0:kreal, fb, :],
                                   start=(fb == 0), stop=(fb == 47))
                    p.inc(ins, "t", 1)

            @block.vector
            def _(v, tg=tg, dto=dto, dpar=dpar, pout=pout):
                v.wait_ge(p.sem("t"), p.tok("t"))
                sl = slice(tg * 256, (tg + 1) * 256)
                ins = v.scalar_tensor_tensor(xT[:, dto, sl], pout[:, 0:256], modc[:, 1, dto, 5:6],
                                             xT[:, dto, sl], OP.mult, OP.add)
                tok6[("pev", dpar)] = p.inc(ins, "v", 1)

    for i, (oc, dt) in enumerate([(oc, dt) for oc in range(8) for dt in range(16)]):
        ypar = i % 2
        ptrY = (ptrY0, ptrY1)[ypar]

        @block.tensor
        def _(t, oc=oc, dt=dt, ypar=ypar, ptrY=ptrY):
            t.wait_ge(p.sem("v"), p.tok("v"))
            if ("ycp", ypar) in tok6:
                t.wait_ge(p.sem("v"), tok6[("ycp", ypar)])
            ins = t.transpose(ptrY[:, :], xT[:, dt, oc * 128:(oc + 1) * 128], identf[:, :])
            tok6[("ytr", ypar)] = p.inc(ins, "t", 1)

        @block.vector
        def _(v, oc=oc, dt=dt, ypar=ypar, ptrY=ptrY):
            v.wait_ge(p.sem("t"), tok6[("ytr", ypar)])
            if ("yst", ypar) in tok6:
                v.wait_ge(p.sem("st_y%d" % ypar), tok6[("yst", ypar)])
            ins = v.tensor_copy(ystg[:, ypar, :], ptrY[:, :])
            tok6[("ycp", ypar)] = p.inc(ins, "v", 1)

        @block.sync
        def _(s, oc=oc, dt=dt, ypar=ypar):
            s.wait_ge(p.sem("v"), tok6[("ycp", ypar)])
            ins = s.dma_start(out=dr["y"][oc * 128:(oc + 1) * 128, dt * 128:(dt + 1) * 128],
                              in_=ystg[:, ypar, :])
            tok6[("yst", ypar)] = p.inc(ins, "st_y%d" % ypar, 16)


def finish(p, block):
    dr = p.dram

    @block.sync
    def _(s):
        for name in list(p.sems):
            s.wait_ge(p.sem(name), p.tok(name))


def dump_copy(p, block):
    if not p.dump:
        return
    bounce = p.sbuf("dumpb", [128, 8192], BF16)
    bf = p.sbuf("dumpf", [128, 4096], F32)

    @block.gpsimd
    def _(g):
        for name in list(p.sems):
            g.wait_ge(p.sem(name), p.tok(name))
        for name in sorted(p.dump):
            src_ap = p.dram[name]
            dst_ap = p.dram[name + "_dump"]
            n = int(np.prod(src_ap.shape))
            tile = bounce if src_ap.dtype == BF16 else bf
            width = 8192 if src_ap.dtype == BF16 else 4096
            fl_s = src_ap
            while len(fl_s.shape) > 1:
                fl_s = fl_s.rearrange(" ".join(chr(97 + i) for i in range(len(fl_s.shape))) +
                                      " -> " + " ".join(chr(97 + i) for i in range(len(fl_s.shape) - 2)) +
                                      " (" + chr(97 + len(fl_s.shape) - 2) + " " + chr(97 + len(fl_s.shape) - 1) + ")")
            fl_d = dst_ap
            while len(fl_d.shape) > 1:
                fl_d = fl_d.rearrange(" ".join(chr(97 + i) for i in range(len(fl_d.shape))) +
                                      " -> " + " ".join(chr(97 + i) for i in range(len(fl_d.shape) - 2)) +
                                      " (" + chr(97 + len(fl_d.shape) - 2) + " " + chr(97 + len(fl_d.shape) - 1) + ")")
            pos = 0
            while pos < n:
                m = min(128 * width, n - pos)
                rows = m // width
                if rows >= 1:
                    m = rows * width
                    sview = fl_s[pos:pos + m].rearrange("(pp w) -> pp w", w=width)
                    dview = fl_d[pos:pos + m].rearrange("(pp w) -> pp w", w=width)
                    p.inc(g.dma_start(out=tile[0:rows, 0:width], in_=sview), "dm_dmp", 16)
                    g.wait_ge(p.sem("dm_dmp"), p.tok("dm_dmp"))
                    p.inc(g.dma_start(out=dview, in_=tile[0:rows, 0:width]), "dm_dmp", 16)
                    g.wait_ge(p.sem("dm_dmp"), p.tok("dm_dmp"))
                else:
                    sview = fl_s[pos:pos + m].rearrange("(o w) -> o w", o=1)
                    dview = fl_d[pos:pos + m].rearrange("(o w) -> o w", o=1)
                    p.inc(g.dma_start(out=tile[0:1, 0:m], in_=sview), "dm_dmp", 16)
                    g.wait_ge(p.sem("dm_dmp"), p.tok("dm_dmp"))
                    p.inc(g.dma_start(out=dview, in_=tile[0:1, 0:m]), "dm_dmp", 16)
                    g.wait_ge(p.sem("dm_dmp"), p.tok("dm_dmp"))
                pos += m


def dummy_y(p, block):
    ysb = p.sbuf("ysb", [128, 512], F32)

    @block.gpsimd
    def _(g):
        g.memset(ysb[:, :], 0.0)
        for r in range(0, TOK, 128):
            for cq in range(4):
                p.inc(g.dma_start(out=p.dram["y"][r:r + 128, cq * 512:(cq + 1) * 512],
                                  in_=ysb[:, :]), "st_y", 16)


def build_program(dump=(), phases=("ph0", "ph1", "ph2"), with_dummy_y=None):
    if with_dummy_y is None:
        with_dummy_y = "ph6" not in phases
    p = Prog(dump=dump)
    declare_io(p)
    with p.stack:
        with p.nc.allow_non_contiguous_dma(reason="small strided/broadcast loads"), p.nc.Block() as block:
            if "ph0" in phases:
                ph0(p, block)
            if "ph1" in phases:
                ph1(p, block)
            if "ph2" in phases:
                ph2(p, block)
            if "ph3" in phases:
                ph3(p, block)
            if "ph4" in phases:
                ph4(p, block)
            if "ph5" in phases:
                ph5(p, block)
            if "ph6" in phases:
                ph6(p, block)
            if with_dummy_y:
                dummy_y(p, block)
            dump_copy(p, block)
            finish(p, block)
    return p


def kernel(**inputs):
    maps = stage_inputs(inputs)
    p = build_program(phases=("ph0", "ph1", "ph2", "ph3", "ph4", "ph5", "ph6"), with_dummy_y=False)
    res = run_bass_kernel_spmd(p.nc, maps, core_ids=list(range(NCORES)))
    out = np.zeros((S, D), np.float32)
    for i in range(NCORES):
        own = np.concatenate([np.arange(c * 128, (c + 1) * 128) for c in range(i, NCHUNK, NCORES)])
        out[own] = res.results[i]["y"]
    return out[None]


# version 06 (final)
# speedup vs baseline: 1.0457x; 1.0457x over previous
import math
import numpy as np
from contextlib import ExitStack

import concourse.bass as bass
import concourse.mybir as mybir
from concourse.bass_utils import run_bass_kernel_spmd
from concourse.alu_op_type import AluOpType as OP

F32 = mybir.dt.float32
BF16 = mybir.dt.bfloat16
AF = mybir.ActivationFunctionType
AX = mybir.AxisListType

NCORES = 8
S = 8192
D = 2048
TOK = S // NCORES
NCHUNK = 64
CH = 64
NPAIR = 64
HEADS = 4
DK = 128
DV = 128
ZC = 512
D_FF = 5632
NEG = -1e30
PH3_PAIRS = NPAIR
W1ROWS = 704
W3ROWS = 256
WQ_ELEMS = 2 * 768 * D + W3ROWS * D_FF

import ml_dtypes
NF4 = np.array([-1.0, -0.6962, -0.5251, -0.3949, -0.2844, -0.1848, -0.0911, 0.0,
                0.0796, 0.1609, 0.2461, 0.3379, 0.4407, 0.5626, 0.723, 1.0],
               dtype=ml_dtypes.bfloat16).astype(np.float32)
NF4_MID = ((NF4[:-1] + NF4[1:]) / 2.0).astype(np.float32)


KCVALS = {1e-5: 0, 1.0: 1, 1e-6: 2, 0.25: 3, -1e30: 4, 0.0: 5}


class Prog:
    def __init__(self, dump=()):
        self.nc = bass.Bass("TRN2", target_bir_lowering=False, debug=False)
        self.stack = ExitStack()
        self.dump = set(dump)
        self.sems = {}
        self.cnt = {}
        self.sb = {}
        self.ps = {}
        self.dram = {}

    def sem(self, name):
        if name not in self.sems:
            self.sems[name] = self.stack.enter_context(self.nc.semaphore(name))
            self.cnt[name] = 0
        return self.sems[name]

    def inc(self, instr, name, n=1):
        instr.then_inc(self.sem(name), n)
        self.cnt[name] += n
        return self.cnt[name]

    def tok(self, name):
        self.sem(name)
        return self.cnt[name]

    def kc(self, val):
        return self.sb["kcol"][:, KCVALS[val]:KCVALS[val] + 1]

    def sbuf(self, name, shape, dtype):
        cm = self.nc.sbuf_tensor(name, shape, dtype)
        t = cm.__enter__()
        self._cms = getattr(self, "_cms", {})
        self._cms[name] = cm
        self.sb[name] = t
        return t

    def psum(self, name, shape, dtype):
        cm = self.nc.psum_tensor(name, shape, dtype)
        t = cm.__enter__()
        self._cms = getattr(self, "_cms", {})
        self._cms[name] = cm
        self.ps[name] = t
        return t

    def free(self, *names):
        order = {n: i for i, n in enumerate(self._cms.keys())}
        for name in sorted(names, key=lambda n: -order[n]):
            self._cms.pop(name).__exit__(None, None, None)
            self.sb.pop(name, None)
            self.ps.pop(name, None)

    def din(self, name, shape, dtype):
        ap = self.nc.dram_tensor(name, shape, dtype, kind="ExternalInput").ap()
        self.dram[name] = ap
        return ap

    def dout(self, name, shape, dtype):
        ap = self.nc.dram_tensor(name, shape, dtype, kind="ExternalOutput").ap()
        self.dram[name] = ap
        return ap

    def dtmp(self, name, shape, dtype, shared=False):
        if shared:
            ap = self.nc.dram_tensor(name, shape, dtype, kind="Internal",
                                     addr_space="Shared").ap()
        else:
            ap = self.nc.dram_tensor(name, shape, dtype, kind="Internal").ap()
        self.dram[name] = ap
        if name in self.dump:
            self.dram[name + "_dump"] = self.nc.dram_tensor(
                name + "_dump", shape, dtype, kind="ExternalOutput").ap()
        return ap


def host_constants(core):
    c = {}
    ident = np.eye(128, dtype=np.float32)
    c["identb"] = ident.astype(ml_dtypes.bfloat16)
    k = np.arange(128)
    c["cumsum_lhsT"] = (((k[:, None] // CH) == (k[None, :] // CH)) &
                        (k[:, None] <= k[None, :])).astype(np.float32)
    selA = np.zeros((128, 128), np.float32); selA[63, :] = 1.0
    selB = np.zeros((128, 128), np.float32); selB[127, :] = 1.0
    selH = np.zeros((128, 128), np.float32); selH[63, 0:64] = 1.0; selH[127, 64:128] = 1.0
    c["glselA"] = selA
    c["glselB"] = selB
    c["glselH"] = selH
    tril_strict = ((k[:, None] // CH) == (k[None, :] // CH)) & (k[:, None] > k[None, :])
    triu_incl = ((k[:, None] // CH) == (k[None, :] // CH)) & (k[:, None] <= k[None, :])
    c["maskA4"] = np.tile(np.where(tril_strict, 0.0, NEG).astype(np.float32), (1, HEADS))
    c["maskT4"] = np.tile(np.where(triu_incl, 0.0, NEG).astype(np.float32), (1, HEADS))
    ql = np.arange(128)
    kl = np.arange(1024)
    c["attn_mask"] = np.where(kl[None, :] <= 128 * core + ql[:, None], 0.0, NEG).astype(np.float32)
    c["identf"] = ident
    c["blockones_f"] = (((k[:, None] // CH) == (k[None, :] // CH))).astype(np.float32)
    inblk = (k[:, None] // CH) == (k[None, :] // CH)
    strict_low = inblk & (k[:, None] > k[None, :])
    strict_up = inblk & (k[:, None] < k[None, :])
    incl_up = inblk & (k[:, None] <= k[None, :])
    c["maskA_n"] = np.tile(np.where(strict_low, 0.0, 1e30).astype(np.float32), (1, HEADS))
    c["maskAT"] = np.tile(np.where(strict_up, 0.0, -1e30).astype(np.float32), (1, HEADS))
    c["maskQT"] = np.tile(np.where(incl_up, 0.0, -1e30).astype(np.float32), (1, HEADS))
    return c


def stage_inputs(inp):
    x = np.ascontiguousarray(np.asarray(inp["L0_attn_hidden_states"])[0])
    xT = np.ascontiguousarray(x.T)
    wqkv = np.asarray(inp["L0_attn_W_qkv"])
    wz = np.asarray(inp["L0_attn_W_z"])
    wb = np.asarray(inp["L0_attn_W_b"])
    wa = np.asarray(inp["L0_attn_W_a"])
    convw = np.asarray(inp["L0_attn_conv_w"])
    base_wT = np.ascontiguousarray(np.asarray(inp["L0_ffn_base_w"]).T)
    router_wT = np.ascontiguousarray(np.asarray(inp["L0_ffn_router_w"]).T)
    lora_A = np.asarray(inp["L0_ffn_lora_A"])
    lora_AR = np.ascontiguousarray(lora_A.transpose(1, 0, 2).reshape(D, 128))
    lora_BS = np.ascontiguousarray(np.asarray(inp["L0_ffn_lora_B"]).reshape(128, D))
    maps = []
    for i in range(NCORES):
        own = np.concatenate([np.arange(c * 128, (c + 1) * 128) for c in range(i, NCHUNK, NCORES)])
        qcols = np.arange(256 * i, 256 * (i + 1))
        cc = np.concatenate([qcols, 2048 + qcols, 4096 + np.arange(512 * i, 512 * (i + 1))])
        bacols = np.arange(4 * i, 4 * (i + 1))
        m = {
            "xT": xT,
            "x_my": np.ascontiguousarray(x[own]),
            "xT_my": np.ascontiguousarray(xT[:, own]),
            "slabidx": ((np.arange(32, dtype=np.int32)[None, :] * 128 + np.arange(128, dtype=np.int32)[:, None]) * 8 + i).astype(np.int32),
            "wqkv_my": np.ascontiguousarray(wqkv[:, cc]),
            "convw_my": np.ascontiguousarray(convw[cc]),
            "wz_my": np.ascontiguousarray(wz[:, 512 * i:512 * (i + 1)]),
            "wb_my": np.ascontiguousarray(wb[:, bacols]),
            "wa_my": np.ascontiguousarray(wa[:, bacols]),
            "dtb_my": np.ascontiguousarray(np.asarray(inp["L0_attn_dt_bias"])[bacols]).reshape(1, 4),
            "alog_my": np.ascontiguousarray(np.asarray(inp["L0_attn_A_log"])[bacols]).reshape(1, 4),
            "normw": np.asarray(inp["L0_attn_norm_w"]).reshape(1, 128),
            "w_out": np.asarray(inp["L0_attn_W_out"]),
            "base_wT": base_wT,
            "base_b": np.asarray(inp["L0_ffn_base_b"]).reshape(1, D),
            "router_wT": router_wT,
            "router_b": np.asarray(inp["L0_ffn_router_b"]).reshape(1, 8),
            "lora_AR": lora_AR,
            "lora_BS": lora_BS,
            "c0": np.asarray(inp["L0_c"]),
            "wmod0_my": np.ascontiguousarray(np.asarray(inp["L0_wmod"])[:, 1536 * i:1536 * (i + 1)]),
            "c1": np.asarray(inp["L1_c"]),
            "wmod1_my": np.ascontiguousarray(np.asarray(inp["L1_wmod"])[:, 1536 * i:1536 * (i + 1)]),
            "qkv1": np.asarray(inp["L1_attn_qkv"]),
            "oproj1": np.asarray(inp["L1_attn_o_proj"]),
            "gamma1": np.asarray(inp["L1_attn_gamma"]).reshape(1, D),
            "beta1": np.asarray(inp["L1_attn_beta"]).reshape(1, D),
            "w1_my": np.ascontiguousarray(np.concatenate([np.asarray(inp["L1_ffn_weight1"])[W1ROWS * i:W1ROWS * (i + 1)], np.zeros((64, D), dtype=np.asarray(inp["L1_ffn_weight1"]).dtype)])),
            "w2_my": np.ascontiguousarray(np.concatenate([np.asarray(inp["L1_ffn_weight2"])[W1ROWS * i:W1ROWS * (i + 1)], np.zeros((64, D), dtype=np.asarray(inp["L1_ffn_weight2"]).dtype)])),
            "w3_my": np.ascontiguousarray(np.asarray(inp["L1_ffn_weight3"])[W3ROWS * i:W3ROWS * (i + 1)]),
        }
        m.update(host_constants(i))
        maps.append(m)
    return maps


def declare_io(p):
    d = p.din
    d("xT", [D, S], F32)
    d("x_my", [TOK, D], F32)
    d("xT_my", [D, TOK], F32)
    d("slabidx", [128, 32], mybir.dt.int32)
    d("wqkv_my", [D, 1024], F32)
    d("convw_my", [1024, 4], F32)
    d("wz_my", [D, ZC], F32)
    d("wb_my", [D, 4], F32)
    d("wa_my", [D, 4], F32)
    d("dtb_my", [1, 4], F32)
    d("alog_my", [1, 4], F32)
    d("normw", [1, 128], F32)
    d("w_out", [4096, D], F32)
    d("base_wT", [D, D], F32)
    d("base_b", [1, D], F32)
    d("router_wT", [D, 8], F32)
    d("router_b", [1, 8], F32)
    d("lora_AR", [D, 128], F32)
    d("lora_BS", [128, D], F32)
    d("c0", [1, D], F32)
    d("wmod0_my", [D, 1536], F32)
    d("c1", [1, D], F32)
    d("wmod1_my", [D, 1536], F32)
    d("qkv1", [D, 6144], F32)
    d("oproj1", [D, D], F32)
    d("gamma1", [1, D], F32)
    d("beta1", [1, D], F32)
    d("w1_my", [768, D], BF16)
    d("w2_my", [768, D], BF16)
    d("w3_my", [W3ROWS, D_FF], BF16)
    d("identb", [128, 128], BF16)
    d("identf", [128, 128], F32)
    d("blockones_f", [128, 128], F32)
    d("maskA_n", [128, 512], F32)
    d("maskAT", [128, 512], F32)
    d("maskQT", [128, 512], F32)
    d("cumsum_lhsT", [128, 128], F32)
    d("glselA", [128, 128], F32)
    d("glselB", [128, 128], F32)
    d("glselH", [128, 128], F32)
    d("maskA4", [128, 512], F32)
    d("maskT4", [128, 512], F32)
    d("attn_mask", [128, 1024], F32)
    p.dtmp("mod_in", [1, 1536], F32)
    p.dtmp("mod_all", [NCORES, 1536], F32, shared=True)
    p.dtmp("stat_in", [2, TOK], F32)
    p.dtmp("stat_all", [NCORES, 2, TOK], F32, shared=True)
    p.dtmp("mixc", [8, 128, S], BF16)
    p.dtmp("ztok", [S, ZC], BF16)
    p.dtmp("gbtok", [S, 8], F32)
    p.dtmp("rows0", [16, S], F32)
    for st in range(8):
        p.dtmp(f"slabi{st}", [ZC, 1024], BF16)
        p.dtmp(f"slaba{st}", [NCORES, ZC, 1024], BF16, shared=True)
    p.dtmp("qTh", [D, TOK], BF16)
    p.dtmp("aoTh", [D, TOK], BF16)
    for sg in range(4):
        p.dtmp(f"kti{sg}", [D, 256], BF16)
        p.dtmp(f"kta{sg}", [NCORES, D, 256], BF16, shared=True)
        p.dtmp(f"vti{sg}", [256, D], BF16)
        p.dtmp(f"vta{sg}", [NCORES, 256, D], BF16, shared=True)
    p.dtmp("wq_in", [1, WQ_ELEMS], BF16)
    p.dtmp("wq_all", [NCORES, WQ_ELEMS], BF16, shared=True)
    p.dtmp("nfsum_in", [1, 4], F32)
    p.dtmp("nfsum_all", [1, 4], F32, shared=True)
    p.dout("y", [TOK, D], F32)




def ph0(p, block):
    dr = p.dram
    kcol = p.sbuf("kcol", [128, 8], F32)

    @block.vector
    def _(v):
        for val, idx in KCVALS.items():
            ins = v.memset(kcol[:, idx:idx + 1], float(val))
        p.inc(ins, "v", 1)

    modc = p.sbuf("modc", [128, 2, 16, 6], F32)
    p.sbuf("identb_s", [128, 128], BF16)
    p.sbuf("identf_s", [128, 128], F32)
    sc0 = p.sbuf("sc0", [128, 16], F32)
    sc1 = p.sbuf("sc1", [128, 16], F32)
    modps = p.psum("modps", [128, 12], F32)
    wmt = p.sbuf("wmt", [128, 16, 128], F32)
    modsb = p.sbuf("modsb", [128, 12], F32)
    tmp2 = p.sbuf("tmp2", [128, 2], F32)
    tmps = p.sbuf("tmps", [128, 16, 4], F32)

    @block.sync
    def _(s):
        p.inc(s.dma_start(out=sc0[:, :], in_=dr["c0"].rearrange("o (j pp) -> pp (o j)", pp=128)), "dm_c", 16)
        p.inc(s.dma_start(out=sc1[:, :], in_=dr["c1"].rearrange("o (j pp) -> pp (o j)", pp=128)), "dm_c", 16)

    @block.scalar
    def _(sc):
        sc.wait_ge(p.sem("dm_c"), p.tok("dm_c"))
        sc.activation(sc0[:, :], sc0[:, :], AF.Silu)
        p.inc(sc.activation(sc1[:, :], sc1[:, :], AF.Silu), "s", 1)

    for L, wn in enumerate(("wmod0_my", "wmod1_my")):
        rhs = (sc0, sc1)[L]
        for ct in range(12):
            @block.sync
            def _(s, wn=wn, ct=ct):
                s.wait_ge(p.sem("t"), p.tok("t"))
                p.inc(s.dma_start(
                    out=wmt[:, :, :],
                    in_=dr[wn].rearrange("(dt pp) c -> pp dt c", pp=128)[:, :, ct * 128:(ct + 1) * 128]),
                    "dm_wmt", 16)

            @block.tensor
            def _(t, L=L, ct=ct):
                t.wait_ge(p.sem("dm_wmt"), p.tok("dm_wmt"))
                t.wait_ge(p.sem("s"), 1)
                t.wait_ge(p.sem("v"), p.tok("v"))
                for dt in range(16):
                    ins = t.matmul(modps[:, ct:ct + 1], wmt[:, dt, :], rhs[:, dt:dt + 1],
                                   start=(dt == 0), stop=(dt == 15))
                p.inc(ins, "t", 1)

        @block.vector
        def _(v):
            v.wait_ge(p.sem("t"), p.tok("t"))
            v.wait_ge(p.sem("st_mod"), p.tok("st_mod"))
            p.inc(v.tensor_copy(modsb[:, :], modps[:, :]), "v", 1)

        @block.gpsimd
        def _(g, L=L):
            g.wait_ge(p.sem("v"), p.tok("v"))
            p.inc(g.dma_start(out=dr["mod_in"].rearrange("o (ct pp) -> pp (o ct)", pp=128),
                              in_=modsb[:, :]), "st_mod", 16)
            g.wait_ge(p.sem("st_mod"), p.tok("st_mod"))
            p.inc(g.collective_compute(
                "AllGather", OP.bypass, replica_groups=[list(range(NCORES))],
                ins=[dr["mod_in"][:, :]], outs=[dr["mod_all"][:, :]]), "cc_mod", 1)
            g.wait_ge(p.sem("cc_mod"), p.tok("cc_mod"))
            flat = dr["mod_all"].rearrange("a b -> (a b)").rearrange("(vv dt pp) -> vv dt pp", vv=6, pp=128)
            for dt in range(16):
                p.inc(g.dma_start(
                    out=modc[:, L, dt, :],
                    in_=flat[0:6, dt, :].rearrange("vv pp -> pp vv")), "dm_modc", 16)

        @block.vector
        def _(v, L=L):
            v.wait_ge(p.sem("dm_modc"), p.tok("dm_modc"))
            for dt in range(16):
                v.tensor_scalar(modc[:, L, dt, 1:2], modc[:, L, dt, 1:2], 1.0, None, OP.add)
                ins = v.tensor_scalar(modc[:, L, dt, 4:5], modc[:, L, dt, 4:5], 1.0, None, OP.add)
            for dt in range(16):
                for j, col in enumerate((2, 5)):
                    ins = v.tensor_scalar(tmps[:, dt, j:j + 1], modc[:, L, dt, col:col + 1], 0.0, 2.0, OP.is_ge, OP.mult)
            v.wait_ge(p.sem("v"), p.inc(ins, "v", 1))
            for dt in range(16):
                for j, col in enumerate((2, 5)):
                    ins = v.tensor_scalar(tmps[:, dt, j:j + 1], tmps[:, dt, j:j + 1], -1.0, None, OP.add)
            for dt in range(16):
                for j, col in enumerate((2, 5)):
                    ins = v.tensor_scalar(tmps[:, dt, 2 + j:3 + j], modc[:, L, dt, col:col + 1], -1.0, None, OP.mult)
            v.wait_ge(p.sem("v"), p.inc(ins, "v", 1))
            for dt in range(16):
                for j, col in enumerate((2, 5)):
                    ins = v.tensor_tensor(tmps[:, dt, 2 + j:3 + j], modc[:, L, dt, col:col + 1], tmps[:, dt, 2 + j:3 + j], OP.max)
            v.wait_ge(p.sem("v"), p.inc(ins, "v", 1))
            for dt in range(16):
                for j, col in enumerate((2, 5)):
                    ins = v.tensor_scalar(tmps[:, dt, 2 + j:3 + j], tmps[:, dt, 2 + j:3 + j], 0.25, None, OP.max)
            v.wait_ge(p.sem("v"), p.inc(ins, "v", 1))
            for dt in range(16):
                for j, col in enumerate((2, 5)):
                    ins = v.tensor_tensor(modc[:, L, dt, col:col + 1], tmps[:, dt, j:j + 1], tmps[:, dt, 2 + j:3 + j], OP.mult)
            p.inc(ins, "v", 1)


def ph1(p, block):
    dr = p.dram
    xrow = p.sbuf("xrow", [128, D], F32)
    sq = p.sbuf("sq", [128, D], F32)
    st = p.sbuf("st", [128, 8], F32)

    for cidx in range(8):
        @block.sync
        def _(s, cidx=cidx):
            s.wait_ge(p.sem("st_stat"), p.tok("st_stat"))
            p.inc(s.dma_start(out=xrow[:, :], in_=dr["x_my"][cidx * 128:(cidx + 1) * 128, :]), "dm_xrow", 16)

        @block.vector
        def _(v):
            v.wait_ge(p.sem("dm_xrow"), p.tok("dm_xrow"))
            v.reduce_sum(st[:, 0:1], xrow[:, :], axis=AX.X)
            v.tensor_tensor(sq[:, :], xrow[:, :], xrow[:, :], OP.mult)
            v.reduce_sum(st[:, 1:2], sq[:, :], axis=AX.X)
            v.tensor_scalar(st[:, 2:3], st[:, 0:1], 1.0 / D, None, OP.mult)
            v.tensor_scalar(st[:, 6:7], st[:, 1:2], 1.0 / D, None, OP.mult)
            v.tensor_scalar(st[:, 3:4], st[:, 2:3], st[:, 2:3], None, OP.mult)
            v.tensor_scalar(st[:, 5:6], st[:, 2:3], -1.0, None, OP.mult)
            p.inc(v.tensor_scalar(st[:, 3:4], st[:, 6:7], st[:, 3:4], None, OP.subtract), "v", 1)

        @block.scalar
        def _(sc):
            sc.wait_ge(p.sem("v"), p.tok("v"))
            sc.activation(st[:, 7:8], st[:, 3:4], AF.Ln, bias=p.kc(1e-5))
            p.inc(sc.activation(st[:, 4:5], st[:, 7:8], AF.Exp, scale=-0.5), "s", 1)

        @block.gpsimd
        def _(g, cidx=cidx):
            g.wait_ge(p.sem("s"), p.tok("s"))
            p.inc(g.dma_start(out=dr["stat_in"][0:1, cidx * 128:(cidx + 1) * 128]
                              .rearrange("o (t u) -> (o t) u", u=1), in_=st[:, 5:6]), "st_stat", 16)
            p.inc(g.dma_start(out=dr["stat_in"][1:2, cidx * 128:(cidx + 1) * 128]
                              .rearrange("o (t u) -> (o t) u", u=1), in_=st[:, 4:5]), "st_stat", 16)

    @block.gpsimd
    def _(g):
        g.wait_ge(p.sem("st_stat"), p.tok("st_stat"))
        p.inc(g.collective_compute(
            "AllGather", OP.bypass, replica_groups=[list(range(NCORES))],
            ins=[dr["stat_in"][:, :]], outs=[dr["stat_all"][:, :, :]]), "cc_stat", 1)


LN_EPS_L0 = 1e-5


def ph2(p, block):
    dr = p.dram
    nc = p.nc
    modc = p.sb["modc"]
    wqw = p.sbuf("wqw", [128, 16, 1024], BF16)
    wzw = p.sbuf("wzw", [128, 16, 520], BF16)
    onessa = p.sbuf("onessa", [128, 16, 4], BF16)
    sbq = p.sbuf("sbq", [2, 1024], F32)
    sbz = p.sbuf("sbz", [2, 520], F32)
    bqcol = p.sbuf("bqcol", [128, 8], F32)
    bzbc = p.sbuf("bzbc", [128, 520], F32)
    convsb = p.sbuf("convsb", [128, 8, 4], F32)
    dtbbc = p.sbuf("dtbbc", [128, 4], F32)
    badd = p.sbuf("badd", [128, 8], F32)
    nealog = p.sbuf("nealog", [128, 4], F32)
    xtb = [p.sbuf(f"xtb{i}", [128, 16, 512], BF16) for i in range(2)]
    rstdbc = p.sbuf("rstdbc", [128, 512], F32)
    negmur = p.sbuf("negmur", [1, 512], F32)
    rstdcol = p.sbuf("rstdcol", [128, 4], F32)
    convbuf = p.sbuf("convbuf", [128, 8, 515], BF16)
    cvtmp = p.sbuf("cvtmp", [128, 515], BF16)
    evtmp = p.sbuf("evtmp", [128, 512], F32)
    ztmp = p.sbuf("ztmp", [128, 512], BF16)
    gtmp = p.sbuf("gtmp", [128, 16], F32)
    mixout = p.sbuf("mixout", [128, 8, 512], BF16)
    pq = [p.psum("pq0", [128, 512], F32), p.psum("pq1", [128, 512], F32)]
    pz = p.psum("pz", [128, 512], F32)
    pba = p.psum("pba", [128, 16], F32)
    pr = p.psum("pr", [2, 1024], F32)

    @block.gpsimd
    def _(g):
        g.wait_ge(p.sem("cc_mod"), 2)
        for dt in range(16):
            p.inc(g.dma_start(out=wqw[:, dt, :], in_=dr["wqkv_my"][dt * 128:(dt + 1) * 128, :]), "dm_wprep", 16)
            p.inc(g.dma_start(out=wzw[:, dt, 0:512], in_=dr["wz_my"][dt * 128:(dt + 1) * 128, :]), "dm_wprep", 16)
            p.inc(g.dma_start(out=wzw[:, dt, 512:516], in_=dr["wb_my"][dt * 128:(dt + 1) * 128, :]), "dm_wprep", 16)
            p.inc(g.dma_start(out=wzw[:, dt, 516:520], in_=dr["wa_my"][dt * 128:(dt + 1) * 128, :]), "dm_wprep", 16)
        for a in range(8):
            p.inc(g.dma_start(out=convsb[:, a, :], in_=dr["convw_my"][a * 128:(a + 1) * 128, :]), "dm_wprep", 16)
        p.inc(g.dma_start(out=dtbbc[:, :], in_=dr["dtb_my"].broadcast_to([128, 4])), "dm_wprep", 16)
        p.inc(g.dma_start(out=nealog[:, :], in_=dr["alog_my"].broadcast_to([128, 4])), "dm_wprep", 16)

    @block.scalar
    def _(sc):
        sc.wait_ge(p.sem("dm_wprep"), p.tok("dm_wprep"))
        p.inc(sc.activation(nealog[:, :], nealog[:, :], AF.Exp, scale=1.0), "s", 1)

    @block.vector
    def _(v):
        v.wait_ge(p.sem("s"), p.tok("s"))
        v.tensor_scalar(nealog[:, :], nealog[:, :], -1.0, None, OP.mult)
        for dt in range(16):
            v.tensor_scalar(wqw[:, dt, :], wqw[:, dt, :], modc[:, 0, dt, 1:2], None, OP.mult)
            v.tensor_scalar(wzw[:, dt, :], wzw[:, dt, :], modc[:, 0, dt, 1:2], None, OP.mult)
            v.memset(onessa[:, dt, 0:1], 1.0)
            ins = v.tensor_copy(onessa[:, dt, 1:2], modc[:, 0, dt, 0:1])
        p.inc(ins, "v", 1)

    @block.tensor
    def _(t):
        t.wait_ge(p.sem("v"), p.tok("v"))
        for half in range(2):
            for dt in range(16):
                i1 = t.matmul(pr[:, half * 512:(half + 1) * 512], onessa[:, dt, 0:2],
                              wqw[:, dt, half * 512:(half + 1) * 512], start=(dt == 0), stop=(dt == 15))
        p.inc(i1, "t", 1)

    @block.vector
    def _(v):
        v.wait_ge(p.sem("t"), p.tok("t"))
        p.inc(v.tensor_copy(sbq[:, :], pr[:, :]), "v", 1)

    @block.tensor
    def _(t):
        t.wait_ge(p.sem("v"), p.tok("v"))
        for dt in range(16):
            t.matmul(pr[:, 0:512], onessa[:, dt, 0:2], wzw[:, dt, 0:512], start=(dt == 0), stop=(dt == 15))
        for dt in range(16):
            i1 = t.matmul(pr[:, 512:520], onessa[:, dt, 0:2], wzw[:, dt, 512:520], start=(dt == 0), stop=(dt == 15))
        p.inc(i1, "t", 1)

    @block.vector
    def _(v):
        v.wait_ge(p.sem("t"), p.tok("t"))
        p.inc(v.tensor_copy(sbz[:, :], pr[0:2, 0:520]), "v", 1)

    @block.gpsimd
    def _(g):
        g.wait_ge(p.sem("v"), p.tok("v"))
        p.inc(g.dma_start(out=dr["rows0"][2:3, 0:1024], in_=sbq[1:2, :]), "st_rows", 16)
        p.inc(g.dma_start(out=dr["rows0"][3:4, 0:520], in_=sbz[1:2, :]), "st_rows", 16)
        g.wait_ge(p.sem("st_rows"), p.tok("st_rows"))
        p.inc(g.dma_start(out=bqcol[:, :], in_=dr["rows0"][2:3, 0:1024].rearrange("o (ct pp) -> (o pp) ct", pp=128)), "dm_bq", 16)
        p.inc(g.dma_start(out=bzbc[:, :], in_=dr["rows0"][3:4, 0:520].broadcast_to([128, 520])), "dm_bq", 16)

    @block.vector
    def _(v):
        v.wait_ge(p.sem("dm_bq"), p.tok("dm_bq"))
        v.tensor_copy(badd[:, 0:4], bzbc[:, 512:516])
        p.inc(v.tensor_tensor(badd[:, 4:8], bzbc[:, 516:520], dtbbc[:, :], OP.add), "v", 1)

    @block.vector
    def _(v):
        p.inc(v.memset(convbuf[:, :, 0:3], 0.0), "v", 1)

    tok_cons = {}
    for tt in range(16):
        t0 = tt * 512
        xb = xtb[tt % 2]

        @block.gpsimd
        def _(g, tt=tt, t0=t0, xb=xb):
            g.wait_ge(p.sem("cc_stat"), 1)
            if tt >= 2:
                g.wait_ge(p.sem("t"), tok_cons[("xtb", tt - 2)])
            for dt in range(16):
                p.inc(g.dma_start(out=xb[:, dt, :], in_=dr["xT"][dt * 128:(dt + 1) * 128, t0:t0 + 512]),
                      "dm_xtb%d" % (tt % 2), 16)

        @block.sync
        def _(s, tt=tt, t0=t0):
            s.wait_ge(p.sem("cc_stat"), 1)
            if tt >= 1:
                s.wait_ge(p.sem("v"), tok_cons[("rows", tt - 1)])
            for off in range(0, 512, 128):
                gc = (t0 + off) // 128
                src = dr["stat_all"][gc % 8, 1:2, (gc // 8) * 128:(gc // 8) * 128 + 128]
                p.inc(s.dma_start(out=rstdbc[:, off:off + 128],
                                  in_=src.broadcast_to([128, 128])), "dm_srow", 16)
                src2 = dr["stat_all"][gc % 8, 0:1, (gc // 8) * 128:(gc // 8) * 128 + 128]
                p.inc(s.dma_start(out=negmur[0:1, off:off + 128],
                                  in_=src2), "dm_srow", 16)
                src3 = dr["stat_all"][gc % 8, 1:2, (gc // 8) * 128:(gc // 8) * 128 + 128]
                p.inc(s.dma_start(out=rstdcol[:, off // 128:off // 128 + 1],
                                  in_=src3.rearrange("o (t u) -> (o t) u", u=1)), "dm_srow", 16)

        for ct in range(8):
            ps = pq[ct % 2]

            @block.tensor
            def _(t, tt=tt, ct=ct, ps=ps, xb=xb):
                t.wait_ge(p.sem("dm_xtb%d" % (tt % 2)), p.tok("dm_xtb%d" % (tt % 2)))
                t.wait_ge(p.sem("dm_srow"), p.tok("dm_srow"))
                if tt == 0 and ct == 0:
                    t.wait_ge(p.sem("dm_bq"), p.tok("dm_bq"))
                key = ("pq", ct % 2)
                if key in tok_cons:
                    t.wait_ge(p.sem("v"), tok_cons[key])
                for dt in range(16):
                    t.matmul(ps[:, :], wqw[:, dt, ct * 128:(ct + 1) * 128], xb[:, dt, :],
                             start=(dt == 0), stop=False)
                ins = t.matmul(ps[:, :], sbq[0:1, ct * 128:(ct + 1) * 128], negmur[0:1, :],
                               start=False, stop=True)
                p.inc(ins, "t", 1)

            @block.vector
            def _(v, tt=tt, ct=ct, ps=ps):
                v.wait_ge(p.sem("t"), p.tok("t"))
                if tt > 0:
                    v.tensor_copy(convbuf[:, ct, 0:3], convbuf[:, ct, 512:515])
                v.tensor_tensor(evtmp[:, :], ps[:, :], rstdbc[:, :], OP.mult)
                key = ("pq", ct % 2)
                tok_cons[key] = p.inc(
                    v.tensor_scalar(convbuf[:, ct, 3:515], evtmp[:, :], bqcol[:, ct:ct + 1], None, OP.add),
                    "v", 1)
                v.tensor_scalar(cvtmp[:, 3:515], convbuf[:, ct, 0:512], convsb[:, ct, 0:1], None, OP.mult)
                v.scalar_tensor_tensor(cvtmp[:, 3:515], convbuf[:, ct, 1:513], convsb[:, ct, 1:2],
                                       cvtmp[:, 3:515], OP.mult, OP.add)
                v.scalar_tensor_tensor(cvtmp[:, 3:515], convbuf[:, ct, 2:514], convsb[:, ct, 2:3],
                                       cvtmp[:, 3:515], OP.mult, OP.add)
                ins = v.scalar_tensor_tensor(cvtmp[:, 3:515], convbuf[:, ct, 3:515], convsb[:, ct, 3:4],
                                             cvtmp[:, 3:515], OP.mult, OP.add)
                p.inc(ins, "v", 1)

            @block.scalar
            def _(sc, tt=tt, ct=ct):
                sc.wait_ge(p.sem("v"), p.tok("v"))
                if ("mixst", ct) in tok_cons:
                    sc.wait_ge(p.sem("st_mixc"), tok_cons[("mixst", ct)])
                p.inc(sc.activation(mixout[:, ct, :], cvtmp[:, 3:515], AF.Silu), "s", 1)

            @block.sync
            def _(s, tt=tt, ct=ct, t0=t0):
                s.wait_ge(p.sem("s"), p.tok("s"))
                tok_cons[("mixst", ct)] = p.inc(
                    s.dma_start(out=dr["mixc"][ct, :, t0:t0 + 512], in_=mixout[:, ct, :]), "st_mixc", 16)

        for tb in range(4):
            @block.tensor
            def _(t, tt=tt, tb=tb, xb=xb):
                key = ("pz",)
                if key in tok_cons:
                    t.wait_ge(p.sem("v"), tok_cons[key])
                for dt in range(16):
                    t.matmul(pz[:, :], xb[:, dt, tb * 128:(tb + 1) * 128], wzw[:, dt, 0:512],
                             start=(dt == 0), stop=False)
                t.matmul(pz[:, :], negmur[0:1, tb * 128:(tb + 1) * 128], sbz[0:1, 0:512],
                         start=False, stop=True)
                for dt in range(16):
                    t.matmul(pba[:, 0:8], xb[:, dt, tb * 128:(tb + 1) * 128], wzw[:, dt, 512:520],
                             start=(dt == 0), stop=False)
                ins = t.matmul(pba[:, 0:8], negmur[0:1, tb * 128:(tb + 1) * 128], sbz[0:1, 512:520],
                               start=False, stop=True)
                p.inc(ins, "t", 1)
                if tb == 3:
                    tok_cons[("xtb", tt)] = p.tok("t")

            @block.vector
            def _(v, tt=tt, tb=tb, t0=t0):
                v.wait_ge(p.sem("t"), p.tok("t"))
                if ("zst",) in tok_cons:
                    v.wait_ge(p.sem("st_z"), tok_cons[("zst",)])
                rc = rstdcol[:, tb:tb + 1]
                v.tensor_scalar(evtmp[:, :], pz[:, :], rc, None, OP.mult)
                v.tensor_scalar(gtmp[:, 0:8], pba[:, 0:8], rc, None, OP.mult)
                v.tensor_tensor(ztmp[:, :], evtmp[:, :], bzbc[:, 0:512], OP.add)
                tok_cons[("pz",)] = p.inc(
                    v.tensor_tensor(gtmp[:, 0:8], gtmp[:, 0:8], badd[:, :], OP.add), "v", 1)

            @block.scalar
            def _(sc):
                sc.wait_ge(p.sem("v"), p.tok("v"))
                sc.activation(gtmp[:, 12:16], gtmp[:, 4:8], AF.Exp)
                p.inc(sc.activation(gtmp[:, 12:16], gtmp[:, 12:16], AF.Ln, bias=p.kc(1.0)), "s", 1)

            @block.vector
            def _(v):
                v.wait_ge(p.sem("s"), p.tok("s"))
                p.inc(v.tensor_tensor(gtmp[:, 8:12], gtmp[:, 12:16], nealog[:, :], OP.mult), "v", 1)

            @block.sync
            def _(s, tt=tt, tb=tb, t0=t0):
                s.wait_ge(p.sem("v"), p.tok("v"))
                r0 = t0 + tb * 128
                p.inc(s.dma_start(out=dr["ztok"][r0:r0 + 128, :], in_=ztmp[:, :]), "st_z", 16)
                p.inc(s.dma_start(out=dr["gbtok"][r0:r0 + 128, 0:4], in_=gtmp[:, 8:12]), "st_z", 16)
                tok_cons[("zst",)] = p.inc(
                    s.dma_start(out=dr["gbtok"][r0:r0 + 128, 4:8], in_=gtmp[:, 0:4]), "st_z", 16)
                if tb == 3:
                    tok_cons[("rows", tt)] = p.tok("v")


def ph3(p, block):
    dr = p.dram
    p.free("wqw", "wzw", "xtb0", "xtb1", "convbuf", "cvtmp", "mixout", "evtmp", "ztmp",
           "rstdbc", "onessa", "bqcol", "bzbc", "gtmp", "rstdcol", "negmur", "sbq", "sbz",
           "badd", "dtbbc", "nealog", "convsb")
    p.free("pq0", "pq1", "pz", "pba", "pr", "modps")

    p.free("st", "sq", "xrow", "tmps", "tmp2", "modsb", "wmt", "sc1", "sc0")
    identb = p.sb["identb_s"]
    identf = p.sb["identf_s"]
    blockones = p.sbuf("blockones_s", [128, 128], F32)
    cum_l = p.sbuf("cum_l", [128, 128], F32)
    glselA = p.sbuf("glselA_s", [128, 128], F32)
    glselB = p.sbuf("glselB_s", [128, 128], F32)
    glselH = p.sbuf("glselH_s", [128, 128], F32)
    maskA = p.sbuf("maskA_s", [128, 512], F32)
    maskAT = p.sbuf("maskAT_s", [128, 512], F32)
    maskQT = p.sbuf("maskQT_s", [128, 512], F32)
    normbc = p.sbuf("normbc", [128, 128], F32)
    gb4 = p.sbuf("gb4", [128, NPAIR, 8], F32)
    beta4 = p.sbuf("beta4", [128, NPAIR, 4], F32)
    gc4 = p.sbuf("gc4", [128, NPAIR, 4], F32)
    egl4 = p.sbuf("egl4", [128, NPAIR, 8], F32)
    gl4 = p.sbuf("gl4", [128, NPAIR, 8], F32)
    glh4 = p.sbuf("glh4", [128, NPAIR, 4], F32)

    @block.sync
    def _(s):
        s.wait_ge(p.sem("st_z"), p.tok("st_z"))
        for nm, t in (("identb", identb), ("identf", identf), ("blockones_f", blockones),
                      ("cumsum_lhsT", cum_l), ("glselA", glselA), ("glselB", glselB),
                      ("glselH", glselH), ("maskA_n", maskA), ("maskAT", maskAT),
                      ("maskQT", maskQT)):
            p.inc(s.dma_start(out=t[:, :], in_=dr[nm][:, :]), "dm_p3c", 16)
        p.inc(s.dma_start(out=normbc[:, :], in_=dr["normw"].broadcast_to([128, 128])), "dm_p3c", 16)
        p.inc(s.dma_start(out=gb4[:, :, :],
                          in_=dr["gbtok"].rearrange("(P pp) c -> pp P c", pp=128)), "dm_p3c", 16)

    @block.scalar
    def _(sc):
        sc.wait_ge(p.sem("dm_p3c"), p.tok("dm_p3c"))
        p.inc(sc.activation(beta4[:, :, :], gb4[:, :, 4:8], AF.Sigmoid), "s", 1)

    pgc = p.psum("pgc", [128, 512], F32)
    for half in range(2):
        @block.tensor
        def _(t, half=half):
            t.wait_ge(p.sem("dm_p3c"), p.tok("dm_p3c"))
            t.wait_ge(p.sem("v"), p.tok("v"))
            for pr_ in range(32):
                P = half * 32 + pr_
                ins = t.matmul(pgc[:, pr_ * 16:pr_ * 16 + 4], cum_l[:, :], gb4[:, P, 0:4],
                               start=True, stop=True)
            p.inc(ins, "t", 1)

        @block.vector
        def _(v, half=half):
            v.wait_ge(p.sem("t"), p.tok("t"))
            for pr_ in range(32):
                P = half * 32 + pr_
                ins = v.tensor_copy(gc4[:, P, :], pgc[:, pr_ * 16:pr_ * 16 + 4])
            p.inc(ins, "v", 1)

    for half in range(2):
        @block.tensor
        def _(t, half=half):
            t.wait_ge(p.sem("v"), p.tok("v"))
            for pr_ in range(32):
                P = half * 32 + pr_
                t.matmul(pgc[:, pr_ * 16:pr_ * 16 + 4], glselA[:, :], gc4[:, P, 0:4],
                         start=True, stop=True)
                t.matmul(pgc[:, pr_ * 16 + 4:pr_ * 16 + 8], glselH[:, :], gc4[:, P, 0:4],
                         start=True, stop=True)
                ins = t.matmul(pgc[:, pr_ * 16 + 8:pr_ * 16 + 12], glselB[:, :], gc4[:, P, 0:4],
                               start=True, stop=True)
            p.inc(ins, "t", 1)

        @block.vector
        def _(v, half=half):
            v.wait_ge(p.sem("t"), p.tok("t"))
            for pr_ in range(32):
                P = half * 32 + pr_
                v.tensor_copy(gl4[:, P, 0:4], pgc[:, pr_ * 16:pr_ * 16 + 4])
                v.tensor_copy(glh4[:, P, :], pgc[:, pr_ * 16 + 4:pr_ * 16 + 8])
                ins = v.tensor_copy(gl4[:, P, 4:8], pgc[:, pr_ * 16 + 8:pr_ * 16 + 12])
            p.inc(ins, "v", 1)

    @block.scalar
    def _(sc):
        sc.wait_ge(p.sem("v"), p.tok("v"))
        p.inc(sc.activation(egl4[:, :, :], gl4[:, :, :], AF.Exp), "s", 1)
    p.free("pgc")

    chm = p.sbuf("chm", [128, 8, 128], BF16)
    tokt = p.sbuf("tokt", [128, 8, 128], BF16)
    lsum = p.sbuf("lsum", [128, 8], F32)
    sqs = p.sbuf("sqs", [128, 128], BF16)
    cols = p.sbuf("cols", [128, 24], F32)
    ecols = p.sbuf("ecols", [128, 12], F32)
    scol = p.sbuf("scol", [128, 12], F32)
    R4 = p.sbuf("R4", [128, 512], F32)
    dexp = p.sbuf("dexp", [128, 3, 512], F32)
    etmp = p.sbuf("etmp", [128, 512], F32)
    A4 = p.sbuf("A4", [128, 512], BF16)
    AT4 = p.sbuf("AT4", [128, 512], BF16)
    QT4 = p.sbuf("QT4", [128, 512], BF16)
    Ttl = p.sbuf("Ttl", [128, 4, 128], BF16)
    TtT = p.sbuf("TtT", [128, 4, 128], BF16)
    Xc = p.sbuf("Xc", [128, 4, 128], BF16)
    Yc = p.sbuf("Yc", [128, 4, 128], BF16)
    u4 = p.sbuf("u4", [128, 4, 128], BF16)
    wT4 = p.sbuf("wT4", [128, 4, 128], BF16)
    vb4 = p.sbuf("vb4", [128, 4, 128], BF16)
    kbg4 = p.sbuf("kbg4", [128, 4, 128], BF16)
    kd4 = p.sbuf("kd4", [128, 4, 128], BF16)
    S4 = p.sbuf("S4", [128, 4, 128], F32)
    S4b = p.sbuf("S4b", [128, 4, 128], BF16)
    vnew4 = p.sbuf("vnew4", [128, 4, 128], BF16)
    e14 = p.sbuf("e14", [128, 4, 128], F32)
    otile = p.sbuf("otile", [128, 4, 128], F32)
    zt3 = p.sbuf("zt3", [128, 512], BF16)
    gat = p.sbuf("gat", [128, 512], BF16)
    slabt = p.sbuf("slabt", [128, 4, 128], BF16)
    rsum = p.sbuf("rsum", [128, 12], F32)
    ptb = p.psum("ptb", [128, 1024], BF16)
    pA = p.psum("pA", [128, 512], F32)
    pQK = p.psum("pQK", [128, 512], F32)
    pT = p.psum("pT", [128, 1024], F32)
    pdec = pT
    psA2 = p.psum("psA2", [128, 512], F32)
    psB2 = p.psum("psB2", [128, 512], F32)
    pSS = p.psum("pSS", [128, 512], F32)

    @block.vector
    def _(v):
        v.memset(S4[:, :, :], 0.0)
        p.inc(v.memset(S4b[:, :, :], 0.0), "v", 1)

    LNDKH = float(np.log(DK ** -0.5))
    tokd = {}
    for P in range(PH3_PAIRS):
        t0 = P * 128

        @block.sync
        def _(g, P=P, t0=t0):
            if P > 0:
                g.wait_ge(p.sem("t"), tokd[("chmfree", P - 1)])
            else:
                g.wait_ge(p.sem("st_mixc"), p.tok("st_mixc"))
                g.wait_ge(p.sem("st_z"), p.tok("st_z"))
            for ct in range(8):
                p.inc(g.dma_start(out=chm[:, ct, :], in_=dr["mixc"][ct, :, t0:t0 + 128]), "dm_chm", 16)
            if P > 0:
                g.wait_ge(p.sem("v"), tokd[("ztfree", P - 1)])
            p.inc(g.dma_start(out=zt3[:, :], in_=dr["ztok"][t0:t0 + 128, :]), "dm_chm", 16)

        @block.tensor
        def _(t, P=P):
            t.wait_ge(p.sem("dm_chm"), p.tok("dm_chm"))
            if P == 0:
                t.wait_ge(p.sem("v"), p.tok("v"))
            if P > 0:
                t.wait_ge(p.sem("v"), tokd[("ttevac", P - 1)])
                t.wait_ge(p.sem("v"), tokd[("tokfree", P - 1)])
            for j in range(8):
                ins = t.transpose(ptb[:, j * 128:(j + 1) * 128], chm[:, j, :], identb[:, :])
            p.inc(ins, "t", 1)

        @block.vector
        def _(v, P=P):
            v.wait_ge(p.sem("t"), p.tok("t"))
            ins = v.tensor_copy(tokt[:, :, :].rearrange("pp a b -> pp (a b)"), ptb[:, 0:1024])
            tokd[("trdone", P)] = p.inc(ins, "v", 1)

        @block.scalar
        def _(sc, P=P):
            sc.wait_ge(p.sem("v"), p.tok("v"))
            sc.memzero(lsum[:, 0:4])
            for j in range(4):
                sc.activation(sqs[:, :], tokt[:, j, :], AF.Square, accum_out=lsum[:, j:j + 1])
            p.inc(sc.activation(lsum[:, 4:8], lsum[:, 0:4], AF.Ln, bias=p.kc(1e-6)), "s", 1)

        @block.vector
        def _(v, P=P):
            v.wait_ge(p.sem("s"), p.tok("s"))
            v.tensor_copy(cols[:, 0:2], lsum[:, 6:7].broadcast_to([128, 2]))
            v.tensor_copy(cols[:, 2:4], lsum[:, 7:8].broadcast_to([128, 2]))
            v.tensor_copy(scol[:, 8:10], lsum[:, 4:5].broadcast_to([128, 2]))
            v.tensor_copy(scol[:, 10:12], lsum[:, 5:6].broadcast_to([128, 2]))
            v.scalar_tensor_tensor(cols[:, 4:8], cols[:, 0:4], 0.5, gc4[:, P, :], OP.mult, OP.add)
            v.scalar_tensor_tensor(cols[:, 12:16], scol[:, 8:12], -0.5, gc4[:, P, :], OP.mult, OP.add)
            v.scalar_tensor_tensor(ecols[:, 0:4], cols[:, 0:4], -0.5, gc4[:, P, :], OP.mult, OP.add)
            v.tensor_copy(cols[:, 20:24], cols[:, 4:8])
            v.scalar_tensor_tensor(ecols[:, 4:8], cols[:, 0:4], -0.5, glh4[:, P, :], OP.mult, OP.add)
            v.tensor_scalar(cols[:, 12:16], cols[:, 12:16], LNDKH, None, OP.add)
            v.tensor_tensor(ecols[:, 4:8], ecols[:, 4:8], gc4[:, P, :], OP.subtract)
            p.inc(v.tensor_scalar(ecols[:, 8:12], gb4[:, P, 4:8], -1.0, None, OP.mult), "v", 1)

        @block.scalar
        def _(sc, P=P):
            sc.wait_ge(p.sem("v"), p.tok("v"))
            sc.activation(scol[:, 0:8], ecols[:, 0:8], AF.Exp)
            sc.activation(ecols[:, 8:12], ecols[:, 8:12], AF.Exp)
            sc.activation(ecols[:, 8:12], ecols[:, 8:12], AF.Ln, bias=p.kc(1.0))
            p.inc(sc.activation(scol[:, 8:12], cols[:, 12:16], AF.Exp), "s", 1)

        @block.vector
        def _(v, P=P):
            v.wait_ge(p.sem("s"), p.tok("s"))
            v.tensor_tensor(scol[:, 0:4], scol[:, 0:4], beta4[:, P, :], OP.mult)
            v.scalar_tensor_tensor(cols[:, 8:12], cols[:, 0:4], -0.5, gc4[:, P, :], OP.mult, OP.add)
            v.tensor_scalar(cols[:, 16:20], cols[:, 0:4], -0.5, None, OP.mult)
            v.tensor_tensor(cols[:, 8:12], cols[:, 8:12], ecols[:, 8:12], OP.subtract)
            p.inc(v.tensor_copy(scol[:, 4:8], scol[:, 4:8]), "v", 1)

        @block.scalar
        def _(sc, P=P):
            sc.wait_ge(p.sem("v"), p.tok("v"))
            p.inc(sc.activation(cols[:, 16:20], cols[:, 16:20], AF.Exp), "s", 1)

        @block.vector
        def _(v, P=P):
            v.wait_ge(p.sem("s"), p.tok("s"))
            v.tensor_tensor(cols[:, 16:20], cols[:, 16:20], beta4[:, P, :], OP.mult)
            p.inc(v.tensor_scalar(cols[:, 16:20], cols[:, 16:20], -1.0, None, OP.mult), "v", 1)

        for ci, colbase in ((0, 4), (1, 8), (2, 12)):
            @block.vector
            def _(v, P=P, ci=ci, colbase=colbase):
                v.wait_ge(p.sem("v"), p.tok("v"))
                if ci == 0:
                    v.wait_ge(p.sem("s"), p.tok("s"))
                for h in range(4):
                    ins = v.tensor_scalar(R4[:, h * 128:(h + 1) * 128], identf[:, :],
                                          cols[:, colbase + h:colbase + h + 1], None, OP.mult)
                p.inc(ins, "v", 1)

            @block.tensor
            def _(t, P=P, ci=ci):
                t.wait_ge(p.sem("v"), p.tok("v"))
                t.wait_ge(p.sem("s"), p.tok("s"))
                ins = t.matmul(pdec[:, 0:512], blockones[:, :], R4[:, :], start=True, stop=True)
                p.inc(ins, "t", 1)

            @block.vector
            def _(v, P=P, ci=ci):
                v.wait_ge(p.sem("t"), p.tok("t"))
                msk = (maskA, maskAT, maskQT)[ci]
                for h in range(4):
                    sub = gc4[:, P, h:h + 1] if ci == 0 else cols[:, 20 + h:21 + h]
                    ins = v.scalar_tensor_tensor(etmp[:, h * 128:(h + 1) * 128],
                                                 pdec[:, h * 128:(h + 1) * 128],
                                                 sub, msk[:, h * 128:(h + 1) * 128],
                                                 OP.subtract, OP.add)
                p.inc(ins, "v", 1)

            @block.scalar
            def _(sc, P=P, ci=ci):
                sc.wait_ge(p.sem("v"), p.tok("v"))
                scale = -1.0 if ci == 0 else 1.0
                p.inc(sc.activation(dexp[:, ci, :], etmp[:, :], AF.Exp, scale=scale), "s", 1)

        @block.tensor
        def _(t, P=P):
            t.wait_ge(p.sem("v"), tokd[("trdone", P)])
            if P > 0:
                t.wait_ge(p.sem("v"), tokd[("pAfree", P - 1)])
            for h in range(4):
                kc_ = chm[:, 2 + h // 2, :]
                t.matmul(pA[:, h * 128:(h + 1) * 128], kc_, kc_, start=True, stop=True)
            for h in range(4):
                ins = t.matmul(pQK[:, h * 128:(h + 1) * 128], chm[:, 2 + h // 2, :],
                               chm[:, h // 2, :], start=True, stop=True)
            p.inc(ins, "t", 1)

        @block.vector
        def _(v, P=P):
            v.wait_ge(p.sem("t"), p.tok("t"))
            v.wait_ge(p.sem("s"), p.tok("s"))
            if P > 0:
                v.wait_ge(p.sem("t"), tokd[("qtfree", P - 1)])
            v.tensor_tensor(etmp[:, :], pA[:, :], dexp[:, 0, :], OP.mult)
            v.scalar_tensor_tensor(AT4[:, :], pA[:, :], -1.0, dexp[:, 1, :], OP.mult, OP.mult)
            v.tensor_tensor(QT4[:, :], pQK[:, :], dexp[:, 2, :], OP.mult)
            for h in range(4):
                ins = v.tensor_scalar(A4[:, h * 128:(h + 1) * 128], etmp[:, h * 128:(h + 1) * 128],
                                      cols[:, 16 + h:17 + h], None, OP.mult)
            tokd[("pAfree", P)] = p.inc(ins, "v", 1)

        @block.vector
        def _(v, P=P):
            v.wait_ge(p.sem("v"), p.tok("v"))
            for h in range(4):
                v.tensor_scalar(vb4[:, h, :], tokt[:, 4 + h, :], beta4[:, P, h:h + 1], None, OP.mult)
            for h in range(4):
                v.tensor_scalar(kbg4[:, h, :], tokt[:, 2 + h // 2, :], scol[:, h:h + 1], None, OP.mult)
            for h in range(4):
                ins = v.tensor_scalar(kd4[:, h, :], tokt[:, 2 + h // 2, :], scol[:, 4 + h:5 + h], None, OP.mult)
            tokd[("tokfree", P)] = p.inc(ins, "v", 1)

        @block.vector
        def _(v, P=P):
            v.wait_ge(p.sem("v"), p.tok("v"))
            if P > 0:
                v.wait_ge(p.sem("s"), tokd[("uwevac", P - 1)])
            for h in range(4):
                v.tensor_copy(Xc[:, h, :], A4[:, h * 128:(h + 1) * 128])
            for h in range(4):
                v.tensor_copy(Yc[:, h, :], AT4[:, h * 128:(h + 1) * 128])
            for h in range(4):
                ins = v.tensor_tensor(Ttl[:, h, :], identb[:, :], A4[:, h * 128:(h + 1) * 128], OP.add)
            p.inc(ins, "v", 1)

        for it in range(5):
            @block.tensor
            def _(t, P=P, it=it):
                t.wait_ge(p.sem("v"), p.tok("v"))
                if it > 0:
                    t.wait_ge(p.sem("s"), p.tok("s"))
                for h in range(4):
                    t.matmul(pT[:, h * 128:(h + 1) * 128], Yc[:, h, :], Xc[:, h, :],
                             start=True, stop=True)
                for h in range(4):
                    ins = t.matmul(pT[:, 512 + h * 128:512 + (h + 1) * 128], Xc[:, h, :], Yc[:, h, :],
                                   start=True, stop=True)
                p.inc(ins, "t", 1)

            @block.scalar
            def _(sc, P=P, it=it):
                sc.wait_ge(p.sem("t"), p.tok("t"))
                if it < 4:
                    for h in range(4):
                        sc.copy(Xc[:, h, :], pT[:, h * 128:(h + 1) * 128])
                for h in range(4):
                    ins = sc.copy(Yc[:, h, :], pT[:, 512 + h * 128:512 + (h + 1) * 128])
                p.inc(ins, "s", 1)

            @block.tensor
            def _(t, P=P, it=it):
                t.wait_ge(p.sem("s"), p.tok("s"))
                for h in range(4):
                    ins = t.matmul(pT[:, h * 128:(h + 1) * 128], Yc[:, h, :], Ttl[:, h, :],
                                   start=True, stop=True)
                p.inc(ins, "t", 1)

            @block.vector
            def _(v, P=P, it=it):
                v.wait_ge(p.sem("t"), p.tok("t"))
                for h in range(4):
                    ins = v.tensor_tensor(Ttl[:, h, :], Ttl[:, h, :], pT[:, h * 128:(h + 1) * 128], OP.add)
                p.inc(ins, "v", 1)

        @block.tensor
        def _(t, P=P):
            t.wait_ge(p.sem("v"), p.tok("v"))
            for h in range(4):
                ins = t.transpose(ptb[:, h * 128:(h + 1) * 128], Ttl[:, h, :], identb[:, :])
            p.inc(ins, "t", 1)

        @block.vector
        def _(v, P=P):
            v.wait_ge(p.sem("t"), p.tok("t"))
            ins = v.tensor_copy(TtT[:, :, :].rearrange("pp a b -> pp (a b)"), ptb[:, 0:512])
            tokd[("ttevac", P)] = p.inc(ins, "v", 1)

        @block.tensor
        def _(t, P=P):
            t.wait_ge(p.sem("v"), p.tok("v"))
            for h in range(4):
                t.matmul(pT[:, h * 128:(h + 1) * 128], TtT[:, h, :], vb4[:, h, :],
                         start=True, stop=True)
            for h in range(4):
                ins = t.matmul(pT[:, 512 + h * 128:512 + (h + 1) * 128], kbg4[:, h, :], TtT[:, h, :],
                               start=True, stop=True)
            p.inc(ins, "t", 1)

        @block.scalar
        def _(sc, P=P):
            sc.wait_ge(p.sem("t"), p.tok("t"))
            for h in range(4):
                sc.copy(u4[:, h, :], pT[:, h * 128:(h + 1) * 128])
            for h in range(4):
                ins = sc.copy(wT4[:, h, :], pT[:, 512 + h * 128:512 + (h + 1) * 128])
            tokd[("uwevac", P)] = p.inc(ins, "s", 1)

        for half in range(2):
            lo, hi = half * 64, half * 64 + 64

            @block.tensor
            def _(t, P=P, half=half, lo=lo, hi=hi):
                t.wait_ge(p.sem("s"), p.tok("s"))
                t.wait_ge(p.sem("v"), p.tok("v"))
                for h in range(4):
                    t.matmul(psA2[lo:hi, h * 128:(h + 1) * 128], wT4[:, h, lo:hi], S4b[:, h, :],
                             start=True, stop=True, tile_position=(0, lo))
                for h in range(4):
                    ins = t.matmul(psB2[lo:hi, h * 128:(h + 1) * 128], chm[:, h // 2, lo:hi],
                                   S4b[:, h, :], start=True, stop=True, tile_position=(0, lo))
                tokd[("chmfree", P)] = p.inc(ins, "t", 1)

            @block.vector
            def _(v, P=P, half=half, lo=lo, hi=hi):
                v.wait_ge(p.sem("t"), p.tok("t"))
                for h in range(4):
                    ins = v.tensor_tensor(vnew4[lo:hi, h, :], u4[lo:hi, h, :],
                                          psA2[lo:hi, h * 128:(h + 1) * 128], OP.subtract)
                p.inc(ins, "v", 1)

            @block.tensor
            def _(t, P=P, half=half, lo=lo, hi=hi):
                t.wait_ge(p.sem("v"), p.tok("v"))
                for h in range(4):
                    t.matmul(psA2[lo:hi, h * 128:(h + 1) * 128],
                             QT4[lo:hi, h * 128 + lo:h * 128 + hi], vnew4[lo:hi, h, :],
                             start=True, stop=True, tile_position=(lo, lo))
                for h in range(4):
                    ins = t.matmul(pSS[:, h * 128:(h + 1) * 128], kd4[lo:hi, h, :], vnew4[lo:hi, h, :],
                                   start=True, stop=True, tile_position=(lo, 0))
                tokd[("qtfree", P)] = p.inc(ins, "t", 1)

            @block.vector
            def _(v, P=P, half=half, lo=lo, hi=hi):
                v.wait_ge(p.sem("t"), p.tok("t"))
                for h in range(4):
                    v.tensor_scalar(e14[lo:hi, h, :], psB2[lo:hi, h * 128:(h + 1) * 128],
                                    scol[lo:hi, 8 + h:9 + h], None, OP.mult)
                for h in range(4):
                    v.tensor_tensor(otile[lo:hi, h, :], e14[lo:hi, h, :],
                                    psA2[lo:hi, h * 128:(h + 1) * 128], OP.add)
                for h in range(4):
                    v.scalar_tensor_tensor(S4[:, h, :], S4[:, h, :],
                                           egl4[:, P, half * 4 + h:half * 4 + h + 1],
                                           pSS[:, h * 128:(h + 1) * 128], OP.mult, OP.add)
                for h in range(4):
                    ins = v.tensor_copy(S4b[:, h, :], S4[:, h, :])
                p.inc(ins, "v", 1)

        @block.scalar
        def _(sc, P=P):
            sc.wait_ge(p.sem("v"), p.tok("v"))
            sc.wait_ge(p.sem("dm_chm"), p.tok("dm_chm"))
            sc.memzero(rsum[:, 0:4])
            for h in range(4):
                sc.activation(sqs[:, :], otile[:, h, :], AF.Square, accum_out=rsum[:, h:h + 1])
            sc.activation(rsum[:, 4:8], rsum[:, 0:4], AF.Ln, scale=1.0 / DV, bias=p.kc(1e-6))
            sc.activation(rsum[:, 8:12], rsum[:, 4:8], AF.Exp, scale=-0.5)
            p.inc(sc.activation(gat[:, :], zt3[:, :], AF.Silu), "s", 1)

        @block.vector
        def _(v, P=P):
            v.wait_ge(p.sem("s"), p.tok("s"))
            for h in range(4):
                v.tensor_scalar(e14[:, h, :], otile[:, h, :], rsum[:, 8 + h:9 + h], None, OP.mult)
            for h in range(4):
                v.tensor_tensor(e14[:, h, :], e14[:, h, :], normbc[:, :], OP.mult)
            for h in range(4):
                ins = v.tensor_tensor(slabt[:, h, :], e14[:, h, :], gat[:, h * 128:(h + 1) * 128],
                                      OP.mult)
            tokd[("ztfree", P)] = p.inc(ins, "v", 1)

        @block.tensor
        def _(t, P=P):
            t.wait_ge(p.sem("v"), p.tok("v"))
            for h in range(4):
                ins = t.transpose(ptb[:, 512 + h * 128:512 + (h + 1) * 128], slabt[:, h, :],
                                  identb[:, :])
            p.inc(ins, "t", 1)

        @block.vector
        def _(v, P=P):
            v.wait_ge(p.sem("t"), p.tok("t"))
            if P > 0:
                v.wait_ge(p.sem("st_slab"), tokd[("slabst", P - 1)])
            ins = v.tensor_copy(gat[:, 0:512], ptb[:, 512:1024])
            p.inc(ins, "v", 1)

        @block.gpsimd
        def _(g, P=P, t0=t0):
            g.wait_ge(p.sem("v"), p.tok("v"))
            st = P // 8
            c0 = (P % 8) * 128
            for h in range(4):
                ins = g.dma_start(out=dr[f"slabi{st}"][h * 128:(h + 1) * 128, c0:c0 + 128],
                                  in_=gat[:, h * 128:(h + 1) * 128])
                p.inc(ins, "st_slab", 16)
            tokd[("slabst", P)] = p.tok("st_slab")
            if P % 8 == 7:
                g.wait_ge(p.sem("st_slab"), p.tok("st_slab"))
                p.inc(g.collective_compute(
                    "AllGather", OP.bypass, replica_groups=[list(range(NCORES))],
                    ins=[dr[f"slabi{st}"][:, :]], outs=[dr[f"slaba{st}"][:, :, :]]), f"cc_sl{st}", 1)


def ln_stats_own(p, block, src_tiles, nt, rowbase, kcol, bf=False):
    dr = p.dram
    sqt = p.sb["sq4"]
    onesl = p.sb["ones2"] if not bf else p.sb["onesb"]
    prr = p.ps["prr"]
    ps0, ps1 = p.ps["ps0"], p.ps["ps1"]
    srow = p.sb["srow"]
    mubc = p.sb["mubc"]
    rsbc = p.sb["rsbc"]

    for qq in range(nt // 256):
        @block.vector
        def _(v, qq=qq):
            v.wait_ge(p.sem("t"), p.tok("t"))
            v.wait_ge(p.sem("v"), p.tok("v"))
            for dt in range(16):
                ins = v.tensor_tensor(sqt[:, dt, :], src_tiles[:, dt, qq * 256:(qq + 1) * 256],
                                      src_tiles[:, dt, qq * 256:(qq + 1) * 256], OP.mult)
            p.inc(ins, "v", 1)

        @block.tensor
        def _(t, qq=qq):
            t.wait_ge(p.sem("v"), p.tok("v"))
            for dt in range(16):
                t.matmul(prr[0:1, qq * 256:(qq + 1) * 256], onesl[:, 0:1],
                         src_tiles[:, dt, qq * 256:(qq + 1) * 256],
                         start=(dt == 0), stop=(dt == 15))
            psh = (ps0, ps1)[qq // 2]
            col = (qq % 2) * 256
            for dt in range(16):
                ins = t.matmul(psh[0:1, col:col + 256], p.sb["ones2"][:, 0:1],
                               sqt[:, dt, :], start=(dt == 0), stop=(dt == 15))
            p.inc(ins, "t", 1)

    @block.vector
    def _(v):
        v.wait_ge(p.sem("t"), p.tok("t"))
        ins = v.tensor_scalar(srow[0:1, 0:nt], prr[0:1, 0:nt], 1.0 / D, None, OP.mult)
        v.wait_ge(p.sem("v"), p.inc(ins, "v", 1))
        ins = v.tensor_tensor(rsbc[0:1, 0:nt], srow[0:1, 0:nt], srow[0:1, 0:nt], OP.mult)
        v.wait_ge(p.sem("v"), p.inc(ins, "v", 1))
        v.tensor_scalar(mubc[0:1, 0:512], ps0[0:1, 0:512], 1.0 / D, None, OP.mult)
        ins = v.tensor_scalar(mubc[0:1, 512:1024], ps1[0:1, 0:512], 1.0 / D, None, OP.mult)
        v.wait_ge(p.sem("v"), p.inc(ins, "v", 1))
        ins = v.tensor_tensor(mubc[0:1, 0:nt], mubc[0:1, 0:nt], rsbc[0:1, 0:nt], OP.subtract)
        p.inc(ins, "v", 1)

    @block.scalar
    def _(sc):
        sc.wait_ge(p.sem("v"), p.tok("v"))
        sc.activation(rsbc[0:1, 0:nt], mubc[0:1, 0:nt], AF.Ln, bias=kcol)
        p.inc(sc.activation(mubc[0:1, 0:nt], rsbc[0:1, 0:nt], AF.Exp, scale=-0.5), "s", 1)

    @block.vector
    def _(v):
        v.wait_ge(p.sem("s"), p.tok("s"))
        ins = v.tensor_tensor(srow[0:1, 0:nt], srow[0:1, 0:nt], mubc[0:1, 0:nt], OP.mult)
        v.wait_ge(p.sem("v"), p.inc(ins, "v", 1))
        ins = v.tensor_scalar(srow[0:1, 0:nt], srow[0:1, 0:nt], -1.0, None, OP.mult)
        p.inc(ins, "v", 1)

    @block.gpsimd
    def _(g):
        g.wait_ge(p.sem("v"), p.tok("v"))
        g.wait_ge(p.sem("dm_lnbc"), p.tok("dm_lnbc"))
        p.inc(g.dma_start(out=dr["rows0"][rowbase:rowbase + 1, 0:nt], in_=srow[0:1, 0:nt]), "st_lnr", 16)
        p.inc(g.dma_start(out=dr["rows0"][rowbase + 1:rowbase + 2, 0:nt], in_=mubc[0:1, 0:nt]), "st_lnr", 16)


def ln_apply(p, block, src_tiles, dst_tiles, nt, rowbase, modslice, extra_gb=None):
    dr = p.dram
    mubc = p.sb["mubc"]
    rsbc = p.sb["rsbc"]
    modc = p.sb["modc"]

    @block.sync
    def _(s):
        s.wait_ge(p.sem("st_lnr"), p.tok("st_lnr"))
        s.wait_ge(p.sem("v"), p.tok("v"))
        p.inc(s.dma_start(out=mubc[:, 0:nt], in_=dr["rows0"][rowbase:rowbase + 1, 0:nt]
                          .broadcast_to([128, nt])), "dm_lnbc", 16)
        p.inc(s.dma_start(out=rsbc[:, 0:nt], in_=dr["rows0"][rowbase + 1:rowbase + 2, 0:nt]
                          .broadcast_to([128, nt])), "dm_lnbc", 16)

    @block.vector
    def _(v):
        v.wait_ge(p.sem("dm_lnbc"), p.tok("dm_lnbc"))
        for dt in range(16):
            ins = v.tensor_tensor(dst_tiles[:, dt, :], src_tiles[:, dt, :], rsbc[:, 0:nt], OP.mult)
        p.inc(ins, "v", 1)
        for dt in range(16):
            ins = v.tensor_tensor(dst_tiles[:, dt, :], dst_tiles[:, dt, :], mubc[:, 0:nt], OP.add)
        v.wait_ge(p.sem("v"), p.inc(ins, "v", 1))
        if extra_gb is None:
            L, c1, c2 = modslice
            for dt in range(16):
                ins = v.tensor_scalar(dst_tiles[:, dt, :], dst_tiles[:, dt, :],
                                      modc[:, L, dt, c1:c1 + 1], modc[:, L, dt, c2:c2 + 1],
                                      OP.mult, OP.add)
        else:
            gbc, bbc = extra_gb
            for dt in range(16):
                ins = v.tensor_scalar(dst_tiles[:, dt, :], dst_tiles[:, dt, :],
                                      gbc[:, dt:dt + 1], bbc[:, dt:dt + 1],
                                      OP.mult, OP.add)
        p.inc(ins, "v", 1)


def ph4(p, block):
    dr = p.dram
    modc = p.sb["modc"]
    kcol = p.sb["kcol"]
    identb = p.sb["identb_s"]
    p.free("chm", "tokt", "lsum", "sqs", "cols", "ecols", "scol", "R4", "dexp", "etmp",
           "A4", "AT4", "QT4", "Ttl", "TtT", "Xc", "Yc", "u4", "wT4", "vb4", "kbg4",
           "kd4", "S4", "S4b", "vnew4", "e14", "otile", "zt3", "gat", "slabt", "rsum",
           "gb4", "beta4", "gc4", "egl4", "gl4", "glh4", "normbc", "maskA_s", "maskAT_s",
           "maskQT_s", "glselA_s", "glselB_s", "glselH_s", "cum_l", "blockones_s")
    p.free("ptb", "pA", "pQK", "pT", "psA2", "psB2", "pSS")

    xT = p.sbuf("xTr", [128, 16, TOK], F32)
    sq4 = p.sbuf("sq4", [128, 16, 256], F32)
    ones2 = p.sbuf("ones2", [128, 2], F32)
    onesb = p.sbuf("onesb", [128, 2], BF16)
    srow = p.sbuf("srow", [6, TOK], F32)
    mubc = p.sbuf("mubc", [128, TOK], F32)
    rsbc = p.sbuf("rsbc", [128, TOK], F32)
    slabix = p.sbuf("slabix", [128, 32], mybir.dt.int32)
    act4 = p.sbuf("act4", [128, 32, 256], BF16)
    w_t = p.sbuf("wtile0", [128, 32, 128], BF16)
    wtile = [w_t, w_t]
    p.sbuf("gbbc", [128, 32], F32)
    xn = p.sbuf("xn", [128, 16, TOK], BF16)
    h2 = p.sbuf("h2", [128, 16, TOK], BF16)
    loraA = p.sbuf("loraA", [128, 16, 128], BF16)
    loraB = p.sbuf("loraB", [128, 16, 128], BF16)
    rwt = p.sbuf("rwt", [128, 16, 8], BF16)
    rbbc = p.sbuf("rbbc", [128, 8], F32)
    gcols = p.sbuf("gcols", [128, 16], F32)
    midg = p.sbuf("midg", [128, 128], BF16)
    midT = p.sbuf("midT", [128, 128], BF16)
    bb2 = p.sbuf("bb2", [128, 16, 2], F32)
    ps0 = p.psum("ps0", [128, 512], F32)
    ps1 = p.psum("ps1", [128, 512], F32)
    prr = p.psum("prr", [64, 1024], F32)
    pmid = p.psum("pmid", [128, 128], F32)
    plog = p.psum("plog", [128, 16], F32)
    ptr4 = p.psum("ptr4", [128, 128], BF16)

    @block.sync
    def _(s):
        s.wait_ge(p.sem("v"), p.tok("v"))
        s.wait_ge(p.sem("st_slab"), p.tok("st_slab"))
        for dt in range(16):
            p.inc(s.dma_start(out=xT[:, dt, :], in_=dr["xT_my"][dt * 128:(dt + 1) * 128, :]), "dm_xres", 16)
        p.inc(s.dma_start(out=slabix[:, :], in_=dr["slabidx"][:, :]), "dm_xres", 16)

    @block.gpsimd
    def _(g):
        g.wait_ge(p.sem("v"), p.tok("v"))
        g.wait_ge(p.sem("st_slab"), p.tok("st_slab"))
        for dt in range(16):
            p.inc(g.dma_start(out=loraA[:, dt, :], in_=dr["lora_AR"][dt * 128:(dt + 1) * 128, :]), "dm_ffnw", 16)
            p.inc(g.dma_start(out=rwt[:, dt, :], in_=dr["router_wT"][dt * 128:(dt + 1) * 128, :]), "dm_ffnw", 16)
        p.inc(g.dma_start(out=loraB[:, :, :].rearrange("pp a b -> pp (a b)"),
                          in_=dr["lora_BS"][:, :]), "dm_ffnw", 16)
        p.inc(g.dma_start(out=rbbc[:, :], in_=dr["router_b"].broadcast_to([128, 8])), "dm_ffnw", 16)
        for dt in range(16):
            p.inc(g.dma_start(out=bb2[:, dt, 0:1],
                              in_=dr["base_b"].rearrange("o (dt pp) -> (o pp) dt", pp=128)[:, dt:dt + 1]),
                  "dm_ffnw", 16)

    @block.vector
    def _(v):
        v.wait_ge(p.sem("dm_ffnw"), p.tok("dm_ffnw"))
        v.memset(ones2[:, :], 1.0)
        v.memset(onesb[:, :], 1.0)
        for dt in range(16):
            ins = v.tensor_scalar(bb2[:, dt, 1:2], bb2[:, dt, 0:1], modc[:, 0, dt, 5:6], None, OP.mult)
        p.inc(ins, "v", 1)

    tokp = {}
    for tg in range(4):
        @block.gpsimd
        def _(g, tg=tg):
            g.wait_ge(p.sem("dm_xres"), p.tok("dm_xres"))
            for oc in range(tg * 2, tg * 2 + 2):
                g.wait_ge(p.sem(f"cc_sl{oc}"), 1)
            if tg > 0:
                g.wait_ge(p.sem("t"), tokp[("act", tg - 1)])
            for ct in range(32):
                for j, oc in enumerate(range(tg * 2, tg * 2 + 2)):
                    strip = dr[f"slaba{oc}"].rearrange("a b c -> (a b c)") \
                        .rearrange("(r e) -> r e", e=128)
                    ins = g.indirect_dma_start(
                        act4[:, ct, j * 128:(j + 1) * 128], None,
                        strip[:, :],
                        bass.IndirectOffsetOnAxis(slabix[:, ct:ct + 1], 0))
                    p.inc(ins, "dm_act", 16)

        for dt in range(16):
            @block.gpsimd
            def _(g, tg=tg, dt=dt):
                if (tg, dt) not in ((0, 0),):
                    g.wait_ge(p.sem("t"), p.tok("t"))
                p.inc(g.dma_start(
                    out=wtile[dt % 2][:, :, :],
                    in_=dr["w_out"].rearrange("(ct pp) d -> pp ct d", pp=128)[:, :, dt * 128:(dt + 1) * 128]),
                    "dm_wo%d" % (dt % 2), 16)

            @block.tensor
            def _(t, tg=tg, dt=dt):
                t.wait_ge(p.sem("dm_act"), p.tok("dm_act"))
                t.wait_ge(p.sem("dm_wo%d" % (dt % 2)), p.tok("dm_wo%d" % (dt % 2)))
                if tg == 0 and dt == 0:
                    t.wait_ge(p.sem("v"), p.tok("v"))
                if ("ps", dt % 2) in tokp:
                    t.wait_ge(p.sem("v"), tokp[("ps", dt % 2)])
                ps = (ps0, ps1)[dt % 2]
                for ct in range(32):
                    ins = t.matmul(ps[:, 0:256], wtile[dt % 2][:, ct, :], act4[:, ct, :],
                                   start=(ct == 0), stop=(ct == 31))
                tokp[("act", tg)] = p.inc(ins, "t", 1)

            @block.vector
            def _(v, tg=tg, dt=dt):
                v.wait_ge(p.sem("t"), p.tok("t"))
                ps = (ps0, ps1)[dt % 2]
                ins = v.scalar_tensor_tensor(xT[:, dt, tg * 256:(tg + 1) * 256], ps[:, 0:256],
                                             modc[:, 0, dt, 2:3], xT[:, dt, tg * 256:(tg + 1) * 256],
                                             OP.mult, OP.add)
                tokp[("ps", dt % 2)] = p.inc(ins, "v", 1)

    ln_stats_own(p, block, xT, TOK, 6, kcol[0:1, KCVALS[1e-5]:KCVALS[1e-5] + 1])
    ln_apply(p, block, xT, h2, TOK, 6, (0, 4, 3))

    for oc in range(8):
        @block.tensor
        def _(t, oc=oc):
            t.wait_ge(p.sem("v"), p.tok("v"))
            if oc > 0:
                t.wait_ge(p.sem("v"), tokp[("mid", oc - 1)])
            for dt in range(16):
                t.matmul(pmid[:, :], h2[:, dt, oc * 128:(oc + 1) * 128], loraA[:, dt, :],
                         start=(dt == 0), stop=(dt == 15))
            for dt in range(16):
                ins = t.matmul(plog[:, 0:8], h2[:, dt, oc * 128:(oc + 1) * 128], rwt[:, dt, :],
                               start=(dt == 0), stop=(dt == 15))
            p.inc(ins, "t", 1)

        @block.vector
        def _(v, oc=oc):
            v.wait_ge(p.sem("t"), p.tok("t"))
            v.tensor_tensor(gcols[:, 8:16], plog[:, 0:8], rbbc[:, :], OP.add)
            v.wait_ge(p.sem("v"), p.inc(v.memset(gcols[:, 0:1], 0.0), "v", 1))
            ins = v.reduce_max(gcols[:, 0:1], gcols[:, 8:16], axis=AX.X)
            v.wait_ge(p.sem("v"), p.inc(ins, "v", 1))
            v.tensor_scalar(gcols[:, 2:3], gcols[:, 0:1], -1.0, None, OP.mult)
            ins = v.tensor_scalar(midg[:, 0:8], gcols[:, 8:16], gcols[:, 0:1], None, OP.is_ge)
            v.wait_ge(p.sem("v"), p.inc(ins, "v", 1))
            ins = v.scalar_tensor_tensor(gcols[:, 8:16], midg[:, 0:8], NEG, gcols[:, 8:16], OP.mult, OP.add)
            v.wait_ge(p.sem("v"), p.inc(ins, "v", 1))
            ins = v.reduce_max(gcols[:, 1:2], gcols[:, 8:16], axis=AX.X)
            v.wait_ge(p.sem("v"), p.inc(ins, "v", 1))
            v.tensor_scalar(gcols[:, 3:4], gcols[:, 2:3], gcols[:, 1:2], None, OP.add)
            ins = v.tensor_copy(gcols[:, 5:6], gcols[:, 1:2])
            v.wait_ge(p.sem("v"), p.inc(ins, "v", 1))
            ins = v.tensor_scalar(gcols[:, 4:5], gcols[:, 3:4], -1.0, None, OP.mult)
            p.inc(ins, "v", 1)

        @block.scalar
        def _(sc, oc=oc):
            sc.wait_ge(p.sem("v"), p.tok("v"))
            sc.activation(gcols[:, 6:7], gcols[:, 4:5], AF.Sigmoid)
            p.inc(sc.activation(gcols[:, 7:8], gcols[:, 3:4], AF.Sigmoid), "s", 1)

        @block.vector
        def _(v, oc=oc):
            v.wait_ge(p.sem("s"), p.tok("s"))
            v.tensor_tensor(gcols[:, 8:16], plog[:, 0:8], rbbc[:, :], OP.add)
            v.tensor_scalar(gcols[:, 6:7], gcols[:, 6:7], 2.0, None, OP.mult)
            ins = v.tensor_scalar(gcols[:, 7:8], gcols[:, 7:8], 2.0, None, OP.mult)
            v.wait_ge(p.sem("v"), p.inc(ins, "v", 1))
            v.tensor_scalar(midg[:, 0:8], gcols[:, 8:16], gcols[:, 0:1], None, OP.is_ge)
            ins = v.tensor_scalar(midg[:, 8:16], gcols[:, 8:16], gcols[:, 1:2], None, OP.is_ge)
            v.wait_ge(p.sem("v"), p.inc(ins, "v", 1))
            v.tensor_tensor(midg[:, 8:16], midg[:, 8:16], midg[:, 0:8], OP.subtract)
            v.tensor_scalar(gcols[:, 8:16], midg[:, 0:8], gcols[:, 6:7], None, OP.mult)
            v.wait_ge(p.sem("v"), p.inc(v.memset(gcols[:, 2:3], 0.0), "v", 1))
            ins = v.scalar_tensor_tensor(gcols[:, 8:16], midg[:, 8:16], gcols[:, 7:8],
                                         gcols[:, 8:16], OP.mult, OP.add)
            v.wait_ge(p.sem("v"), p.inc(ins, "v", 1))
            for e in range(8):
                ins = v.tensor_scalar(midg[:, e * 16:(e + 1) * 16], pmid[:, e * 16:(e + 1) * 16],
                                      gcols[:, 8 + e:9 + e], None, OP.mult)
            tokp[("mid", oc)] = p.inc(ins, "v", 1)

        @block.tensor
        def _(t, oc=oc):
            t.wait_ge(p.sem("v"), p.tok("v"))
            ins = t.transpose(ptr4[:, :], midg[:, 0:128], identb[:, :])
            p.inc(ins, "t", 1)

        @block.vector
        def _(v, oc=oc):
            v.wait_ge(p.sem("t"), p.tok("t"))
            ins = v.tensor_copy(midT[:, :], ptr4[:, :])
            tokp[("midT", oc)] = p.inc(ins, "v", 1)

        for dto in range(16):
            @block.gpsimd
            def _(g, oc=oc, dto=dto):
                g.wait_ge(p.sem("t"), p.tok("t"))
                p.inc(g.dma_start(
                    out=wtile[dto % 2][:, 0:16, :],
                    in_=dr["base_wT"].rearrange("(dt pp) d -> pp dt d", pp=128)[:, :, dto * 128:(dto + 1) * 128]),
                    "dm_wo%d" % (dto % 2), 16)

            @block.tensor
            def _(t, oc=oc, dto=dto):
                t.wait_ge(p.sem("dm_wo%d" % (dto % 2)), p.tok("dm_wo%d" % (dto % 2)))
                t.wait_ge(p.sem("v"), tokp[("midT", oc)])
                if ("fps", dto % 2) in tokp:
                    t.wait_ge(p.sem("v"), tokp[("fps", dto % 2)])
                ps = (ps0, ps1)[dto % 2]
                for dt in range(16):
                    t.matmul(ps[:, 0:128], wtile[dto % 2][:, dt, :], h2[:, dt, oc * 128:(oc + 1) * 128],
                             start=(dt == 0), stop=False)
                ins = t.matmul(ps[:, 0:128], loraB[:, dto, :], midT[:, :], start=False, stop=True)
                p.inc(ins, "t", 1)

            @block.vector
            def _(v, oc=oc, dto=dto):
                v.wait_ge(p.sem("t"), p.tok("t"))
                ps = (ps0, ps1)[dto % 2]
                v.scalar_tensor_tensor(xT[:, dto, oc * 128:(oc + 1) * 128], ps[:, 0:128],
                                       modc[:, 0, dto, 5:6], xT[:, dto, oc * 128:(oc + 1) * 128],
                                       OP.mult, OP.add)
                ins = v.tensor_scalar(xT[:, dto, oc * 128:(oc + 1) * 128],
                                      xT[:, dto, oc * 128:(oc + 1) * 128],
                                      bb2[:, dto, 1:2], None, OP.add)
                tokp[("fps", dto % 2)] = p.inc(ins, "v", 1)


def ph5(p, block):
    dr = p.dram
    modc = p.sb["modc"]
    kcol = p.sb["kcol"]
    identb = p.sb["identb_s"]
    xT = p.sb["xTr"]
    h2 = p.sb["h2"]
    srow = p.sb["srow"]
    mubc = p.sb["mubc"]
    rsbc = p.sb["rsbc"]
    w_t = p.sb["wtile0"]
    act4 = p.sb["act4"]
    ps0, ps1, prr = p.ps["ps0"], p.ps["ps1"], p.ps["prr"]
    p.free("bb2", "midT", "midg", "gcols", "rbbc", "rwt", "loraB", "loraA")
    p.free("ptr4", "plog", "pmid")
    xn = p.sb["xn"]
    gbbc = p.sb["gbbc"]
    pav = p.psum("pav", [128, 1024], F32)
    ptrb = p.psum("ptrb", [128, 512], BF16)
    tokq = {}

    @block.gpsimd
    def _(g):
        g.wait_ge(p.sem("v"), p.tok("v"))
        for dt in range(16):
            p.inc(g.dma_start(out=gbbc[:, dt:dt + 1],
                              in_=dr["gamma1"].rearrange("o (dt pp) -> (o pp) dt", pp=128)[:, dt:dt + 1]),
                  "dm_gb1", 16)
            p.inc(g.dma_start(out=gbbc[:, 16 + dt:17 + dt],
                              in_=dr["beta1"].rearrange("o (dt pp) -> (o pp) dt", pp=128)[:, dt:dt + 1]),
                  "dm_gb1", 16)
    ln_stats_own(p, block, xT, TOK, 8, kcol[0:1, KCVALS[1e-5]:KCVALS[1e-5] + 1])
    ln_apply(p, block, xT, h2, TOK, 8, (1, 1, 0))
    ln_stats_own(p, block, h2, TOK, 10, kcol[0:1, KCVALS[1e-5]:KCVALS[1e-5] + 1], bf=True)

    @block.vector
    def _(v):
        v.wait_ge(p.sem("dm_gb1"), p.tok("dm_gb1"))
        p.inc(v.tensor_copy(gbbc[:, 0:1], gbbc[:, 0:1]), "v", 1)
    ln_apply(p, block, h2, xn, TOK, 10, None, extra_gb=(gbbc[:, 0:16], gbbc[:, 16:32]))
    p.free("h2")
    macc = p.sbuf("macc", [128, 16], F32)
    vstg = p.sbuf("vstg", [128, 512], BF16)
    qkstg = p.sbuf("qkstg", [128, 2, 512], BF16)
    amask = p.sbuf("amask", [128, 1024], F32)

    @block.gpsimd
    def _(g):
        g.wait_ge(p.sem("v"), p.tok("v"))
        g.wait_ge(p.sem("t"), p.tok("t"))
        p.inc(g.dma_start(out=amask[:, :], in_=dr["attn_mask"][:, :]), "dm_gb1", 16)

    for pc in range(2):
        for dto in range(16):
            @block.gpsimd
            def _(g, pc=pc, dto=dto):
                g.wait_ge(p.sem("t"), p.tok("t"))
                p.inc(g.dma_start(
                    out=w_t[:, 0:16, :],
                    in_=dr["qkv1"].rearrange("(dt pp) d -> pp dt d", pp=128)
                    [:, :, pc * 2048 + dto * 128:pc * 2048 + (dto + 1) * 128]), "dm_wo0", 16)

            @block.tensor
            def _(t, pc=pc, dto=dto):
                t.wait_ge(p.sem("dm_wo0"), p.tok("dm_wo0"))
                t.wait_ge(p.sem("v"), p.tok("v"))
                if ("spill",) in tokq:
                    t.wait_ge(p.sem("st_kv"), tokq[("spill",)])
                for tg in range(2):
                    for dt in range(16):
                        ins = t.matmul((ps0, ps1)[tg][:, :], w_t[:, dt, :],
                                       xn[:, dt, tg * 512:(tg + 1) * 512],
                                       start=(dt == 0), stop=(dt == 15))
                p.inc(ins, "t", 1)

            @block.vector
            def _(v, pc=pc, dto=dto):
                v.wait_ge(p.sem("t"), p.tok("t"))
                if ("qkst",) in tokq:
                    v.wait_ge(p.sem("st_kv"), tokq[("qkst",)])
                sc_ = (2048.0 ** -0.5) if pc == 0 else 1.0
                for tg in range(2):
                    ins = v.tensor_scalar(qkstg[:, tg, :], (ps0, ps1)[tg][:, :],
                                          sc_, None, OP.mult)
                p.inc(ins, "v", 1)

            @block.sync
            def _(g, pc=pc, dto=dto):
                g.wait_ge(p.sem("v"), p.tok("v"))
                for oc in range(8):
                    tg, col = oc // 4, (oc % 4) * 128
                    if pc == 0:
                        ins = g.dma_start(out=dr["qTh"][dto * 128:(dto + 1) * 128,
                                                        oc * 128:(oc + 1) * 128],
                                          in_=qkstg[:, tg, col:col + 128])
                    else:
                        sg, half = oc // 2, oc % 2
                        ins = g.dma_start(out=dr[f"kti{sg}"][dto * 128:(dto + 1) * 128,
                                                             half * 128:(half + 1) * 128],
                                          in_=qkstg[:, tg, col:col + 128])
                    p.inc(ins, "st_kv", 16)
                tokq[("qkst",)] = p.tok("st_kv")
        tokq[("spill",)] = p.tok("st_kv")

    wv = p.sbuf("wv", [128, 16, 512], BF16)
    for dvg in range(4):
        @block.gpsimd
        def _(g, dvg=dvg):
            g.wait_ge(p.sem("t"), p.tok("t"))
            for dt in range(16):
                p.inc(g.dma_start(out=wv[:, dt, :],
                                  in_=dr["qkv1"][dt * 128:(dt + 1) * 128,
                                                 4096 + dvg * 512:4096 + dvg * 512 + 512]),
                      "dm_wv", 16)

        for oc in range(8):
            @block.tensor
            def _(t, dvg=dvg, oc=oc):
                t.wait_ge(p.sem("dm_wv"), p.tok("dm_wv"))
                if ("vps",) in tokq:
                    t.wait_ge(p.sem("st_kv"), tokq[("vps",)])
                for dt in range(16):
                    ins = t.matmul(ps0[:, :], xn[:, dt, oc * 128:(oc + 1) * 128], wv[:, dt, :],
                                   start=(dt == 0), stop=(dt == 15))
                p.inc(ins, "t", 1)

            @block.vector
            def _(v, dvg=dvg, oc=oc):
                v.wait_ge(p.sem("t"), p.tok("t"))
                p.inc(v.tensor_copy(vstg[:, :], ps0[:, :]), "v", 1)

            @block.sync
            def _(g, dvg=dvg, oc=oc):
                g.wait_ge(p.sem("v"), p.tok("v"))
                sg, half = oc // 2, oc % 2
                ins = g.dma_start(out=dr[f"vti{sg}"][half * 128:(half + 1) * 128,
                                                     dvg * 512:(dvg + 1) * 512],
                                  in_=vstg[:, :])
                tokq[("vps",)] = p.inc(ins, "st_kv", 16)

    @block.gpsimd
    def _(g):
        g.wait_ge(p.sem("st_kv"), p.tok("st_kv"))
        for sg in range(4):
            p.inc(g.collective_compute(
                "AllGather", OP.bypass, replica_groups=[list(range(NCORES))],
                ins=[dr[f"kti{sg}"][:, :]], outs=[dr[f"kta{sg}"][:, :, :]]), f"cc_kv{sg}", 1)
            p.inc(g.collective_compute(
                "AllGather", OP.bypass, replica_groups=[list(range(NCORES))],
                ins=[dr[f"vti{sg}"][:, :]], outs=[dr[f"vta{sg}"][:, :, :]]), f"cc_kv{sg}", 1)

    p.free("wv")
    accf = p.sbuf("accf", [128, 2048], F32)
    ptt = p.sbuf("ptt", [128, 512], BF16)
    scf = p.sbuf("scf", [128, 512], F32)
    qsl = p.sbuf("qsl", [128, 16, 128], BF16)
    ktile = p.sbuf("ktile", [128, 16, 512], BF16)
    vtile = p.sbuf("vtile", [128, 4, 2048], BF16)

    for s in range(8):
        @block.sync
        def _(g, s=s):
            if ("qfree",) in tokq:
                g.wait_ge(p.sem("t"), tokq[("qfree",)])
            for dt in range(16):
                p.inc(g.dma_start(out=qsl[:, dt, :],
                                  in_=dr["qTh"][dt * 128:(dt + 1) * 128, s * 128:(s + 1) * 128]),
                      "dm_q", 16)

        @block.vector
        def _(v, s=s):
            v.wait_ge(p.sem("v"), p.tok("v"))
            if ("accread",) in tokq:
                v.wait_ge(p.sem("v"), tokq[("accread",)])
            v.memset(accf[:, :], 0.0)
            v.memset(macc[:, 1:2], 0.0)
            ins = v.memset(macc[:, 0:1], -1e30)
            p.inc(ins, "v", 1)

        nkv = 8 * (s + 1)
        for kt0 in range(0, nkv, 4):
            last_sb = (kt0 + 4 == nkv)

            @block.sync
            def _(g, s=s, kt0=kt0):
                for j in range(4):
                    c = kt0 + j
                    g_, snd, hf = c // 16, c % 8, (c // 8) % 2
                    g.wait_ge(p.sem(f"cc_kv{g_}"), 2)
                    if ("kvfree",) in tokq:
                        g.wait_ge(p.sem("t"), tokq[("kvfree",)])
                    for dt in range(16):
                        p.inc(g.dma_start(out=ktile[:, dt, j * 128:(j + 1) * 128],
                                          in_=dr[f"kta{g_}"][snd, dt * 128:(dt + 1) * 128,
                                                             hf * 128:(hf + 1) * 128]), "dm_kv", 16)
                    p.inc(g.dma_start(out=vtile[:, j, :],
                                      in_=dr[f"vta{g_}"][snd, hf * 128:(hf + 1) * 128, :]), "dm_kv", 16)

            @block.tensor
            def _(t, s=s, kt0=kt0):
                t.wait_ge(p.sem("dm_q"), p.tok("dm_q"))
                t.wait_ge(p.sem("dm_kv"), p.tok("dm_kv"))
                t.wait_ge(p.sem("v"), p.tok("v"))
                for dt in range(16):
                    ins = t.matmul(ps0[:, :], qsl[:, dt, :], ktile[:, dt, :],
                                   start=(dt == 0), stop=(dt == 15))
                p.inc(ins, "t", 1)

            @block.vector
            def _(v, s=s, kt0=kt0, last_sb=last_sb):
                v.wait_ge(p.sem("t"), p.tok("t"))
                if last_sb:
                    off = (kt0 % 8) * 128
                    ins = v.tensor_tensor(scf[:, :], ps0[:, :], amask[:, off:off + 512], OP.add)
                else:
                    ins = v.tensor_copy(scf[:, :], ps0[:, :])
                v.wait_ge(p.sem("v"), p.inc(ins, "v", 1))
                ins = v.reduce_max(macc[:, 2:3], scf[:, :], axis=AX.X)
                v.wait_ge(p.sem("v"), p.inc(ins, "v", 1))
                v.tensor_scalar(macc[:, 3:4], macc[:, 2:3], macc[:, 0:1], None, OP.max)
                v.wait_ge(p.sem("v"), p.inc(v.tensor_copy(macc[:, 6:7], macc[:, 0:1]), "v", 1))
                v.tensor_scalar(macc[:, 4:5], macc[:, 3:4], -1.0, None, OP.mult)
                ins = v.tensor_scalar(macc[:, 5:6], macc[:, 6:7], macc[:, 3:4], None, OP.subtract)
                v.wait_ge(p.sem("v"), p.inc(ins, "v", 1))
                p.inc(v.tensor_copy(macc[:, 0:1], macc[:, 3:4]), "v", 1)

            @block.scalar
            def _(sc, s=s, kt0=kt0):
                sc.wait_ge(p.sem("v"), p.tok("v"))
                sc.memzero(macc[:, 7:8])
                sc.activation(ptile[:, :] if False else ptt[:, :], scf[:, :], AF.Exp,
                              bias=macc[:, 4:5], accum_out=macc[:, 7:8])
                p.inc(sc.activation(macc[:, 8:9], macc[:, 5:6], AF.Exp), "s", 1)

            @block.vector
            def _(v, s=s, kt0=kt0):
                v.wait_ge(p.sem("s"), p.tok("s"))
                ins = v.scalar_tensor_tensor(macc[:, 1:2], macc[:, 1:2], macc[:, 8:9],
                                             macc[:, 7:8], OP.mult, OP.add)
                p.inc(ins, "v", 1)

            @block.tensor
            def _(t, s=s, kt0=kt0):
                t.wait_ge(p.sem("s"), p.tok("s"))
                for j in range(4):
                    ins = t.transpose(ptrb[:, j * 128:(j + 1) * 128], ptt[:, j * 128:(j + 1) * 128],
                                      identb[:, :])
                p.inc(ins, "t", 1)

            @block.vector
            def _(v, s=s, kt0=kt0):
                v.wait_ge(p.sem("t"), p.tok("t"))
                ins = v.tensor_copy(ptt[:, :], ptrb[:, :])
                p.inc(ins, "v", 1)

            for dh in range(2):
                @block.tensor
                def _(t, s=s, kt0=kt0, dh=dh):
                    t.wait_ge(p.sem("v"), p.tok("v"))
                    for dvq in range(2):
                        for j in range(4):
                            ins = t.matmul(pav[:, dvq * 512:(dvq + 1) * 512], ptt[:, j * 128:(j + 1) * 128],
                                           vtile[:, j, dh * 1024 + dvq * 512:dh * 1024 + (dvq + 1) * 512],
                                           start=(j == 0), stop=(j == 3))
                    tokq[("kvfree",)] = p.inc(ins, "t", 1)

                @block.vector
                def _(v, s=s, kt0=kt0, dh=dh):
                    v.wait_ge(p.sem("t"), p.tok("t"))
                    for dvq in range(2):
                        ins = v.scalar_tensor_tensor(accf[:, dh * 1024 + dvq * 512:dh * 1024 + (dvq + 1) * 512],
                                                     accf[:, dh * 1024 + dvq * 512:dh * 1024 + (dvq + 1) * 512],
                                                     macc[:, 8:9], pav[:, dvq * 512:(dvq + 1) * 512],
                                                     OP.mult, OP.add)
                    p.inc(ins, "v", 1)

        @block.vector
        def _(v, s=s):
            v.wait_ge(p.sem("v"), p.tok("v"))
            ins = v.reciprocal(macc[:, 9:10], macc[:, 1:2])
            v.wait_ge(p.sem("v"), p.inc(ins, "v", 1))

        for dvq in range(4):
            @block.vector
            def _(v, s=s, dvq=dvq):
                if ("epist",) in tokq:
                    v.wait_ge(p.sem("st_kv"), tokq[("epist",)])
                ins = v.tensor_scalar(ptt[:, :], accf[:, dvq * 512:(dvq + 1) * 512],
                                      macc[:, 9:10], None, OP.mult)
                p.inc(ins, "v", 1)
                if dvq == 3:
                    tokq[("accread",)] = p.tok("v")

            @block.tensor
            def _(t, s=s, dvq=dvq):
                t.wait_ge(p.sem("v"), p.tok("v"))
                for j in range(4):
                    ins = t.transpose(ptrb[:, j * 128:(j + 1) * 128], ptt[:, j * 128:(j + 1) * 128],
                                      identb[:, :])
                p.inc(ins, "t", 1)

            @block.vector
            def _(v, s=s, dvq=dvq):
                v.wait_ge(p.sem("t"), p.tok("t"))
                ins = v.tensor_copy(qkstg[:, 0, :], ptrb[:, :])
                p.inc(ins, "v", 1)

            @block.sync
            def _(g, s=s, dvq=dvq):
                g.wait_ge(p.sem("v"), p.tok("v"))
                for j in range(4):
                    r0 = (dvq * 4 + j) * 128
                    ins = g.dma_start(out=dr["aoTh"][r0:r0 + 128, s * 128:(s + 1) * 128],
                                      in_=qkstg[:, 0, j * 128:(j + 1) * 128])
                    p.inc(ins, "st_kv", 16)
                tokq[("epist",)] = p.tok("st_kv")
        tokq[("qfree",)] = p.tok("t")

    @block.sync
    def _(s):
        s.wait_ge(p.sem("v"), p.tok("v"))
        p.inc(s.dma_start(out=mubc[:, :], in_=dr["rows0"][8:9, 0:TOK]
                          .broadcast_to([128, TOK])), "dm_lnbc", 16)
        p.inc(s.dma_start(out=rsbc[:, :], in_=dr["rows0"][9:10, 0:TOK]
                          .broadcast_to([128, TOK])), "dm_lnbc", 16)

    for tg in range(2):
        @block.sync
        def _(g, tg=tg):
            g.wait_ge(p.sem("st_kv"), p.tok("st_kv"))
            g.wait_ge(p.sem("t"), p.tok("t"))
            for dt in range(16):
                p.inc(g.dma_start(out=ktile[:, dt, :],
                                  in_=dr["aoTh"][dt * 128:(dt + 1) * 128, tg * 512:(tg + 1) * 512]),
                      "dm_ao%d" % tg, 16)

        for dto in range(16):
            @block.gpsimd
            def _(g, tg=tg, dto=dto):
                g.wait_ge(p.sem("t"), p.tok("t"))
                p.inc(g.dma_start(
                    out=w_t[:, 0:16, :],
                    in_=dr["oproj1"].rearrange("(dt pp) d -> pp dt d", pp=128)
                    [:, :, dto * 128:(dto + 1) * 128]), "dm_wo0", 16)

            @block.tensor
            def _(t, tg=tg, dto=dto):
                t.wait_ge(p.sem("dm_wo0"), p.tok("dm_wo0"))
                t.wait_ge(p.sem("dm_ao%d" % tg), p.tok("dm_ao%d" % tg))
                if ("ops",) in tokq:
                    t.wait_ge(p.sem("v"), tokq[("ops",)])
                for dt in range(16):
                    ins = t.matmul(ps0[:, :], w_t[:, dt, :], ktile[:, dt, :],
                                   start=(dt == 0), stop=(dt == 15))
                p.inc(ins, "t", 1)

            @block.vector
            def _(v, tg=tg, dto=dto):
                v.wait_ge(p.sem("t"), p.tok("t"))
                v.wait_ge(p.sem("dm_lnbc"), p.tok("dm_lnbc"))
                sl = slice(tg * 512, (tg + 1) * 512)
                v.tensor_tensor(scf[:, :], xT[:, dto, sl], rsbc[:, sl], OP.mult)
                v.wait_ge(p.sem("v"), p.inc(v.memset(macc[:, 10:11], 0.0), "v", 1))
                ins = v.tensor_tensor(scf[:, :], scf[:, :], mubc[:, sl], OP.add)
                v.wait_ge(p.sem("v"), p.inc(ins, "v", 1))
                ins = v.tensor_scalar(scf[:, :], scf[:, :], modc[:, 1, dto, 1:2],
                                      modc[:, 1, dto, 0:1], OP.mult, OP.add)
                v.wait_ge(p.sem("v"), p.inc(ins, "v", 1))
                ins = v.tensor_tensor(scf[:, :], ps0[:, :], scf[:, :], OP.add)
                v.wait_ge(p.sem("v"), p.inc(ins, "v", 1))
                ins = v.scalar_tensor_tensor(xT[:, dto, sl], scf[:, :],
                                             modc[:, 1, dto, 2:3], xT[:, dto, sl],
                                             OP.mult, OP.add)
                tokq[("ops",)] = p.inc(ins, "v", 1)

    ln_stats_own(p, block, xT, TOK, 12, kcol[0:1, KCVALS[1e-5]:KCVALS[1e-5] + 1])
    ln_apply(p, block, xT, xn, TOK, 12, (1, 4, 3))



def ph6(p, block):
    dr = p.dram
    modc = p.sb["modc"]
    identb = p.sb["identb_s"]
    identf = p.sb["identf_s"]
    xT = p.sb["xTr"]
    h4 = p.sb["xn"]
    ps0, ps1, prr = p.ps["ps0"], p.ps["ps1"], p.ps["prr"]
    p.free("vtile", "ktile", "qsl", "scf", "ptt", "accf",
           "amask", "qkstg", "vstg", "macc")
    p.free("pav", "ptrb")

    scw = p.sbuf("scw", [128, 16, 44], F32)
    rcol = p.sbuf("rcol", [128, 10], F32)
    rrow = p.sbuf("rrow", [1, 512], F32)
    bct = p.sbuf("bct", [128, 44], F32)
    cen = p.sbuf("cen", [128, 44], BF16)
    qsc = p.sbuf("qsc", [128, 44], F32)
    deqc = p.sbuf("deqc", [128, 44], BF16)
    deqf = p.sbuf("deqf", [128, 44], F32)
    wstg = p.sbuf("wstg", [128, 128], BF16)
    wtl = p.sbuf("wtl", [128, 2816], BF16)
    ttl = p.sbuf("ttl", [128, 2816], BF16)
    Nacc = p.sbuf("Nacc", [128, 2816], BF16)
    Pacc = p.sbuf("Pacc", [128, 2816], BF16)
    pout = p.psum("pout", [128, 256], F32)
    ptr6 = p.psum("ptr6", [128, 128], F32)
    ptb6 = p.psum("ptb6", [128, 128], BF16)
    tok6 = {}
    BIG = 2.0 ** 100
    W1TSH = 2048 * 768
    W3OFF = 2 * W1TSH
    items = [(0, ti) for ti in range(6)] + [(1, ti) for ti in range(6)] + \
            [(2, ti) for ti in range(4)]

    def load_tile(s, wi, ti, width):
        if wi < 2:
            nm = ("w1_my", "w2_my")[wi]
            return s.dma_start(out=wtl[:, 0:width], in_=dr[nm][ti * 128:(ti + 1) * 128, :])
        flat = dr["w3_my"].rearrange("a b -> (a b)").rearrange(
            "(g q pp e) -> g q pp e", q=2, pp=128, e=64)
        return s.dma_start(out=wtl[:, 0:width].rearrange("pp (gq e) -> pp gq e", e=64),
                           in_=flat[ti * 22:(ti + 1) * 22].rearrange("g q pp e -> pp (g q) e"))

    for wi, ti in items:
        nseg = 32 if wi < 2 else 44
        width = nseg * 64
        slot = wi * 6 + ti

        @block.sync
        def _(s, wi=wi, ti=ti, width=width):
            s.wait_ge(p.sem("v"), p.tok("v"))
            p.inc(load_tile(s, wi, ti, width), "dm_nf", 16)

        @block.vector
        def _(v, wi=wi, ti=ti, slot=slot, nseg=nseg, width=width):
            v.wait_ge(p.sem("dm_nf"), p.tok("dm_nf"))
            v.tensor_scalar(ttl[:, 0:width], wtl[:, 0:width], -1.0, None, OP.mult)
            ins = v.tensor_tensor(ttl[:, 0:width], ttl[:, 0:width], wtl[:, 0:width], OP.max)
            v.wait_ge(p.sem("v"), p.inc(ins, "v", 1))
            for j in range(nseg):
                ins = v.reduce_max(scw[:, slot, j:j + 1], ttl[:, j * 64:(j + 1) * 64], axis=AX.X)
            v.wait_ge(p.sem("v"), p.inc(ins, "v", 1))
            ins = v.reduce_sum(rcol[:, wi:wi + 1], scw[:, slot, 0:nseg], axis=AX.X)
            p.inc(ins, "v", 1)

        @block.tensor
        def _(t, wi=wi, ti=ti):
            t.wait_ge(p.sem("v"), p.tok("v"))
            ins = t.matmul(prr[0:1, 8 + wi:9 + wi], p.sb["ones2"][:, 0:1], rcol[:, wi:wi + 1],
                           start=(ti == 0), stop=(ti == (5 if wi < 2 else 3)))
            p.inc(ins, "t", 1)

    @block.vector
    def _(v):
        v.wait_ge(p.sem("t"), p.tok("t"))
        ins = v.memset(rrow[0:1, 0:4], 0.0)
        v.wait_ge(p.sem("v"), p.inc(ins, "v", 1))
        p.inc(v.tensor_copy(rrow[0:1, 0:3], prr[0:1, 8:11]), "v", 1)

    @block.gpsimd
    def _(g):
        g.wait_ge(p.sem("v"), p.tok("v"))
        p.inc(g.dma_start(out=dr["nfsum_in"][:, :], in_=rrow[0:1, 0:4]), "st_nf3", 16)
        g.wait_ge(p.sem("st_nf3"), p.tok("st_nf3"))
        p.inc(g.collective_compute(
            "AllReduce", OP.add, replica_groups=[list(range(NCORES))],
            ins=[dr["nfsum_in"][:, :]], outs=[dr["nfsum_all"][:, :]]), "cc_nf", 1)
        g.wait_ge(p.sem("cc_nf"), p.tok("cc_nf"))
        p.inc(g.dma_start(out=rcol[:, 4:7],
                          in_=dr["nfsum_all"][0:1, 0:3].broadcast_to([128, 3])), "dm_nf3", 16)

    @block.vector
    def _(v):
        v.wait_ge(p.sem("dm_nf3"), p.tok("dm_nf3"))
        v.tensor_scalar(rcol[:, 4:5], rcol[:, 4:5], 1.0 / (NCORES * 704 * 32), None, OP.mult)
        v.tensor_scalar(rcol[:, 5:6], rcol[:, 5:6], 1.0 / (NCORES * 704 * 32), None, OP.mult)
        ins = v.tensor_scalar(rcol[:, 6:7], rcol[:, 6:7], 1.0 / (NCORES * 22528), None, OP.mult)
        v.wait_ge(p.sem("v"), p.inc(ins, "v", 1))
        ins = v.tensor_copy(cen[:, 0:3], rcol[:, 4:7])
        v.wait_ge(p.sem("v"), p.inc(ins, "v", 1))
        p.inc(v.tensor_copy(rcol[:, 4:7], cen[:, 0:3]), "v", 1)

    def ladder(v, width):
        v.tensor_scalar(Nacc[:, 0:width], wtl[:, 0:width], float(NF4_MID[0]), BIG,
                        OP.subtract, OP.mult)
        v.tensor_scalar(Nacc[:, 0:width], Nacc[:, 0:width], float(NF4[0]), None, OP.max)
        for j in range(2, 9):
            v.tensor_scalar(ttl[:, 0:width], wtl[:, 0:width], float(NF4_MID[j - 1]), BIG,
                            OP.subtract, OP.mult)
            v.scalar_tensor_tensor(Nacc[:, 0:width], ttl[:, 0:width], float(NF4[j - 1]),
                                   Nacc[:, 0:width], OP.max, OP.min)
        v.tensor_scalar(ttl[:, 0:width], wtl[:, 0:width], float(NF4_MID[8]), BIG,
                        OP.subtract, OP.mult)
        v.tensor_scalar(Pacc[:, 0:width], ttl[:, 0:width], float(NF4[9]), float(NF4[8]),
                        OP.min, OP.max)
        for j in range(10, 16):
            v.tensor_scalar(ttl[:, 0:width], wtl[:, 0:width], float(NF4_MID[j - 1]), BIG,
                            OP.subtract, OP.mult)
            v.scalar_tensor_tensor(Pacc[:, 0:width], ttl[:, 0:width], float(NF4[j]),
                                   Pacc[:, 0:width], OP.min, OP.max)
        ins = v.tensor_tensor(Nacc[:, 0:width], Nacc[:, 0:width], Pacc[:, 0:width], OP.min)
        v.wait_ge(p.sem("v"), p.inc(ins, "v", 1))

    for wi, ti in items:
        nseg = 32 if wi < 2 else 44
        width = nseg * 64
        slot = wi * 6 + ti

        @block.sync
        def _(s, wi=wi, ti=ti, width=width):
            s.wait_ge(p.sem("v"), p.tok("v"))
            s.wait_ge(p.sem("st_wq"), p.tok("st_wq"))
            p.inc(load_tile(s, wi, ti, width), "dm_nf", 16)

        @block.vector
        def _(v, wi=wi, ti=ti, slot=slot, nseg=nseg):
            def rt(ins):
                v.wait_ge(p.sem("v"), p.inc(ins, "v", 1))
            v.wait_ge(p.sem("dm_nf"), p.tok("dm_nf"))
            rt(v.tensor_scalar(bct[:, 0:nseg], scw[:, slot, 0:nseg], 1e-30, None, OP.max))
            rt(v.reciprocal(bct[:, 0:nseg], bct[:, 0:nseg]))
            rt(v.tensor_scalar(cen[:, 0:nseg], scw[:, slot, 0:nseg], rcol[:, 4 + wi:5 + wi],
                               None, OP.subtract))
            rt(v.tensor_copy(qsc[:, 0:nseg], cen[:, 0:nseg]))
            rt(v.tensor_scalar(deqf[:, 0:nseg], qsc[:, 0:nseg], -1.0, None, OP.mult))
            rt(v.tensor_tensor(qsc[:, 0:nseg], qsc[:, 0:nseg], deqf[:, 0:nseg], OP.max))
            if wi < 2:
                rt(v.reduce_max(rcol[:, 3:4], qsc[:, 0:nseg], axis=AX.X))
            else:
                a = qsc[:, 0:44].rearrange("pp (g u) -> pp g u", u=2)
                rt(v.tensor_tensor(a[:, :, 0], a[:, :, 0], a[:, :, 1], OP.max))

        @block.tensor
        def _(t, wi=wi, ti=ti):
            t.wait_ge(p.sem("v"), p.tok("v"))
            if wi < 2:
                ins = t.transpose(ptr6[0:1, 0:128], rcol[:, 3:4], identf[:, :])
            else:
                ins = t.transpose(ptr6[0:44, 0:128], qsc[:, 0:44], identf[:, :])
            p.inc(ins, "t", 1)

        @block.vector
        def _(v, wi=wi, ti=ti):
            def rt(ins):
                v.wait_ge(p.sem("v"), p.inc(ins, "v", 1))
            v.wait_ge(p.sem("t"), p.tok("t"))
            if wi < 2:
                rt(v.tensor_copy(rrow[0:1, 0:128], ptr6[0:1, 0:128]))
                a = rrow[0:1, 0:128].rearrange("o (g k) -> o g k", k=8)
                rt(v.tensor_tensor(a[:, :, 0:4], a[:, :, 0:4], a[:, :, 4:8], OP.max))
                rt(v.tensor_tensor(a[:, :, 0:2], a[:, :, 0:2], a[:, :, 2:4], OP.max))
                rt(v.tensor_tensor(a[:, :, 0:1], a[:, :, 0:1], a[:, :, 1:2], OP.max))
                for k in range(1, 8):
                    ins = v.tensor_copy(a[:, :, k:k + 1], a[:, :, 0:1])
                rt(ins)
            else:
                rt(v.reduce_max(rcol[0:44, 3:4], ptr6[0:44, 0:128], axis=AX.X))

        if wi == 2:
            @block.tensor
            def _(t, wi=wi, ti=ti):
                t.wait_ge(p.sem("v"), p.tok("v"))
                ins = t.transpose(ptr6[0:1, 0:44], rcol[0:44, 3:4], identf[0:44, 0:44])
                p.inc(ins, "t", 1)

            @block.vector
            def _(v, wi=wi, ti=ti):
                def rt(ins):
                    v.wait_ge(p.sem("v"), p.inc(ins, "v", 1))
                v.wait_ge(p.sem("t"), p.tok("t"))
                rt(v.tensor_copy(rrow[0:1, 0:44], ptr6[0:1, 0:44]))
                b = rrow[0:1, 0:44].rearrange("o (g u) -> o g u", u=2)
                rt(v.tensor_copy(b[:, :, 1:2], b[:, :, 0:1]))

        nw = 128 if wi < 2 else 44

        @block.vector
        def _(v, wi=wi, ti=ti, nw=nw):
            def rt(ins):
                v.wait_ge(p.sem("v"), p.inc(ins, "v", 1))
            rt(v.tensor_scalar(rrow[0:1, 128:128 + nw], rrow[0:1, 0:nw], 2.0, None, OP.mult))
            rt(v.reciprocal(rrow[0:1, 128:128 + nw], rrow[0:1, 128:128 + nw]))
            rt(v.tensor_scalar(rrow[0:1, 128:128 + nw], rrow[0:1, 128:128 + nw], 256.0,
                               None, OP.mult))
            rt(v.tensor_copy(wstg[0:1, 0:nw], rrow[0:1, 128:128 + nw]))
            rt(v.tensor_copy(rrow[0:1, 128:128 + nw], wstg[0:1, 0:nw]))
            rt(v.reciprocal(rrow[0:1, 256:256 + nw], rrow[0:1, 128:128 + nw]))

        if wi < 2:
            @block.tensor
            def _(t, wi=wi, ti=ti):
                t.wait_ge(p.sem("v"), p.tok("v"))
                t.transpose(ptr6[:, 0:1], rrow[0:1, 128:256], identf[0:1, 0:1])
                ins = t.transpose(ptr6[:, 1:2], rrow[0:1, 256:384], identf[0:1, 0:1])
                p.inc(ins, "t", 1)

            @block.vector
            def _(v, wi=wi, ti=ti):
                v.wait_ge(p.sem("t"), p.tok("t"))
                ins = v.tensor_copy(rcol[:, 8:10], ptr6[:, 0:2])
                v.wait_ge(p.sem("v"), p.inc(ins, "v", 1))

            @block.vector
            def _(v, wi=wi, ti=ti, nseg=nseg):
                def rt(ins):
                    v.wait_ge(p.sem("v"), p.inc(ins, "v", 1))
                v.wait_ge(p.sem("dm_nf"), p.tok("dm_nf"))
                rt(v.tensor_scalar(cen[:, 0:nseg], cen[:, 0:nseg], rcol[:, 8:9], None, OP.mult))
                rt(v.tensor_copy(qsc[:, 0:nseg], cen[:, 0:nseg]))
                rt(v.tensor_scalar(qsc[:, 0:nseg], qsc[:, 0:nseg], 8388608.0, 8388608.0,
                                   OP.add, OP.subtract))
                rt(v.tensor_scalar(qsc[:, 0:nseg], qsc[:, 0:nseg], 127.0, -128.0,
                                   OP.min, OP.max))
                rt(v.tensor_scalar(deqc[:, 0:nseg], qsc[:, 0:nseg], rcol[:, 9:10], None, OP.mult))
                rt(v.tensor_scalar(deqc[:, 0:nseg], deqc[:, 0:nseg], rcol[:, 4 + wi:5 + wi],
                                   None, OP.add))
                rt(v.tensor_copy(qsc[:, 0:nseg], deqc[:, 0:nseg]))
        else:
            @block.sync
            def _(s, wi=wi, ti=ti):
                s.wait_ge(p.sem("v"), p.tok("v"))
                p.inc(s.dma_start(out=dr["rows0"][14:15, 0:44], in_=rrow[0:1, 128:172]), "st_nf", 16)
                p.inc(s.dma_start(out=dr["rows0"][15:16, 0:44], in_=rrow[0:1, 256:300]), "st_nf", 16)
                s.wait_ge(p.sem("st_nf"), p.tok("st_nf"))
                p.inc(s.dma_start(out=deqf[:, 0:44],
                                  in_=dr["rows0"][15:16, 0:44].broadcast_to([128, 44])),
                      "dm_nf", 16)

            @block.gpsimd
            def _(g, wi=wi, ti=ti):
                g.wait_ge(p.sem("st_nf"), p.tok("st_nf"))
                p.inc(g.dma_start(out=deqc[:, 0:44],
                                  in_=dr["rows0"][14:15, 0:44].broadcast_to([128, 44])),
                      "dm_nf3", 16)

            @block.vector
            def _(v, wi=wi, ti=ti):
                def rt(ins):
                    v.wait_ge(p.sem("v"), p.inc(ins, "v", 1))
                v.wait_ge(p.sem("dm_nf"), p.tok("dm_nf"))
                v.wait_ge(p.sem("dm_nf3"), p.tok("dm_nf3"))
                rt(v.tensor_tensor(cen[:, 0:44], cen[:, 0:44], deqc[:, 0:44], OP.mult))
                rt(v.tensor_copy(qsc[:, 0:44], cen[:, 0:44]))
                rt(v.tensor_scalar(qsc[:, 0:44], qsc[:, 0:44], 8388608.0, 8388608.0,
                                   OP.add, OP.subtract))
                rt(v.tensor_scalar(qsc[:, 0:44], qsc[:, 0:44], 127.0, -128.0, OP.min, OP.max))
                rt(v.tensor_tensor(qsc[:, 0:44], qsc[:, 0:44], deqf[:, 0:44], OP.mult))
                rt(v.tensor_copy(deqc[:, 0:44], qsc[:, 0:44]))
                rt(v.tensor_scalar(deqc[:, 0:44], deqc[:, 0:44], rcol[:, 6:7], None, OP.add))
                rt(v.tensor_copy(qsc[:, 0:44], deqc[:, 0:44]))

        @block.vector
        def _(v, wi=wi, ti=ti, nseg=nseg, width=width):
            v.wait_ge(p.sem("v"), p.tok("v"))
            for j in range(nseg):
                ins = v.tensor_scalar(wtl[:, j * 64:(j + 1) * 64], wtl[:, j * 64:(j + 1) * 64],
                                      bct[:, j:j + 1], None, OP.mult)
            v.wait_ge(p.sem("v"), p.inc(ins, "v", 1))
            ladder(v, width)
            for j in range(nseg):
                ins = v.tensor_scalar(wtl[:, j * 64:(j + 1) * 64], Nacc[:, j * 64:(j + 1) * 64],
                                      qsc[:, j:j + 1], None, OP.mult)
            p.inc(ins, "v", 1)

        if wi < 2:
            for dt in range(16):
                @block.tensor
                def _(t, wi=wi, ti=ti, dt=dt):
                    t.wait_ge(p.sem("v"), p.tok("v"))
                    ins = t.transpose(ptb6[:, 0:128], wtl[:, dt * 128:(dt + 1) * 128], identb[:, :])
                    p.inc(ins, "t", 1)

                @block.vector
                def _(v, wi=wi, ti=ti, dt=dt):
                    v.wait_ge(p.sem("t"), p.tok("t"))
                    if ("wst",) in tok6:
                        v.wait_ge(p.sem("st_wq"), tok6[("wst",)])
                    ins = v.tensor_copy(wstg[:, :], ptb6[:, 0:128])
                    p.inc(ins, "v", 1)

                @block.sync
                def _(s, wi=wi, ti=ti, dt=dt):
                    s.wait_ge(p.sem("v"), p.tok("v"))
                    vw = dr["wq_in"][0, wi * W1TSH:(wi + 1) * W1TSH] \
                        .rearrange("(d f) -> d f", f=768)
                    ins = s.dma_start(out=vw[dt * 128:(dt + 1) * 128, ti * 128:(ti + 1) * 128],
                                      in_=wstg[:, :])
                    tok6[("wst",)] = p.inc(ins, "st_wq", 16)
        else:
            @block.sync
            def _(s, wi=wi, ti=ti):
                s.wait_ge(p.sem("v"), p.tok("v"))
                base = W3OFF + ti * 2816 * 128
                ins = s.dma_start(out=dr["wq_in"][0, base:base + 2816 * 128]
                                  .rearrange("(gq pp e) -> gq pp e", pp=128, e=64)
                                  .rearrange("gq pp e -> pp gq e"),
                                  in_=wtl[:, 0:2816].rearrange("pp (gq e) -> pp gq e", e=64))
                tok6[("wst",)] = p.inc(ins, "st_wq", 16)

    @block.gpsimd
    def _(g):
        g.wait_ge(p.sem("st_wq"), p.tok("st_wq"))
        p.inc(g.collective_compute(
            "AllGather", OP.bypass, replica_groups=[list(range(NCORES))],
            ins=[dr["wq_in"][:, :]], outs=[dr["wq_all"][:, :]]), "cc_wq", 1)

    p.free("Pacc", "Nacc", "ttl", "wtl")
    p.free("ptb6", "ptr6", "pout", "prr")
    hh = p.sbuf("hh", [128, 48, 256], BF16)
    wf = p.sbuf("wf", [128, 2, 16, 128], BF16)
    wf2 = p.sbuf("wf2", [128, 2, 16, 128], BF16)
    wt3 = p.sbuf("wt3", [128, 2, 128], BF16)
    tl3 = p.sbuf("tl3", [128, 2, 128], BF16)
    sstg = p.sbuf("sstg", [128, 256], F32)
    ystg = p.sbuf("ystg", [128, 2, 128], F32)
    poutA = p.psum("poutA", [128, 512], F32)
    poutB = p.psum("poutB", [128, 512], F32)
    ptrY0 = p.psum("ptrY0", [128, 128], F32)
    ptrY1 = p.psum("ptrY1", [128, 128], F32)
    ptbA = p.psum("ptbA", [128, 128], BF16)
    ptbB = p.psum("ptbB", [128, 128], BF16)

    for tg in range(4):
        for fb in range(48):
            par = fb % 2
            sh, b = fb // 6, fb % 6

            @block.sync
            def _(s, tg=tg, fb=fb, sh=sh, b=b, par=par):
                s.wait_ge(p.sem("cc_wq"), p.tok("cc_wq"))
                if ("ffw", par) in tok6:
                    s.wait_ge(p.sem("t"), tok6[("ffw", par)])
                v1 = dr["wq_all"][sh, 0:W1TSH].rearrange("(dt pp f) -> pp dt f", pp=128, f=768)
                v2 = dr["wq_all"][sh, W1TSH:2 * W1TSH].rearrange("(dt pp f) -> pp dt f",
                                                                 pp=128, f=768)
                p.inc(s.dma_start(out=wf[:, par, :, :], in_=v1[:, :, b * 128:(b + 1) * 128]),
                      "dm_ff%d" % par, 16)
                p.inc(s.dma_start(out=wf2[:, par, :, :], in_=v2[:, :, b * 128:(b + 1) * 128]),
                      "dm_ff%d" % par, 16)
                tok6[("ffld", par)] = p.tok("dm_ff%d" % par)

            @block.tensor
            def _(t, tg=tg, fb=fb, par=par):
                t.wait_ge(p.sem("dm_ff%d" % par), tok6[("ffld", par)])
                if ("ffps",) in tok6:
                    t.wait_ge(p.sem("v"), tok6[("ffps",)])
                for dt in range(16):
                    t.matmul(ps0[:, 0:256], wf[:, par, dt, :], h4[:, dt, tg * 256:(tg + 1) * 256],
                             start=(dt == 0), stop=(dt == 15))
                for dt in range(16):
                    ins = t.matmul(ps1[:, 0:256], wf2[:, par, dt, :],
                                   h4[:, dt, tg * 256:(tg + 1) * 256],
                                   start=(dt == 0), stop=(dt == 15))
                tok6[("ffw", par)] = p.inc(ins, "t", 1)

            @block.scalar
            def _(sc, tg=tg, fb=fb):
                sc.wait_ge(p.sem("t"), p.tok("t"))
                p.inc(sc.activation(sstg[:, 0:256], ps0[:, 0:256], AF.Silu), "s", 1)

            @block.vector
            def _(v, tg=tg, fb=fb):
                v.wait_ge(p.sem("s"), p.tok("s"))
                v.wait_ge(p.sem("t"), p.tok("t"))
                ins = v.tensor_tensor(hh[:, fb, :], sstg[:, 0:256], ps1[:, 0:256], OP.mult)
                tok6[("ffps",)] = p.inc(ins, "v", 1)

        for dto in range(16):
            dpar = dto % 2
            pout = (poutA, poutB)[dpar]
            for fb in range(48):
                wpar = fb % 2
                ptb = (ptbA, ptbB)[wpar]
                sh, b = fb // 6, fb % 6
                kreal = 64 if b == 5 else 128
                fcol = 704 * sh + 128 * b

                @block.sync
                def _(s, tg=tg, dto=dto, fb=fb, wpar=wpar, kreal=kreal, fcol=fcol):
                    if ("w3tr", wpar) in tok6:
                        s.wait_ge(p.sem("t"), tok6[("w3tr", wpar)])
                    w3v = dr["wq_all"][dto // 2, W3OFF:W3OFF + 256 * 5632] \
                        .rearrange("(d f) -> d f", f=5632)
                    ins = s.dma_start(out=tl3[:, wpar, 0:kreal],
                                      in_=w3v[(dto % 2) * 128:(dto % 2) * 128 + 128,
                                              fcol:fcol + kreal])
                    p.inc(ins, "dm_f3%d" % wpar, 16)
                    tok6[("f3ld", wpar)] = p.tok("dm_f3%d" % wpar)

                @block.tensor
                def _(t, tg=tg, dto=dto, fb=fb, wpar=wpar, kreal=kreal, ptb=ptb):
                    t.wait_ge(p.sem("dm_f3%d" % wpar), tok6[("f3ld", wpar)])
                    if ("w3ev", wpar) in tok6:
                        t.wait_ge(p.sem("v"), tok6[("w3ev", wpar)])
                    ins = t.transpose(ptb[0:kreal, 0:128], tl3[:, wpar, 0:kreal], identb[:, :])
                    tok6[("w3tr", wpar)] = p.inc(ins, "t", 1)

                @block.vector
                def _(v, tg=tg, dto=dto, fb=fb, wpar=wpar, kreal=kreal, ptb=ptb):
                    v.wait_ge(p.sem("t"), tok6[("w3tr", wpar)])
                    ins = v.tensor_copy(wt3[0:kreal, wpar, :], ptb[0:kreal, 0:128])
                    tok6[("w3ev", wpar)] = p.inc(ins, "v", 1)

                @block.tensor
                def _(t, tg=tg, dto=dto, fb=fb, wpar=wpar, kreal=kreal, dpar=dpar, pout=pout):
                    t.wait_ge(p.sem("v"), tok6[("w3ev", wpar)])
                    if fb == 0 and ("pev", dpar) in tok6:
                        t.wait_ge(p.sem("v"), tok6[("pev", dpar)])
                    ins = t.matmul(pout[:, 0:256], wt3[0:kreal, wpar, :], hh[0:kreal, fb, :],
                                   start=(fb == 0), stop=(fb == 47))
                    p.inc(ins, "t", 1)

            @block.vector
            def _(v, tg=tg, dto=dto, dpar=dpar, pout=pout):
                v.wait_ge(p.sem("t"), p.tok("t"))
                sl = slice(tg * 256, (tg + 1) * 256)
                ins = v.scalar_tensor_tensor(xT[:, dto, sl], pout[:, 0:256], modc[:, 1, dto, 5:6],
                                             xT[:, dto, sl], OP.mult, OP.add)
                tok6[("pev", dpar)] = p.inc(ins, "v", 1)

    for i, (oc, dt) in enumerate([(oc, dt) for oc in range(8) for dt in range(16)]):
        ypar = i % 2
        ptrY = (ptrY0, ptrY1)[ypar]

        @block.tensor
        def _(t, oc=oc, dt=dt, ypar=ypar, ptrY=ptrY):
            t.wait_ge(p.sem("v"), p.tok("v"))
            if ("ycp", ypar) in tok6:
                t.wait_ge(p.sem("v"), tok6[("ycp", ypar)])
            ins = t.transpose(ptrY[:, :], xT[:, dt, oc * 128:(oc + 1) * 128], identf[:, :])
            tok6[("ytr", ypar)] = p.inc(ins, "t", 1)

        @block.vector
        def _(v, oc=oc, dt=dt, ypar=ypar, ptrY=ptrY):
            v.wait_ge(p.sem("t"), tok6[("ytr", ypar)])
            if ("yst", ypar) in tok6:
                v.wait_ge(p.sem("st_y%d" % ypar), tok6[("yst", ypar)])
            ins = v.tensor_copy(ystg[:, ypar, :], ptrY[:, :])
            tok6[("ycp", ypar)] = p.inc(ins, "v", 1)

        @block.sync
        def _(s, oc=oc, dt=dt, ypar=ypar):
            s.wait_ge(p.sem("v"), tok6[("ycp", ypar)])
            ins = s.dma_start(out=dr["y"][oc * 128:(oc + 1) * 128, dt * 128:(dt + 1) * 128],
                              in_=ystg[:, ypar, :])
            tok6[("yst", ypar)] = p.inc(ins, "st_y%d" % ypar, 16)


def finish(p, block):
    dr = p.dram

    @block.sync
    def _(s):
        for name in list(p.sems):
            s.wait_ge(p.sem(name), p.tok(name))


def dump_copy(p, block):
    if not p.dump:
        return
    bounce = p.sbuf("dumpb", [128, 8192], BF16)
    bf = p.sbuf("dumpf", [128, 4096], F32)

    @block.gpsimd
    def _(g):
        for name in list(p.sems):
            g.wait_ge(p.sem(name), p.tok(name))
        for name in sorted(p.dump):
            src_ap = p.dram[name]
            dst_ap = p.dram[name + "_dump"]
            n = int(np.prod(src_ap.shape))
            tile = bounce if src_ap.dtype == BF16 else bf
            width = 8192 if src_ap.dtype == BF16 else 4096
            fl_s = src_ap
            while len(fl_s.shape) > 1:
                fl_s = fl_s.rearrange(" ".join(chr(97 + i) for i in range(len(fl_s.shape))) +
                                      " -> " + " ".join(chr(97 + i) for i in range(len(fl_s.shape) - 2)) +
                                      " (" + chr(97 + len(fl_s.shape) - 2) + " " + chr(97 + len(fl_s.shape) - 1) + ")")
            fl_d = dst_ap
            while len(fl_d.shape) > 1:
                fl_d = fl_d.rearrange(" ".join(chr(97 + i) for i in range(len(fl_d.shape))) +
                                      " -> " + " ".join(chr(97 + i) for i in range(len(fl_d.shape) - 2)) +
                                      " (" + chr(97 + len(fl_d.shape) - 2) + " " + chr(97 + len(fl_d.shape) - 1) + ")")
            pos = 0
            while pos < n:
                m = min(128 * width, n - pos)
                rows = m // width
                if rows >= 1:
                    m = rows * width
                    sview = fl_s[pos:pos + m].rearrange("(pp w) -> pp w", w=width)
                    dview = fl_d[pos:pos + m].rearrange("(pp w) -> pp w", w=width)
                    p.inc(g.dma_start(out=tile[0:rows, 0:width], in_=sview), "dm_dmp", 16)
                    g.wait_ge(p.sem("dm_dmp"), p.tok("dm_dmp"))
                    p.inc(g.dma_start(out=dview, in_=tile[0:rows, 0:width]), "dm_dmp", 16)
                    g.wait_ge(p.sem("dm_dmp"), p.tok("dm_dmp"))
                else:
                    sview = fl_s[pos:pos + m].rearrange("(o w) -> o w", o=1)
                    dview = fl_d[pos:pos + m].rearrange("(o w) -> o w", o=1)
                    p.inc(g.dma_start(out=tile[0:1, 0:m], in_=sview), "dm_dmp", 16)
                    g.wait_ge(p.sem("dm_dmp"), p.tok("dm_dmp"))
                    p.inc(g.dma_start(out=dview, in_=tile[0:1, 0:m]), "dm_dmp", 16)
                    g.wait_ge(p.sem("dm_dmp"), p.tok("dm_dmp"))
                pos += m


def dummy_y(p, block):
    ysb = p.sbuf("ysb", [128, 512], F32)

    @block.gpsimd
    def _(g):
        g.memset(ysb[:, :], 0.0)
        for r in range(0, TOK, 128):
            for cq in range(4):
                p.inc(g.dma_start(out=p.dram["y"][r:r + 128, cq * 512:(cq + 1) * 512],
                                  in_=ysb[:, :]), "st_y", 16)


def build_program(dump=(), phases=("ph0", "ph1", "ph2"), with_dummy_y=None):
    if with_dummy_y is None:
        with_dummy_y = "ph6" not in phases
    p = Prog(dump=dump)
    declare_io(p)
    with p.stack:
        with p.nc.allow_non_contiguous_dma(reason="small strided/broadcast loads"), p.nc.Block() as block:
            if "ph0" in phases:
                ph0(p, block)
            if "ph1" in phases:
                ph1(p, block)
            if "ph2" in phases:
                ph2(p, block)
            if "ph3" in phases:
                ph3(p, block)
            if "ph4" in phases:
                ph4(p, block)
            if "ph5" in phases:
                ph5(p, block)
            if "ph6" in phases:
                ph6(p, block)
            if with_dummy_y:
                dummy_y(p, block)
            dump_copy(p, block)
            finish(p, block)
    return p


def kernel(**inputs):
    maps = stage_inputs(inputs)
    p = build_program(phases=("ph0", "ph1", "ph2", "ph3", "ph4", "ph5", "ph6"), with_dummy_y=False)
    res = run_bass_kernel_spmd(p.nc, maps, core_ids=list(range(NCORES)))
    out = np.zeros((S, D), np.float32)
    for i in range(NCORES):
        own = np.concatenate([np.arange(c * 128, (c + 1) * 128) for c in range(i, NCHUNK, NCORES)])
        out[own] = res.results[i]["y"]
    return out[None]
